# version 02
# baseline (speedup 1.0000x reference)
import os
import numpy as np
import concourse.bass as bass
import concourse.mybir as mybir
from concourse.bass_utils import run_bass_kernel_spmd

F32 = mybir.dt.float32
BF16 = mybir.dt.bfloat16
AF = mybir.ActivationFunctionType
ALU = mybir.AluOpType

D = 1024
T = 2048
KD = D // 128
DH = 4096
NSUB = int(os.environ.get("KSUB", "8"))

ENGS = ("sync", "scalar", "vector", "tensor", "gpsimd")


class Buf:
    __slots__ = ("w", "r")
    def __init__(self):
        self.w = None
        self.r = []


class K:
    def __init__(self, nc, sems):
        self.nc = nc
        self.sems = sems
        self.cnt = {e: 0 for e in ENGS}
        self.prog = {e: [] for e in ENGS}
        self.hwm = {}
        self._barrier_snap = None
        self._barrier_done = {}

    def op(self, eng, fn, reads=(), writes=(), dma=False, psum_rw=None, sreads=()):
        deps = []
        force = []
        for b in sreads:
            if b.w is not None:
                force.append(b.w)
                deps.append(b.w)
        bs = self._barrier_snap
        if bs is not None and not self._barrier_done.get(eng, False):
            deps.extend((pe, v) for pe, v in bs.items() if pe != eng and v > 0)
            self._barrier_done[eng] = True
        for b in reads:
            if b.w is not None:
                deps.append(b.w)
        for b in writes:
            if b.w is not None:
                deps.append(b.w)
            deps.extend(b.r)
        if psum_rw is not None:
            if psum_rw.w is not None:
                deps.append(psum_rw.w)
            deps.extend(psum_rw.r)
        waits = []
        forceset = set((d[0], d[1]) for d in force)
        for dep in deps:
            pe, v = dep[0], dep[1]
            pdma = dep[2] if len(dep) > 2 else False
            forced = (pe, v) in forceset
            if pdma:
                v = self.cnt[pe]
            if pe == eng and not pdma and not forced:
                continue
            key = (eng, pe)
            if self.hwm.get(key, 0) >= v:
                continue
            self.hwm[key] = v
            waits.append((pe, v))
        inc = 16 if dma else 1
        self.cnt[eng] += inc
        tok = (eng, self.cnt[eng], dma)
        self.prog[eng].append((fn, waits, inc))
        for b in reads:
            b.r.append(tok)
        for b in writes:
            b.w = tok
            b.r = []
        if psum_rw is not None:
            psum_rw.w = tok
        return tok

    def dma(self, eng, out_ap, in_ap, reads=(), writes=()):
        return self.op(eng, lambda e, o=out_ap, i=in_ap: e.dma_start(out=o, in_=i),
                       reads=reads, writes=writes, dma=True)

    def barrier(self):
        self._barrier_snap = dict(self.cnt)
        self._barrier_done = {}

    def finalize(self, block, final_engine="sync"):
        snap = dict(self.cnt)
        progs, sems = self.prog, self.sems

        def make(engname):
            def emit(e):
                for fn, waits, inc in progs[engname]:
                    for (pe, v) in waits:
                        e.wait_ge(sems[pe], v)
                    fn(e).then_inc(sems[engname], inc)
                if engname == final_engine:
                    for pe in ENGS:
                        if snap[pe] > 0:
                            e.wait_ge(sems[pe], snap[pe])
            return emit

        block.sync(make("sync"))
        block.scalar(make("scalar"))
        block.vector(make("vector"))
        block.tensor(make("tensor"))
        block.gpsimd(make("gpsimd"))


def _pack_kd_cols(w):
    d_in, m = w.shape
    nk = d_in // 128
    return np.ascontiguousarray(w.reshape(nk, 128, m).transpose(1, 0, 2).reshape(128, nk * m))


def _cols(v):
    n = v.shape[0] // 128
    return np.ascontiguousarray(v.reshape(n, 128).T)


def _perm_order():
    g, wl, b = np.meshgrid(np.arange(32), np.arange(4), np.arange(32), indexing="ij")
    return (b * 128 + 4 * g + wl).reshape(-1)

_PERM = _perm_order()


def _bfly_coeffs(weights):
    out = np.empty((12, 4, 2048), np.float32)
    for j in range(12):
        s = 2 ** j
        if s <= 64:
            nb_l = 128 // (2 * s)
            b, nb, sl = np.meshgrid(np.arange(32), np.arange(nb_l), np.arange(s), indexing="ij")
            kmap = (b * (64 // s) + nb) * s + sl
        else:
            m = j - 7
            sm = 2 ** m
            nb_l = 32 // (2 * sm)
            g, wl, nb, sl = np.meshgrid(np.arange(32), np.arange(4), np.arange(nb_l), np.arange(sm), indexing="ij")
            kmap = nb * (128 * sm) + sl * 128 + (4 * g + wl)
        kf = kmap.reshape(-1)
        for ci, (r, c) in enumerate(((0, 0), (0, 1), (1, 0), (1, 1))):
            out[j, ci, :] = weights[j, kf, r, c]
    o = np.broadcast_to(out[:, :, None, :], (12, 4, 128, 2048))
    return np.ascontiguousarray(o.transpose(0, 2, 1, 3).reshape(12, 128, 4 * 2048))


def host_prep(I):
    H = {}
    H["sl1_wzT"] = _pack_kd_cols(np.ascontiguousarray(I["L0_attn_a_Wz"].T))
    H["sl1_whT"] = _pack_kd_cols(np.ascontiguousarray(I["L0_attn_a_Wh"].T))
    H["sl1_bz"] = _cols(I["L0_attn_a_bz"])
    H["sl1_bh"] = _cols(I["L0_attn_a_bh"])
    H["sl2_wexp"] = _pack_kd_cols(I["L0_ffn_W_exp"])
    wcp = I["L0_ffn_W_con"][_PERM, :]
    H["sl2_wcon"] = _pack_kd_cols(np.ascontiguousarray(wcp))
    H["sl2_bexp"] = _cols(I["L0_ffn_b_exp"])
    H["sl2_upbp"] = _cols(np.ascontiguousarray(I["L0_ffn_up_bias"][_PERM]))
    H["sl2_bcon"] = _cols(I["L0_ffn_b_con"])
    H["sl2_dwnb"] = _cols(I["L0_ffn_down_bias"])
    H["sl2_upco"] = _bfly_coeffs(I["L0_ffn_up_weights"])
    H["sl2_dnco"] = _bfly_coeffs(I["L0_ffn_down_weights"])
    H["sl3_wxp"] = _pack_kd_cols(I["L0_attn_b_w_xproj"])
    H["sl3_wdt"] = np.ascontiguousarray(I["L0_attn_b_w_dt"])
    H["sl3_bdt"] = _cols(I["L0_attn_b_b_dt"])
    H["sl3_alogc"] = _pack_kd_cols(I["L0_attn_b_A_log"])
    H["sl3_dssm"] = _cols(I["L0_attn_b_D_ssm"])
    H["sl3_lng"] = _cols(I["L0_attn_b_ln_g"])
    H["sl3_lnb"] = _cols(I["L0_attn_b_ln_b"])
    H["sl3_wconv"] = _pack_kd_cols(I["L0_attn_b_w_conv"])
    H["sl3_bconv"] = _cols(I["L0_attn_b_b_conv"])
    H["sl3_wssm"] = _pack_kd_cols(I["L0_attn_b_w_ssm_out"])
    H["sl3_wq"] = _pack_kd_cols(I["L0_attn_b_wq"])
    H["sl3_wk"] = _pack_kd_cols(I["L0_attn_b_wk"])
    H["sl3_wv"] = _pack_kd_cols(I["L0_attn_b_wv"])
    H["sl3_wo"] = _pack_kd_cols(I["L0_attn_b_wo"])
    H["sl3_wwm"] = _pack_kd_cols(I["L0_attn_b_w_wm"])
    H["sl3_bwm"] = _cols(I["L0_attn_b_b_wm"])
    H["sl3_episT"] = np.ascontiguousarray(I["L0_attn_b_episodic"][0].T)
    H["sl3_wdc1"] = np.ascontiguousarray(I["L0_attn_b_w_dc1"])
    H["sl3_bdc1"] = _cols(I["L0_attn_b_b_dc1"])
    H["sl3_wdc2"] = _pack_kd_cols(I["L0_attn_b_w_dc2"])
    H["sl3_bdc2"] = _cols(I["L0_attn_b_b_dc2"])
    H["sl3_bdc2b"] = np.ascontiguousarray(np.broadcast_to(I["L0_attn_b_b_dc2"][None, :], (128, 1024)))
    rb = I["L0_attn_b_rel_bias"]
    qi = np.arange(256)[:, None]
    kj = np.arange(768)[None, :]
    rel = kj - 256 - qi
    win = np.abs(rel) <= 256
    bias = rb[np.clip(rel + 256, 0, 512), :]
    bias = np.where(win[:, :, None], bias, np.float32(-1e9))
    H["sl3_bias"] = np.ascontiguousarray(bias.transpose(2, 0, 1).reshape(16, 2, 128, 768))
    wqkv = I["L1_attn_a_w_qkv"]
    hh_, kk_ = np.meshgrid(np.arange(16), np.arange(32), indexing="ij")
    ev = (hh_ * 64 + 2 * kk_).reshape(-1)
    od = (hh_ * 64 + 2 * kk_ + 1).reshape(-1)
    qperm = np.concatenate([ev, od])
    wq_p = wqkv[qperm, :]
    wk_p = wqkv[1024 + qperm, :]
    wv_n = wqkv[2048:, :]
    wqkv_cols = np.concatenate([wq_p, wk_p, wv_n], axis=0).T
    H["sl5_wqkvT"] = _pack_kd_cols(np.ascontiguousarray(wqkv_cols))
    H["sl5_woutT"] = _pack_kd_cols(np.ascontiguousarray(I["L1_attn_a_w_out"].T))
    H["sl5_bout"] = _cols(I["L1_attn_a_b_out"])
    H["sl5_tri"] = np.ascontiguousarray(np.tril(np.ones((128, 128), np.float32)))
    i32 = np.arange(32, dtype=np.float64)
    th64 = 10000.0 ** (-2.0 * (i32 - 1.0) / 64.0)
    th_p = np.tile(th64, 4)
    thhi = th_p.astype(np.float32)
    thlo = (th_p - thhi.astype(np.float64)).astype(np.float32)
    H["sl5_thhi"] = np.ascontiguousarray(thhi[:, None])
    H["sl5_thlo"] = np.ascontiguousarray(thlo[:, None])
    m64 = np.arange(2048, dtype=np.float64)
    H["sl5_m128"] = np.ascontiguousarray(np.broadcast_to(m64.astype(np.float32)[None, :], (128, 2048)))
    ang = th_p[:, None] * m64[None, :]
    tp = 2.0 * np.pi
    ks_ = np.round(ang / tp)
    sh_s = -tp * ks_
    H["sl5_shi"] = np.ascontiguousarray(sh_s.astype(np.float32))
    H["sl5_slo"] = np.ascontiguousarray((sh_s - H["sl5_shi"].astype(np.float64)).astype(np.float32))
    angc = ang + np.pi / 2
    kc_ = np.round(angc / tp)
    sh_c = np.pi / 2 - tp * kc_
    H["sl5_shic"] = np.ascontiguousarray(sh_c.astype(np.float32))
    H["sl5_sloc"] = np.ascontiguousarray((sh_c - H["sl5_shic"].astype(np.float64)).astype(np.float32))
    H["sl6_gwT"] = _pack_kd_cols(np.ascontiguousarray(I["L1_ffn_gate_w"].T))
    H["sl6_nw"] = _cols(I["L1_ffn_norm_weight"])
    H["sl6_w1T"] = np.ascontiguousarray(np.stack([_pack_kd_cols(np.ascontiguousarray(I["L1_ffn_w1"][e].T)) for e in range(8)]))
    H["sl6_w2T"] = np.ascontiguousarray(np.stack([_pack_kd_cols(np.ascontiguousarray(I["L1_ffn_w2"][e].T)) for e in range(8)]))
    H["sl6_b1"] = np.ascontiguousarray(np.stack([_cols(I["L1_ffn_b1"][e]) for e in range(8)]))
    H["sl6_b2"] = np.ascontiguousarray(np.stack([_cols(I["L1_ffn_b2"][e]) for e in range(8)]))
    H["sl7_waT"] = _pack_kd_cols(np.ascontiguousarray(I["L1_attn_b_w_attn"].T))
    H["sl7_ba"] = _cols(I["L1_attn_b_b_attn"])
    H["sl7_laT"] = _pack_kd_cols(np.ascontiguousarray(I["L1_attn_b_la_attn"].T))
    H["sl7_lbT"] = np.ascontiguousarray(I["L1_attn_b_lb_attn"].T)
    H["sl7_wpT"] = _pack_kd_cols(np.ascontiguousarray(I["L1_attn_b_w_proj"].T))
    H["sl7_bp"] = _cols(I["L1_attn_b_b_proj"])
    H["sl7_lapT"] = _pack_kd_cols(np.ascontiguousarray(I["L1_attn_b_la_proj"].T))
    H["sl7_lbpT"] = np.ascontiguousarray(I["L1_attn_b_lb_proj"].T)
    H["sl7_bav"] = np.ascontiguousarray(np.broadcast_to(I["L1_attn_b_b_attn"][2048:][None, :], (128, 1024)))
    H["sl7_tria"] = np.ascontiguousarray(np.triu(np.full((128, 128), -1e9, np.float32), 1))
    H["eyerep"] = np.ascontiguousarray(
        np.broadcast_to(np.eye(128, dtype=np.float32)[None], (32, 128, 128))
        .transpose(1, 0, 2).reshape(128, 32 * 128))
    return H


HOST_SPECS = {
    "sl1_wzT": (128, KD * D), "sl1_whT": (128, KD * D),
    "sl1_bz": (128, KD), "sl1_bh": (128, KD),
    "sl2_wexp": (128, KD * DH), "sl2_wcon": (128, 32 * D),
    "sl2_bexp": (128, 32), "sl2_upbp": (128, 32),
    "sl2_bcon": (128, KD), "sl2_dwnb": (128, KD),
    "sl2_upco": (12, 128, 4 * 2048), "sl2_dnco": (12, 128, 4 * 2048),
    "eyerep": (128, 32 * 128),
    "sl3_wxp": (128, KD * 192), "sl3_wdt": (64, D), "sl3_bdt": (128, KD),
    "sl3_alogc": (128, KD * 64), "sl3_dssm": (128, KD), "sl3_lng": (128, KD),
    "sl3_lnb": (128, KD), "sl3_wconv": (128, 32), "sl3_bconv": (128, KD),
    "sl3_wssm": (128, KD * D), "sl3_wq": (128, KD * D), "sl3_wk": (128, KD * 256),
    "sl3_wv": (128, KD * 256), "sl3_wo": (128, KD * D), "sl3_wwm": (128, KD * D),
    "sl3_bwm": (128, KD), "sl3_episT": (128, 4096), "sl3_wdc1": (128, 512),
    "sl3_bdc1": (128, 4), "sl3_wdc2": (128, 4 * D), "sl3_bdc2": (128, KD),
    "sl3_bdc2b": (128, D), "sl3_bias": (16, 2, 128, 768),
    "sl5_wqkvT": (128, KD * 3072), "sl5_woutT": (128, KD * D), "sl5_bout": (128, KD),
    "sl5_tri": (128, 128), "sl5_thhi": (128, 1), "sl5_thlo": (128, 1),
    "sl5_m128": (128, T), "sl5_shi": (128, T), "sl5_slo": (128, T),
    "sl5_shic": (128, T), "sl5_sloc": (128, T),
    "sl6_gwT": (128, KD * 8), "sl6_nw": (128, KD),
    "sl6_w1T": (8, 128, KD * 2048), "sl6_w2T": (8, 128, 16 * D),
    "sl6_b1": (8, 128, 16), "sl6_b2": (8, 128, KD),
    "sl7_waT": (128, KD * 3072), "sl7_ba": (128, 24), "sl7_laT": (128, KD * 32),
    "sl7_lbT": (32, 3072), "sl7_wpT": (128, KD * D), "sl7_bp": (128, KD),
    "sl7_lapT": (128, KD * 32), "sl7_lbpT": (32, D), "sl7_bav": (128, D),
    "sl7_tria": (128, 128),
}


def build_program():
    nc = bass.Bass("TRN2", target_bir_lowering=False)
    x_h = nc.dram_tensor("x", [D, T], F32, kind="ExternalInput").ap()
    y_h = nc.dram_tensor("y", [D, T], F32, kind="ExternalOutput").ap()
    h_in = {n: nc.dram_tensor(n, list(sh), F32, kind="ExternalInput").ap()
            for n, sh in HOST_SPECS.items()}

    from contextlib import ExitStack
    with ExitStack() as stack:
        sems = {e: stack.enter_context(nc.semaphore("s_" + e)) for e in ENGS}
        k = K(nc, sems)
        sb = lambda name, shape, dt=F32: stack.enter_context(nc.sbuf_tensor(name, list(shape), dt))
        ps = lambda name, shape: stack.enter_context(nc.psum_tensor(name, list(shape), F32))

        X = sb("X", (128, KD * T))
        ONES1 = sb("ONES1", (1, 128))
        ONESC = sb("ONESC", (128, 1))
        CONSTC = sb("CONSTC", (128, 8))
        bufs = {}
        def B(name):
            if name not in bufs:
                bufs[name] = Buf()
            return bufs[name]
        PS = [ps("ps%d" % i, (128, 512)) for i in range(8)]
        PSB = [B("ps%d" % i) for i in range(8)]

        def free_psum(i, tok_buf):
            PSB[i].r = []
            PSB[i].w = tok_buf.w

        k.op("vector", lambda e: e.memset(ONES1[:], 1.0), writes=[B("ones1")])
        k.op("vector", lambda e: e.memset(ONESC[:], 1.0), writes=[B("onesc")])
        k.op("vector", lambda e: e.memset(CONSTC[:, 0:1], 1e-5), writes=[B("constc")])
        k.op("vector", lambda e: e.memset(CONSTC[:, 1:2], 0.5), writes=[B("constc")])
        k.op("vector", lambda e: e.memset(CONSTC[:, 2:3], 1e-6), writes=[B("constc")])

        for kd in range(KD):
            k.dma("sync", X[:, kd * T:(kd + 1) * T], x_h[kd * 128:(kd + 1) * 128, :], writes=[B("X%d" % kd)])

        def ln_stats(RB, MB, SQ, STATR, eps_col):
            mu = STATR[0:1, 0:512]
            ex2 = STATR[0:1, 512:1024]
            rstd = STATR[0:1, 1024:1536]
            for c in range(4):
                for kd in range(KD):
                    xap = X[:, kd * T + c * 512: kd * T + (c + 1) * 512]
                    k.op("scalar", lambda e, xap=xap, kd=kd: e.activation(
                        out=SQ[:, (kd % 2) * 512:(kd % 2) * 512 + 512], in_=xap, func=AF.Square),
                        reads=[B("X%d" % kd)], writes=[B("sq%d" % (kd % 2))])
                    k.op("tensor", lambda e, xap=xap, kd=kd: e.matmul(
                        PS[0][0:1, 0:512], ONESC[:, 0:1], xap, start=(kd == 0), stop=(kd == KD - 1)),
                        reads=[B("onesc"), B("X%d" % kd)], psum_rw=PSB[0])
                    k.op("tensor", lambda e, kd=kd: e.matmul(
                        PS[1][0:1, 0:512], ONESC[:, 0:1], SQ[:, (kd % 2) * 512:(kd % 2) * 512 + 512],
                        start=(kd == 0), stop=(kd == KD - 1)),
                        reads=[B("onesc"), B("sq%d" % (kd % 2))], psum_rw=PSB[1])
                k.op("scalar", lambda e: e.mul(mu, PS[0][0:1, 0:512], 1.0 / D),
                     reads=[PSB[0]], writes=[B("mu")])
                free_psum(0, B("mu"))
                k.op("scalar", lambda e: e.mul(ex2, PS[1][0:1, 0:512], 1.0 / D),
                     reads=[PSB[1]], writes=[B("ex2")])
                free_psum(1, B("ex2"))
                k.op("vector", lambda e: e.tensor_tensor(out=rstd, in0=mu, in1=mu, op=ALU.mult),
                     reads=[B("mu")], writes=[B("rstd")])
                k.op("vector", lambda e: e.tensor_tensor(out=rstd, in0=ex2, in1=rstd, op=ALU.subtract),
                     reads=[B("ex2"), B("rstd")], writes=[B("rstd")])
                k.op("scalar", lambda e: e.activation(out=rstd, in_=rstd, func=AF.Sqrt, bias=eps_col[0:1, :]),
                     reads=[B("rstd"), B("constc")], writes=[B("rstd")])
                k.op("vector", lambda e: e.reciprocal(out=rstd, in_=rstd), reads=[B("rstd")], writes=[B("rstd")])
                k.op("vector", lambda e: e.tensor_tensor(out=ex2, in0=mu, in1=rstd, op=ALU.mult),
                     reads=[B("mu"), B("rstd")], writes=[B("ex2")])
                k.op("tensor", lambda e: e.matmul(PS[0][:, 0:512], ONES1[0:1, :], rstd, start=True, stop=True),
                     reads=[B("ones1"), B("rstd")], writes=[PSB[0]])
                k.op("vector", lambda e, c=c: e.tensor_copy(out=RB[:, c * 512:(c + 1) * 512], in_=PS[0][:, 0:512]),
                     reads=[PSB[0]], writes=[B("RB")])
                free_psum(0, B("RB"))
                k.op("tensor", lambda e: e.matmul(PS[1][:, 0:512], ONES1[0:1, :], ex2, start=True, stop=True),
                     reads=[B("ones1"), B("ex2")], writes=[PSB[1]])
                k.op("vector", lambda e, c=c: e.tensor_copy(out=MB[:, c * 512:(c + 1) * 512], in_=PS[1][:, 0:512]),
                     reads=[PSB[1]], writes=[B("MB")])
                free_psum(1, B("MB"))

        def make_xnb(RB, MB, SQ, dst, dst_bname, kd, csl, dsl):
            xap = X[:, kd * T:(kd + 1) * T][:, csl]
            n = csl.stop - csl.start
            sq = SQ[:, 0:n]
            k.op("vector", lambda e, xap=xap, sq=sq, csl=csl: e.tensor_tensor(
                out=sq, in0=xap, in1=RB[:, csl], op=ALU.mult),
                reads=[B("X%d" % kd), B("RB")], writes=[B("sqx")])
            k.op("vector", lambda e, sq=sq, dsl=dsl, csl=csl: e.tensor_tensor(
                out=dst[:, dsl], in0=sq, in1=MB[:, csl], op=ALU.subtract),
                reads=[B("sqx"), B("MB")], writes=[B(dst_bname)])

        def sublayer1(sb, ps):
            RB = sb("RB", (128, T)); MB = sb("MB", (128, T))
            SQ = sb("SQ", (128, T)); STATR = sb("STATR", (1, 1536))
            XNB = sb("XNB1", (128, KD * T), BF16)
            wz = sb("wz", (128, KD * D), BF16)
            wh = sb("wh", (128, KD * D), BF16)
            bzc = sb("bzc", (128, KD)); bhc = sb("bhc", (128, KD)); nbz = sb("nbz", (128, KD))
            Ct = sb("Ct", (128, T)); PREt = sb("PREt", (128, T)); Gt = sb("Gt", (128, T))
            for kd in range(KD):
                k.dma("gpsimd", wz[:, kd * D:(kd + 1) * D], h_in["sl1_wzT"][:, kd * D:(kd + 1) * D], writes=[B("wz")])
                k.dma("gpsimd", wh[:, kd * D:(kd + 1) * D], h_in["sl1_whT"][:, kd * D:(kd + 1) * D], writes=[B("wh")])
            k.dma("sync", bzc[:], h_in["sl1_bz"], writes=[B("bzc")])
            k.dma("sync", bhc[:], h_in["sl1_bh"], writes=[B("bhc")])
            k.op("vector", lambda e: e.tensor_scalar(out=nbz[:], in0=bzc[:], scalar1=-1.0, scalar2=None, op0=ALU.mult),
                 reads=[B("bzc")], writes=[B("nbz")])
            ln_stats(RB, MB, SQ, STATR, CONSTC[:, 0:1])
            k.barrier()
            for kd in range(KD):
                make_xnb(RB, MB, SQ, XNB, "XNB%d" % kd, kd, slice(0, T), slice(kd * T, (kd + 1) * T))
            for od in range(KD):
                for kd in range(KD):
                    for c in range(4):
                        rhs = XNB[:, kd * T + c * 512: kd * T + (c + 1) * 512]
                        lw = wz[:, kd * D + od * 128: kd * D + (od + 1) * 128]
                        k.op("tensor", lambda e, lw=lw, rhs=rhs, kd=kd, c=c: e.matmul(
                            PS[c][:, 0:512], lw, rhs, start=(kd == 0), stop=(kd == KD - 1)),
                            reads=[B("wz"), B("XNB%d" % kd)], psum_rw=PSB[c])
                    for c in range(4):
                        rhs = XNB[:, kd * T + c * 512: kd * T + (c + 1) * 512]
                        lw2 = wh[:, kd * D + od * 128: kd * D + (od + 1) * 128]
                        k.op("tensor", lambda e, lw2=lw2, rhs=rhs, kd=kd, c=c: e.matmul(
                            PS[4 + c][:, 0:512], lw2, rhs, start=(kd == 0), stop=(kd == KD - 1)),
                            reads=[B("wh"), B("XNB%d" % kd)], psum_rw=PSB[4 + c])
                for c in range(4):
                    k.op("scalar", lambda e, c=c, od=od: e.activation(
                        out=Ct[:, c * 512:(c + 1) * 512], in_=PS[c][:, 0:512], func=AF.Sigmoid,
                        bias=nbz[:, od:od + 1], scale=-1.0),
                        reads=[PSB[c], B("nbz")], writes=[B("Ct")])
                    free_psum(c, B("Ct"))
                    k.op("scalar", lambda e, c=c, od=od: e.activation(
                        out=PREt[:, c * 512:(c + 1) * 512], in_=PS[4 + c][:, 0:512], func=AF.Identity,
                        bias=bhc[:, od:od + 1]),
                        reads=[PSB[4 + c], B("bhc")], writes=[B("PREt")])
                    free_psum(4 + c, B("PREt"))
                k.op("vector", lambda e: e.tensor_scalar(out=SQ[:], in0=Ct[:], scalar1=-1.0, scalar2=1.0,
                                                         op0=ALU.mult, op1=ALU.add),
                     reads=[B("Ct")], writes=[B("sqx")])
                k.op("vector", lambda e: e.tensor_scalar(out=Gt[:], in0=PREt[:], scalar1=0.0, scalar2=None,
                                                         op0=ALU.min),
                     reads=[B("PREt")], writes=[B("Gt")])
                k.op("scalar", lambda e: e.activation(out=Gt[:], in_=Gt[:], func=AF.Sigmoid),
                     reads=[B("Gt")], writes=[B("Gt")])
                k.op("vector", lambda e: e.scalar_tensor_tensor(out=Gt[:], in0=PREt[:], scalar=0.0, in1=Gt[:],
                                                                op0=ALU.max, op1=ALU.add),
                     reads=[B("PREt"), B("Gt")], writes=[B("Gt")])
                k.op("vector", lambda e: e.tensor_tensor(out=Gt[:], in0=Gt[:], in1=SQ[:], op=ALU.mult),
                     reads=[B("Gt"), B("sqx")], writes=[B("Gt")])
                k.op("vector", lambda e: e.tensor_tensor_scan(PREt[:], Ct[:], Gt[:], CONSTC[:, 1:2],
                                                              ALU.mult, ALU.add),
                     reads=[B("Ct"), B("Gt"), B("constc")], writes=[B("PREt")])
                k.op("vector", lambda e, od=od: e.tensor_tensor(
                    out=X[:, od * T:(od + 1) * T], in0=X[:, od * T:(od + 1) * T], in1=PREt[:], op=ALU.add),
                    reads=[B("PREt"), B("X%d" % od)], writes=[B("X%d" % od)])

        bfst_h = nc.dram_tensor("bfst", [128, 4 * 4096], BF16, kind="Internal").ap()

        def compose_bfly(sb, ps):
            STLB_UP = sb("STLB_UP", (128, 32 * 128), BF16)
            STHB_UP = sb("STHB_UP", (128, 32 * 128), BF16)
            STLB_DN = sb("STLB_DN", (128, 32 * 128), BF16)
            STHB_DN = sb("STHB_DN", (128, 32 * 128), BF16)
            PING = sb("PING", (128, 32 * 128))
            PONG = sb("PONG", (128, 32 * 128))
            CO = sb("CO", (128, 4 * 2048))
            TA = sb("TA", (128, 2048)); TB = sb("TB", (128, 2048))

            def compose(co_name, stages, dstB, dname):
                k.dma("sync", PING[:], h_in["eyerep"], writes=[B("PING")])
                cur, nxt = PING, PONG
                curb, nxtb = "PING", "PONG"
                for j in stages:
                    s = 2 ** j
                    k.dma("sync", CO[:], h_in[co_name][j], writes=[B("CO")])
                    if s <= 64:
                        nb_l = 128 // (2 * s)
                        view = lambda t, nb_l=nb_l: t.rearrange("p (b nb two s) -> p b nb two s", b=32, nb=nb_l, two=2)
                        cview = lambda t, nb_l=nb_l: t.rearrange("p (b nb s) -> p b nb s", b=32, nb=nb_l)
                    else:
                        sm = s // 128
                        nb_l = 32 // (2 * sm)
                        view = lambda t, nb_l=nb_l: t.rearrange("p (g wl nb two s) -> p g wl nb two s", g=32, wl=4, nb=nb_l, two=2)
                        cview = lambda t, nb_l=nb_l: t.rearrange("p (g wl nb s) -> p g wl nb s", g=32, wl=4, nb=nb_l)
                    sel_a = (slice(None),) * 1
                    av = view(cur[:])
                    bv = view(cur[:])
                    nav = view(nxt[:])
                    nbv = view(nxt[:])
                    nd = len(av.shape)
                    idx_a = tuple([slice(None)] * (nd - 2) + [0, slice(None)])
                    idx_b = tuple([slice(None)] * (nd - 2) + [1, slice(None)])
                    av = av[idx_a]; bv = bv[idx_b]; nav = nav[idx_a]; nbv = nbv[idx_b]
                    w00 = cview(CO[:, 0:2048]); w01 = cview(CO[:, 2048:4096])
                    w10 = cview(CO[:, 4096:6144]); w11 = cview(CO[:, 6144:8192])
                    TAv = cview(TA[:]); TBv = cview(TB[:])
                    k.op("vector", lambda e, av=av, w00=w00, TAv=TAv: e.tensor_tensor(out=TAv, in0=av, in1=w00, op=ALU.mult),
                         reads=[B(curb), B("CO")], writes=[B("TA")])
                    k.op("vector", lambda e, bv=bv, w01=w01, TBv=TBv: e.tensor_tensor(out=TBv, in0=bv, in1=w01, op=ALU.mult),
                         reads=[B(curb), B("CO")], writes=[B("TB")])
                    k.op("vector", lambda e, nav=nav, TAv=TAv, TBv=TBv: e.tensor_tensor(out=nav, in0=TAv, in1=TBv, op=ALU.add),
                         reads=[B("TA"), B("TB")], writes=[B(nxtb)])
                    k.op("vector", lambda e, av=av, w10=w10, TAv=TAv: e.tensor_tensor(out=TAv, in0=av, in1=w10, op=ALU.mult),
                         reads=[B(curb), B("CO")], writes=[B("TA")])
                    k.op("vector", lambda e, bv=bv, w11=w11, TBv=TBv: e.tensor_tensor(out=TBv, in0=bv, in1=w11, op=ALU.mult),
                         reads=[B(curb), B("CO")], writes=[B("TB")])
                    k.op("vector", lambda e, nbv=nbv, TAv=TAv, TBv=TBv: e.tensor_tensor(out=nbv, in0=TAv, in1=TBv, op=ALU.add),
                         reads=[B("TA"), B("TB")], writes=[B(nxtb)])
                    cur, nxt = nxt, cur
                    curb, nxtb = nxtb, curb
                k.op("vector", lambda e, cur=cur, dstB=dstB: e.tensor_copy(out=dstB[:], in_=cur[:]),
                     reads=[B(curb)], writes=[B(dname)])

            compose("sl2_upco", range(0, 7), STLB_UP, "STLB_UP")
            compose("sl2_upco", range(7, 12), STHB_UP, "STHB_UP")
            compose("sl2_dnco", range(0, 7), STLB_DN, "STLB_DN")
            compose("sl2_dnco", range(7, 12), STHB_DN, "STHB_DN")
            for i, (t, nm) in enumerate(((STLB_UP, "STLB_UP"), (STHB_UP, "STHB_UP"),
                                         (STLB_DN, "STLB_DN"), (STHB_DN, "STHB_DN"))):
                k.dma("sync", bfst_h[:, i * 4096:(i + 1) * 4096], t[:], reads=[B(nm)], writes=[B("bfspill")])
            if os.environ.get("KDBG"):
                for nm, t in (("dbg_lu", STLB_UP), ("dbg_hu", STHB_UP), ("dbg_ld", STLB_DN), ("dbg_hd", STHB_DN)):
                    o = nc.dram_tensor(nm, [128, 32 * 128], BF16, kind="ExternalOutput").ap()
                    k.dma("sync", o, t[:], reads=[B(nm.replace("dbg_lu", "STLB_UP").replace("dbg_hu", "STHB_UP").replace("dbg_ld", "STLB_DN").replace("dbg_hd", "STHB_DN"))], writes=[B("dbgdone")])

        def sublayer_ffn(sb, ps):
            STLB_UP = sb("STLB_UP", (128, 32 * 128), BF16)
            STHB_UP = sb("STHB_UP", (128, 32 * 128), BF16)
            STLB_DN = sb("STLB_DN", (128, 32 * 128), BF16)
            STHB_DN = sb("STHB_DN", (128, 32 * 128), BF16)
            for i, (t, nm) in enumerate(((STLB_UP, "STLB_UP"), (STHB_UP, "STHB_UP"),
                                         (STLB_DN, "STLB_DN"), (STHB_DN, "STHB_DN"))):
                k.dma("sync", t[:], bfst_h[:, i * 4096:(i + 1) * 4096], reads=[B("bfspill")], writes=[B(nm)])
            RB = sb("RB", (128, T)); MB = sb("MB", (128, T))
            SQ = sb("SQ", (128, 1024)); STATR = sb("STATR", (1, 1536))
            XNC = sb("XNC", (128, KD * 512), BF16)
            WE = sb("WE", (128, 4 * 1024), BF16)
            WC = sb("WC", (128, 2 * 1024), BF16)
            HA = sb("HA", (128, 32 * 512), BF16)
            HB = sb("HB", (128, 32 * 512), BF16)
            BEX = sb("BEX", (128, 32)); UPB = sb("UPB", (128, 32))
            BCN = sb("BCN", (128, KD)); DWN = sb("DWN", (128, KD))
            k.dma("sync", BEX[:], h_in["sl2_bexp"], writes=[B("BEX")])
            k.dma("sync", UPB[:], h_in["sl2_upbp"], writes=[B("UPB")])
            k.dma("sync", BCN[:], h_in["sl2_bcon"], writes=[B("BCN")])
            k.dma("sync", DWN[:], h_in["sl2_dwnb"], writes=[B("DWN")])
            ln_stats(RB, MB, SQ, STATR, CONSTC[:, 0:1])
            k.barrier()
            for tc in range(4):
                csl = slice(tc * 512, (tc + 1) * 512)
                for kd in range(KD):
                    make_xnb(RB, MB, SQ, XNC, "XNC%d" % kd, kd, csl, slice(kd * 512, (kd + 1) * 512))
                for hog in range(8):
                    for kd in range(KD):
                        wsl = slice(kd * DH + hog * 512, kd * DH + (hog + 1) * 512)
                        k.dma("gpsimd", WE[:, kd * 512:(kd + 1) * 512], h_in["sl2_wexp"][:, wsl],
                              writes=[B("WE%d" % kd)])
                    for kd in range(KD):
                        for hh in range(4):
                            lw = WE[:, kd * 512 + hh * 128: kd * 512 + (hh + 1) * 128]
                            rhs = XNC[:, kd * 512:(kd + 1) * 512]
                            k.op("tensor", lambda e, lw=lw, rhs=rhs, kd=kd, hh=hh: e.matmul(
                                PS[hh][:, 0:512], lw, rhs, start=(kd == 0), stop=(kd == KD - 1)),
                                reads=[B("WE%d" % kd), B("XNC%d" % kd)], psum_rw=PSB[hh])
                    for hh in range(4):
                        ho = hog * 4 + hh
                        k.op("scalar", lambda e, hh=hh, ho=ho: e.activation(
                            out=HA[:, ho * 512:(ho + 1) * 512], in_=PS[hh][:, 0:512], func=AF.Identity,
                            bias=BEX[:, ho:ho + 1]),
                            reads=[PSB[hh], B("BEX")], writes=[B("HA%d" % ho)])
                        free_psum(hh, B("HA%d" % ho))
                if os.environ.get("KDBG2") and tc == 0:
                    o = nc.dram_tensor("dbgp1", [128, 32 * 512], BF16, kind="ExternalOutput").ap()
                    k.dma("sync", o, HA[:], reads=[B("HA%d" % bq) for bq in range(32)], writes=[B("dbg")])
                for bg in range(4):
                    for bb in range(8):
                        b_ = bg * 8 + bb
                        lw = STLB_UP[:, b_ * 128:(b_ + 1) * 128]
                        rhs = HA[:, b_ * 512:(b_ + 1) * 512]
                        k.op("tensor", lambda e, lw=lw, rhs=rhs, bb=bb: e.matmul(
                            PS[bb][:, 0:512], lw, rhs, start=True, stop=True),
                            reads=[B("STLB_UP"), B("HA%d" % b_)], writes=[PSB[bb]])
                    for bb in range(8):
                        b_ = bg * 8 + bb
                        k.op("vector", lambda e, bb=bb, b_=b_: e.tensor_copy(
                            out=HA[:, b_ * 512:(b_ + 1) * 512], in_=PS[bb][:, 0:512]),
                            reads=[PSB[bb]], writes=[B("HA%d" % b_)])
                        free_psum(bb, B("HA%d" % b_))
                if os.environ.get("KDBG2") and tc == 0:
                    o = nc.dram_tensor("dbgp2", [128, 32 * 512], BF16, kind="ExternalOutput").ap()
                    k.dma("sync", o, HA[:], reads=[B("HA%d" % bq) for bq in range(32)], writes=[B("dbg")])
                ha_all = [B("HA%d" % bq) for bq in range(32)]
                p3_last = None
                for g in range(32):
                    src = HA[4 * g:4 * g + 4, :].rearrange("p (b t) -> p b t", b=32)
                    p3_last = k.dma("scalar", HB[:, g * 512:(g + 1) * 512], src,
                                    reads=ha_all, writes=[B("HB%d" % g)])
                for bq in range(32):
                    B("HA%d" % bq).r = [p3_last]
                if os.environ.get("KDBG2") and tc == 0:
                    o = nc.dram_tensor("dbgp3", [128, 32 * 512], BF16, kind="ExternalOutput").ap()
                    k.dma("sync", o, HB[:], reads=[B("HB%d" % g) for g in range(32)], writes=[B("dbg")])
                for gg in range(4):
                    for gb in range(8):
                        g = gg * 8 + gb
                        lw = STHB_UP[:, g * 128:(g + 1) * 128]
                        rhs = HB[:, g * 512:(g + 1) * 512]
                        k.op("tensor", lambda e, lw=lw, rhs=rhs, gb=gb: e.matmul(
                            PS[gb][:, 0:512], lw, rhs, start=True, stop=True),
                            reads=[B("STHB_UP"), B("HB%d" % g)], writes=[PSB[gb]])
                    for gb in range(8):
                        g = gg * 8 + gb
                        k.op("scalar", lambda e, gb=gb, g=g: e.activation(
                            out=HB[:, g * 512:(g + 1) * 512], in_=PS[gb][:, 0:512], func=AF.Gelu,
                            bias=UPB[:, g:g + 1]),
                            reads=[PSB[gb], B("UPB")], writes=[B("HB%d" % g)])
                        free_psum(gb, B("HB%d" % g))
                if os.environ.get("KDBG2") and tc == 0:
                    o = nc.dram_tensor("dbgp4", [128, 32 * 512], BF16, kind="ExternalOutput").ap()
                    k.dma("sync", o, HB[:], reads=[B("HB%d" % g) for g in range(32)], writes=[B("dbg")])
                ha_all = [B("HA%d" % bq) for bq in range(32)]
                last = None
                for g in range(32):
                    dst = HA[4 * g:4 * g + 4, :].rearrange("p (b t) -> p b t", b=32)
                    last = k.dma("scalar", dst, HB[:, g * 512:(g + 1) * 512],
                                 reads=[B("HB%d" % g)] + (ha_all if g == 0 else []),
                                 writes=[])
                for bq in range(32):
                    B("HA%d" % bq).w = last
                    B("HA%d" % bq).r = []
                if os.environ.get("KDBG2") and tc == 0:
                    o = nc.dram_tensor("dbgp5", [128, 32 * 512], BF16, kind="ExternalOutput").ap()
                    k.dma("sync", o, HA[:], reads=[B("HA%d" % bq) for bq in range(32)], writes=[B("dbg")])
                for bg in range(4):
                    for bb in range(8):
                        b_ = bg * 8 + bb
                        lw = STLB_DN[:, b_ * 128:(b_ + 1) * 128]
                        rhs = HA[:, b_ * 512:(b_ + 1) * 512]
                        k.op("tensor", lambda e, lw=lw, rhs=rhs, bb=bb: e.matmul(
                            PS[bb][:, 0:512], lw, rhs, start=True, stop=True),
                            reads=[B("STLB_DN"), B("HA%d" % b_)], writes=[PSB[bb]])
                    for bb in range(8):
                        b_ = bg * 8 + bb
                        k.op("vector", lambda e, bb=bb, b_=b_: e.tensor_copy(
                            out=HA[:, b_ * 512:(b_ + 1) * 512], in_=PS[bb][:, 0:512]),
                            reads=[PSB[bb]], writes=[B("HA%d" % b_)])
                        free_psum(bb, B("HA%d" % b_))
                if os.environ.get("KDBG2") and tc == 0:
                    o = nc.dram_tensor("dbgp6", [128, 32 * 512], BF16, kind="ExternalOutput").ap()
                    k.dma("sync", o, HA[:], reads=[B("HA%d" % bq) for bq in range(32)], writes=[B("dbg")])
                ha_all = [B("HA%d" % bq) for bq in range(32)]
                for g in range(32):
                    src = HA[4 * g:4 * g + 4, :].rearrange("p (b t) -> p b t", b=32)
                    k.dma("scalar", HB[:, g * 512:(g + 1) * 512], src,
                          reads=ha_all, writes=[B("HB%d" % g)])
                for gg in range(4):
                    for gb in range(8):
                        g = gg * 8 + gb
                        lw = STHB_DN[:, g * 128:(g + 1) * 128]
                        rhs = HB[:, g * 512:(g + 1) * 512]
                        k.op("tensor", lambda e, lw=lw, rhs=rhs, gb=gb: e.matmul(
                            PS[gb][:, 0:512], lw, rhs, start=True, stop=True),
                            reads=[B("STHB_DN"), B("HB%d" % g)], writes=[PSB[gb]])
                    for gb in range(8):
                        g = gg * 8 + gb
                        k.op("vector", lambda e, gb=gb, g=g: e.tensor_copy(
                            out=HB[:, g * 512:(g + 1) * 512], in_=PS[gb][:, 0:512]),
                            reads=[PSB[gb]], writes=[B("HB%d" % g)])
                        free_psum(gb, B("HB%d" % g))
                if os.environ.get("KDBG2") and tc == 0:
                    o = nc.dram_tensor("dbgp8", [128, 32 * 512], BF16, kind="ExternalOutput").ap()
                    k.dma("sync", o, HB[:], reads=[B("HB%d" % g) for g in range(32)], writes=[B("dbg")])
                for kh in range(32):
                    wsl = slice(kh * 1024, (kh + 1) * 1024)
                    wbuf = kh % 2
                    k.dma("gpsimd", WC[:, wbuf * 1024:(wbuf + 1) * 1024], h_in["sl2_wcon"][:, wsl],
                          writes=[B("WC%d" % wbuf)])
                    for do in range(8):
                        lw = WC[:, wbuf * 1024 + do * 128: wbuf * 1024 + (do + 1) * 128]
                        rhs = HB[:, kh * 512:(kh + 1) * 512]
                        k.op("tensor", lambda e, lw=lw, rhs=rhs, kh=kh, do=do: e.matmul(
                            PS[do][:, 0:512], lw, rhs, start=(kh == 0), stop=(kh == 31)),
                            reads=[B("WC%d" % wbuf), B("HB%d" % kh)], psum_rw=PSB[do])
                for do in range(8):
                    k.op("scalar", lambda e, do=do: e.activation(
                        out=SQ[:, 0:512], in_=PS[do][:, 0:512], func=AF.Identity, bias=BCN[:, do:do + 1]),
                        reads=[PSB[do], B("BCN")], writes=[B("sqx")])
                    free_psum(do, B("sqx"))
                    xap = X[:, do * T:(do + 1) * T][:, csl]
                    k.op("vector", lambda e, xap=xap, do=do: e.scalar_tensor_tensor(
                        out=xap, in0=SQ[:, 0:512], scalar=DWN[:, do:do + 1], in1=xap,
                        op0=ALU.add, op1=ALU.add),
                        reads=[B("sqx"), B("DWN"), B("X%d" % do)], writes=[B("X%d" % do)])


        delta_h = nc.dram_tensor("delta_h", [128, KD * T], F32, kind="Internal").ap()
        xof_h = nc.dram_tensor("xof_h", [128, KD * T], F32, kind="Internal").ap()
        dtx_h = nc.dram_tensor("dtx_h", [128, KD * T], F32, kind="Internal").ap()
        _kd3 = "ExternalOutput" if os.environ.get("KDBG3") else "Internal"
        dect_h = nc.dram_tensor("dect_h", [128, KD * 4096], BF16, kind=_kd3).ap()
        decn_h = nc.dram_tensor("decn_h", [128, 32 * D], BF16, kind=_kd3).ap()

        def ln_stats_src(srcfn, RB, MB, STATR, eps_col, tagp):
            mu = STATR[0:1, 0:512]
            ex2 = STATR[0:1, 512:1024]
            rstd = STATR[0:1, 1024:1536]
            for c in range(4):
                for kd in range(KD):
                    xap, rdb, sqslot, sqb = srcfn(kd, c)
                    k.op("scalar", lambda e, xap=xap, sqslot=sqslot: e.activation(
                        out=sqslot, in_=xap, func=AF.Square),
                        reads=rdb, writes=[sqb])
                    k.op("tensor", lambda e, xap=xap, kd=kd: e.matmul(
                        PS[0][0:1, 0:512], ONESC[:, 0:1], xap, start=(kd == 0), stop=(kd == KD - 1)),
                        reads=[B("onesc")] + rdb, psum_rw=PSB[0])
                    k.op("tensor", lambda e, kd=kd, sqslot=sqslot: e.matmul(
                        PS[1][0:1, 0:512], ONESC[:, 0:1], sqslot, start=(kd == 0), stop=(kd == KD - 1)),
                        reads=[B("onesc"), sqb], psum_rw=PSB[1])
                k.op("scalar", lambda e: e.mul(mu, PS[0][0:1, 0:512], 1.0 / D), reads=[PSB[0]], writes=[B(tagp + "mu")])
                free_psum(0, B(tagp + "mu"))
                k.op("scalar", lambda e: e.mul(ex2, PS[1][0:1, 0:512], 1.0 / D), reads=[PSB[1]], writes=[B(tagp + "ex2")])
                free_psum(1, B(tagp + "ex2"))
                k.op("vector", lambda e: e.tensor_tensor(out=rstd, in0=mu, in1=mu, op=ALU.mult),
                     reads=[B(tagp + "mu")], writes=[B(tagp + "rstd")])
                k.op("vector", lambda e: e.tensor_tensor(out=rstd, in0=ex2, in1=rstd, op=ALU.subtract),
                     reads=[B(tagp + "ex2"), B(tagp + "rstd")], writes=[B(tagp + "rstd")])
                k.op("scalar", lambda e: e.activation(out=rstd, in_=rstd, func=AF.Sqrt, bias=eps_col[0:1, :]),
                     reads=[B(tagp + "rstd"), B("constc")], writes=[B(tagp + "rstd")])
                k.op("vector", lambda e: e.reciprocal(out=rstd, in_=rstd), reads=[B(tagp + "rstd")], writes=[B(tagp + "rstd")])
                k.op("vector", lambda e: e.tensor_tensor(out=ex2, in0=mu, in1=rstd, op=ALU.mult),
                     reads=[B(tagp + "mu"), B(tagp + "rstd")], writes=[B(tagp + "ex2")])
                k.op("tensor", lambda e: e.matmul(PS[0][:, 0:512], ONES1[0:1, :], rstd, start=True, stop=True),
                     reads=[B("ones1"), B(tagp + "rstd")], writes=[PSB[0]])
                k.op("vector", lambda e, c=c: e.tensor_copy(out=RB[:, c * 512:(c + 1) * 512], in_=PS[0][:, 0:512]),
                     reads=[PSB[0]], writes=[B(tagp + "RB")])
                free_psum(0, B(tagp + "RB"))
                k.op("tensor", lambda e: e.matmul(PS[1][:, 0:512], ONES1[0:1, :], ex2, start=True, stop=True),
                     reads=[B("ones1"), B(tagp + "ex2")], writes=[PSB[1]])
                k.op("vector", lambda e, c=c: e.tensor_copy(out=MB[:, c * 512:(c + 1) * 512], in_=PS[1][:, 0:512]),
                     reads=[PSB[1]], writes=[B(tagp + "MB")])
                free_psum(1, B(tagp + "MB"))

        def xof_chunk_gen(RBx, MBx, rbn, mbn):
            def f(dst, kd, csl, extra_reads=()):
                xap = X[:, kd * T:(kd + 1) * T][:, csl]
                k.op("vector", lambda e, xap=xap, dst=dst, csl=csl: e.tensor_tensor(
                    out=dst, in0=xap, in1=RBx[:, csl], op=ALU.mult),
                    reads=[B("X%d" % kd), B(rbn)] + list(extra_reads), writes=[B("xoft")])
                k.op("vector", lambda e, dst=dst, csl=csl: e.tensor_tensor(
                    out=dst, in0=dst, in1=MBx[:, csl], op=ALU.subtract),
                    reads=[B("xoft"), B(mbn)], writes=[B("xoft")])
                return B("xoft")
            return f

        def sublayer3(sb, ps):
            KPH = int(os.environ.get("KPH", "9"))
            from contextlib import ExitStack as ES3
            RBo = sb("RBo", (128, T)); MBo = sb("MBo", (128, T))
            BRB = sb("BRB", (64, T)); CRB = sb("CRB", (64, T))
            ACOL = sb("ACOL", (128, KD * 64))
            SQ3 = sb("SQ3", (128, T))
            COLS = sb("COLS3", (128, 8 * KD))
            WCV = sb("WCV", (128, 32))
            IDB = sb("IDB", (128, 128), BF16)
            dsc = lambda i: COLS[:, 0 * KD + i:0 * KD + i + 1]
            lng = lambda i: COLS[:, 1 * KD + i:1 * KD + i + 1]
            lnb = lambda i: COLS[:, 2 * KD + i:2 * KD + i + 1]
            bdt = lambda i: COLS[:, 3 * KD + i:3 * KD + i + 1]
            bcv = lambda i: COLS[:, 4 * KD + i:4 * KD + i + 1]
            bwm = lambda i: COLS[:, 5 * KD + i:5 * KD + i + 1]
            onec = COLS[:, 6 * KD:6 * KD + 1]
            for nm, col in (("sl3_dssm", 0), ("sl3_lng", 1), ("sl3_lnb", 2), ("sl3_bdt", 3),
                            ("sl3_bconv", 4), ("sl3_bwm", 5)):
                k.dma("sync", COLS[:, col * KD:(col + 1) * KD], h_in[nm], writes=[B("cols3")])
            k.op("vector", lambda e: e.memset(onec, 1.0), writes=[B("cols3")])
            k.dma("sync", WCV[:], h_in["sl3_wconv"], writes=[B("wcv")])
            k.dma("gpsimd", IDB[:], h_in["eyerep"][:, 0:128], writes=[B("idb")])
            k.dma("sync", ACOL[:], h_in["sl3_alogc"], writes=[B("acol")])
            k.op("scalar", lambda e: e.activation(out=ACOL[:], in_=ACOL[:], func=AF.Exp),
                 reads=[B("acol")], writes=[B("acol")])
            k.op("vector", lambda e: e.tensor_scalar(out=ACOL[:], in0=ACOL[:], scalar1=-1.0, scalar2=None, op0=ALU.mult),
                 reads=[B("acol")], writes=[B("acol")])

            xof_chunk = xof_chunk_gen(RBo, MBo, "oRB", "oMB")

            with ES3() as s_ab:
                sbb = lambda n2, sh, dt=F32: s_ab.enter_context(nc.sbuf_tensor("ab_" + n2, list(sh), dt))
                XOB = sbb("XOB", (128, KD * T), BF16)
                with ES3() as s_a:
                    sba = lambda n2, sh, dt=F32: s_a.enter_context(nc.sbuf_tensor("a_" + n2, list(sh), dt))
                    STATR = sba("STATR", (1, 1536))
                    RB2 = sba("RB2", (128, T)); MB2 = sba("MB2", (128, T))
                    XIBT = sba("XIBT", (128, T), BF16)
                    DTIB = sba("DTIB", (64, T), BF16)
                    WXP = sba("WXP", (128, KD * 192), BF16)
                    WDT = sba("WDT", (64, D), BF16)
                    XOFT = sba("XOFT", (128, T))
                    CVP = sba("CVP", (128, T + 3))
                    DELT = sba("DELT", (128, T))
                    DTXT = sba("DTXT", (128, T))
                    k.dma("gpsimd", WXP[:], h_in["sl3_wxp"], writes=[B("wxp")])
                    k.dma("gpsimd", WDT[:], h_in["sl3_wdt"], writes=[B("wdt")])
                    def src_x(kd, c):
                        xap = X[:, kd * T + c * 512: kd * T + (c + 1) * 512]
                        slot = SQ3[:, (kd % 2) * 512:(kd % 2) * 512 + 512]
                        return xap, [B("X%d" % kd)], slot, B("sq3%d" % (kd % 2))
                    ln_stats_src(src_x, RBo, MBo, STATR, CONSTC[:, 0:1], "o")
                    k.barrier()
                    for kd in range(KD):
                        xb = xof_chunk(XOFT[:, 0:T], kd, slice(0, T))
                        k.op("vector", lambda e, kd=kd: e.tensor_copy(
                            out=XOB[:, kd * T:(kd + 1) * T], in_=XOFT[:, 0:T]),
                            reads=[xb], writes=[B("XOB%d" % kd)])
                        k.dma("sync", xof_h[:, kd * T:(kd + 1) * T], XOFT[:, 0:T],
                              reads=[xb], writes=[B("xofsp")])
                    k.barrier()
                    def src_xof(kd, c):
                        csl = slice(c * 512, (c + 1) * 512)
                        slot2 = XOFT[:, c * 512:(c + 1) * 512]
                        xb = xof_chunk(slot2, kd, csl)
                        sq = SQ3[:, (kd % 2) * 512:(kd % 2) * 512 + 512]
                        return slot2, [xb], sq, B("sq3%d" % (kd % 2))
                    ln_stats_src(src_xof, RB2, MB2, STATR, CONSTC[:, 0:1], "i")
                    k.barrier()
                    for c in range(4):
                        csl = slice(c * 512, (c + 1) * 512)
                        for kd in range(KD):
                            xb = xof_chunk(XOFT[:, 0:512], kd, csl)
                            k.op("vector", lambda e, csl=csl: e.tensor_tensor(
                                out=XOFT[:, 0:512], in0=XOFT[:, 0:512], in1=RB2[:, csl], op=ALU.mult),
                                reads=[xb, B("iRB")], writes=[B("xoft")])
                            k.op("vector", lambda e, csl=csl: e.tensor_tensor(
                                out=XOFT[:, 0:512], in0=XOFT[:, 0:512], in1=MB2[:, csl], op=ALU.subtract),
                                reads=[B("xoft"), B("iMB")], writes=[B("xoft")])
                            k.op("scalar", lambda e, kd=kd: e.activation(
                                out=XIBT[:, 0:512], in_=XOFT[:, 0:512], func=AF.Identity,
                                bias=lnb(kd), scale=lng(kd)),
                                reads=[B("xoft"), B("cols3")], writes=[B("xibt")])
                            for mi, (m0, m1, bank) in enumerate(((0, 64, 0), (64, 128, 1), (128, 192, 2))):
                                lw = WXP[:, kd * 192 + m0: kd * 192 + m1]
                                k.op("tensor", lambda e, lw=lw, kd=kd, bank=bank, m1=m1, m0=m0: e.matmul(
                                    PS[bank][0:m1 - m0, 0:512], lw, XIBT[:, 0:512],
                                    start=(kd == 0), stop=(kd == KD - 1)),
                                    reads=[B("wxp"), B("xibt")], psum_rw=PSB[bank])
                        k.op("scalar", lambda e, csl=csl: e.copy(out=DTIB[:, csl], in_=PS[0][0:64, 0:512]),
                             reads=[PSB[0]], writes=[B("dtib")])
                        free_psum(0, B("dtib"))
                        k.op("scalar", lambda e, csl=csl: e.copy(out=BRB[:, csl], in_=PS[1][0:64, 0:512]),
                             reads=[PSB[1]], writes=[B("brb")])
                        free_psum(1, B("brb"))
                        k.op("scalar", lambda e, csl=csl: e.copy(out=CRB[:, csl], in_=PS[2][0:64, 0:512]),
                             reads=[PSB[2]], writes=[B("crb")])
                        free_psum(2, B("crb"))
                    for od in range(KD):
                        for c in range(4):
                            csl = slice(c * 512, (c + 1) * 512)
                            lw = WDT[:, od * 128:(od + 1) * 128]
                            k.op("tensor", lambda e, lw=lw, csl=csl: e.matmul(
                                PS[3][:, 0:512], lw, DTIB[:, csl], start=True, stop=True),
                                reads=[B("wdt"), B("dtib")], writes=[PSB[3]])
                            k.op("scalar", lambda e, od=od, csl=csl: e.activation(
                                out=SQ3[:, 0:512], in_=PS[3][:, 0:512], func=AF.Relu, bias=bdt(od)),
                                reads=[PSB[3], B("cols3")], writes=[B("sq3c")])
                            k.op("scalar", lambda e, od=od, csl=csl: e.activation(
                                out=DELT[:, csl], in_=PS[3][:, 0:512], func=AF.Abs, bias=bdt(od)),
                                reads=[PSB[3], B("cols3")], writes=[B("delt")])
                            free_psum(3, B("delt"))
                            k.op("scalar", lambda e, csl=csl: e.activation(
                                out=DELT[:, csl], in_=DELT[:, csl], func=AF.Exp, scale=-1.0),
                                reads=[B("delt")], writes=[B("delt")])
                            k.op("scalar", lambda e, csl=csl: e.activation(
                                out=DELT[:, csl], in_=DELT[:, csl], func=AF.Ln, bias=onec),
                                reads=[B("delt"), B("cols3")], writes=[B("delt")])
                            k.op("vector", lambda e, csl=csl: e.tensor_tensor(
                                out=DELT[:, csl], in0=DELT[:, csl], in1=SQ3[:, 0:512], op=ALU.add),
                                reads=[B("delt"), B("sq3c")], writes=[B("delt")])
                        k.dma("sync", delta_h[:, od * T:(od + 1) * T], DELT[:], reads=[B("delt")], writes=[B("dspill")])
                        for c in range(4):
                            csl = slice(c * 512, (c + 1) * 512)
                            xb = xof_chunk(XOFT[:, 0:512], od, csl)
                            k.op("vector", lambda e, csl=csl: e.tensor_tensor(
                                out=XOFT[:, 0:512], in0=XOFT[:, 0:512], in1=RB2[:, csl], op=ALU.mult),
                                reads=[xb, B("iRB")], writes=[B("xoft")])
                            k.op("vector", lambda e, csl=csl: e.tensor_tensor(
                                out=XOFT[:, 0:512], in0=XOFT[:, 0:512], in1=MB2[:, csl], op=ALU.subtract),
                                reads=[B("xoft"), B("iMB")], writes=[B("xoft")])
                            k.op("scalar", lambda e, od=od: e.activation(
                                out=XOFT[:, 0:512], in_=XOFT[:, 0:512], func=AF.Identity,
                                bias=lnb(od), scale=lng(od)),
                                reads=[B("xoft"), B("cols3")], writes=[B("xoft")])
                            k.op("vector", lambda e, csl=csl: e.tensor_tensor(
                                out=DTXT[:, csl], in0=DELT[:, csl], in1=XOFT[:, 0:512], op=ALU.mult),
                                reads=[B("delt"), B("xoft")], writes=[B("dtxt")])
                        k.dma("sync", dtx_h[:, od * T:(od + 1) * T], DTXT[:], reads=[B("dtxt")], writes=[B("dxspill")])
                    k.barrier()
                    for kd in range(KD):
                        xb = xof_chunk(XOFT[:, 0:T], kd, slice(0, T))
                        k.op("vector", lambda e: e.memset(CVP[:, 0:3], 0.0), writes=[B("cvp")])
                        k.op("vector", lambda e: e.tensor_copy(out=CVP[:, 3:T + 3], in_=XOFT[:, 0:T]),
                             reads=[xb], writes=[B("cvp")])
                        k.op("vector", lambda e, kd=kd: e.tensor_scalar(
                            out=SQ3[:], in0=CVP[:, 0:T], scalar1=WCV[:, kd * 4:kd * 4 + 1], scalar2=None, op0=ALU.mult),
                            reads=[B("cvp"), B("wcv")], writes=[B("sq3c")])
                        for j in (1, 2, 3):
                            k.op("vector", lambda e, kd=kd, j=j: e.scalar_tensor_tensor(
                                out=SQ3[:], in0=CVP[:, j:T + j], scalar=WCV[:, kd * 4 + j:kd * 4 + j + 1],
                                in1=SQ3[:], op0=ALU.mult, op1=ALU.add),
                                reads=[B("cvp"), B("wcv"), B("sq3c")], writes=[B("sq3c")])
                        k.op("scalar", lambda e, kd=kd: e.activation(out=SQ3[:], in_=SQ3[:], func=AF.Silu,
                                                                     bias=bcv(kd)),
                             reads=[B("sq3c"), B("cols3")], writes=[B("sq3c")])
                        k.op("vector", lambda e, kd=kd: e.scalar_tensor_tensor(
                            out=X[:, kd * T:(kd + 1) * T], in0=SQ3[:], scalar=0.15,
                            in1=X[:, kd * T:(kd + 1) * T], op0=ALU.mult, op1=ALU.add),
                            reads=[B("sq3c"), B("X%d" % kd)], writes=[B("X%d" % kd)])
                k.barrier()
                if KPH < 2:
                    return
                with ES3() as s_b1:
                    sb1 = lambda n2, sh, dt=F32: s_b1.enter_context(nc.sbuf_tensor("b1_" + n2, list(sh), dt))
                    WWM = sb1("WWM", (128, KD * D), BF16)
                    k.dma("gpsimd", WWM[:], h_in["sl3_wwm"], writes=[B("wwm")])
                    for od in range(KD):
                        for kd in range(KD):
                            for c in range(4):
                                lw = WWM[:, kd * D + od * 128: kd * D + (od + 1) * 128]
                                rhs = XOB[:, kd * T + c * 512: kd * T + (c + 1) * 512]
                                k.op("tensor", lambda e, lw=lw, rhs=rhs, kd=kd, c=c: e.matmul(
                                    PS[c][:, 0:512], lw, rhs, start=(kd == 0), stop=(kd == KD - 1)),
                                    reads=[B("wwm"), B("XOB%d" % kd)], psum_rw=PSB[c])
                        for c in range(4):
                            k.op("scalar", lambda e, c=c, od=od: e.activation(
                                out=SQ3[:, 0:512], in_=PS[c][:, 0:512], func=AF.Identity, bias=bwm(od)),
                                reads=[PSB[c], B("cols3")], writes=[B("sq3c")])
                            free_psum(c, B("sq3c"))
                            xap = X[:, od * T + c * 512: od * T + (c + 1) * 512]
                            k.op("vector", lambda e, xap=xap: e.scalar_tensor_tensor(
                                out=xap, in0=SQ3[:, 0:512], scalar=0.2, in1=xap, op0=ALU.mult, op1=ALU.add),
                                reads=[B("sq3c"), B("X%d" % od)], writes=[B("X%d" % od)])
                k.barrier()
                if KPH < 3:
                    return
                with ES3() as s_b2:
                    sb2a = lambda n2, sh, dt=F32: s_b2.enter_context(nc.sbuf_tensor("b2_" + n2, list(sh), dt))
                    GT = sb2a("GT", (128, 4 * 4096), BF16)
                    EPIT = sb2a("EPIT", (128, 4096), BF16)
                    WDC1 = sb2a("WDC1", (128, 512), BF16)
                    WDC2 = sb2a("WDC2", (128, 4 * D), BF16)
                    CB2 = sb2a("CB2", (128, 16))
                    B2B = sb2a("B2B", (128, D))
                    DECTT = sb2a("DECTT", (128, 4096), BF16)
                    k.dma("gpsimd", EPIT[:], h_in["sl3_episT"], writes=[B("epit")])
                    k.dma("gpsimd", WDC1[:], h_in["sl3_wdc1"], writes=[B("wdc1")])
                    k.dma("gpsimd", WDC2[:], h_in["sl3_wdc2"], writes=[B("wdc2")])
                    k.dma("sync", CB2[:, 0:4], h_in["sl3_bdc1"], writes=[B("cb2")])
                    k.dma("sync", CB2[:, 4:12], h_in["sl3_bdc2"], writes=[B("cb2")])
                    k.dma("sync", B2B[:], h_in["sl3_bdc2b"], writes=[B("b2b")])
                    for m in range(4):
                        for c in range(8):
                            lw = WDC1[:, m * 128:(m + 1) * 128]
                            k.op("tensor", lambda e, lw=lw, m=m, c=c: e.matmul(
                                PS[c][:, 0:512], lw, EPIT[:, c * 512:(c + 1) * 512], start=True, stop=True),
                                reads=[B("wdc1"), B("epit")], writes=[PSB[c]])
                        for c in range(8):
                            k.op("scalar", lambda e, m=m, c=c: e.activation(
                                out=GT[:, m * 4096 + c * 512: m * 4096 + (c + 1) * 512],
                                in_=PS[c][:, 0:512], func=AF.Gelu, bias=CB2[:, m:m + 1]),
                                reads=[PSB[c], B("cb2")], writes=[B("gt")])
                            free_psum(c, B("gt"))
                    for od in range(KD):
                        for kh in range(4):
                            for c in range(8):
                                lw = WDC2[:, kh * D + od * 128: kh * D + (od + 1) * 128]
                                rhs = GT[:, kh * 4096 + c * 512: kh * 4096 + (c + 1) * 512]
                                k.op("tensor", lambda e, lw=lw, rhs=rhs, kh=kh, c=c: e.matmul(
                                    PS[c][:, 0:512], lw, rhs, start=(kh == 0), stop=(kh == 3)),
                                    reads=[B("wdc2"), B("gt")], psum_rw=PSB[c])
                        for c in range(8):
                            k.op("scalar", lambda e, od=od, c=c: e.activation(
                                out=DECTT[:, c * 512:(c + 1) * 512], in_=PS[c][:, 0:512],
                                func=AF.Identity, bias=CB2[:, 4 + od:5 + od]),
                                reads=[PSB[c], B("cb2")], writes=[B("dectt")])
                            free_psum(c, B("dectt"))
                        k.dma("sync", dect_h[:, od * 4096:(od + 1) * 4096], DECTT[:], reads=[B("dectt")], writes=[B("dectsp")])
                    for st in range(32):
                        for kh in range(4):
                            for dc in range(2):
                                lw = GT[:, kh * 4096 + st * 128:(kh * 4096 + (st + 1) * 128)]
                                rhs = WDC2[:, kh * D + dc * 512: kh * D + (dc + 1) * 512]
                                k.op("tensor", lambda e, lw=lw, rhs=rhs, kh=kh, dc=dc: e.matmul(
                                    PS[4 + dc][:, 0:512], lw, rhs, start=(kh == 0), stop=(kh == 3)),
                                    reads=[B("gt"), B("wdc2")], psum_rw=PSB[4 + dc])
                        for dc in range(2):
                            k.op("vector", lambda e, dc=dc: e.tensor_tensor(
                                out=DECTT.bitcast(F32)[:, 0:512] if False else SQ3[:, 0:512],
                                in0=PS[4 + dc][:, 0:512], in1=B2B[:, dc * 512:(dc + 1) * 512], op=ALU.add),
                                reads=[PSB[4 + dc], B("b2b")], writes=[B("sq3c")])
                            free_psum(4 + dc, B("sq3c"))
                            k.op("vector", lambda e, st=st, dc=dc: e.tensor_copy(
                                out=DECTT[:, dc * 512:(dc + 1) * 512], in_=SQ3[:, 0:512]),
                                reads=[B("sq3c")], writes=[B("dectt")])
                        k.dma("sync", decn_h[:, st * D: st * D + D], DECTT[:, 0:1024], reads=[B("dectt")], writes=[B("decnsp")])
                k.barrier()
                if KPH < 4:
                    return
                with ES3() as s_b2b:
                    sbeb = lambda n2, sh, dt=F32: s_b2b.enter_context(nc.sbuf_tensor("b2b_" + n2, list(sh), dt))
                    ES_ = sbeb("ES", (128, 4096))
                    ES2 = sbeb("ES2", (128, 4096))
                    EST2 = sbeb("EST2", (128, 32 * 256), BF16)
                    IDF = sbeb("IDF", (128, 128))
                    DECTS = sbeb("DECTS", (128, 2 * 4096), BF16)
                    DECNS = sbeb("DECNS", (128, 2 * D), BF16)
                    MXC = sbeb("MXC", (128, 4))
                    k.dma("sync", IDF[:], h_in["eyerep"][:, 0:128], writes=[B("idf")])
                    for tg in range(8):
                        for tt in range(2):
                            ti = tg * 2 + tt
                            for kd in range(KD):
                                wb = kd % 2
                                k.dma("sync", DECTS[:, wb * 4096:(wb + 1) * 4096],
                                      dect_h[:, kd * 4096:(kd + 1) * 4096], writes=[B("dects%d" % wb)])
                                lw = XOB[:, kd * T + ti * 128: kd * T + (ti + 1) * 128]
                                for sc in range(8):
                                    rhs = DECTS[:, wb * 4096 + sc * 512: wb * 4096 + (sc + 1) * 512]
                                    k.op("tensor", lambda e, lw=lw, rhs=rhs, kd=kd, sc=sc: e.matmul(
                                        PS[sc][:, 0:512], lw, rhs, start=(kd == 0), stop=(kd == KD - 1)),
                                        reads=[B("XOB%d" % kd), B("dects%d" % wb)], psum_rw=PSB[sc])
                            for sc in range(8):
                                k.op("scalar", lambda e, sc=sc: e.mul(
                                    ES_[:, sc * 512:(sc + 1) * 512], PS[sc][:, 0:512], 0.03125),
                                    reads=[PSB[sc]], writes=[B("es_")])
                                free_psum(sc, B("es_"))
                            k.op("vector", lambda e: e.tensor_reduce(
                                out=MXC[:, 0:1], in_=ES_[:], axis=mybir.AxisListType.X, op=ALU.max, negate=True),
                                reads=[B("es_")], writes=[B("mxc")])
                            k.op("scalar", lambda e: e.activation(
                                out=ES2[:], in_=ES_[:], func=AF.Exp, bias=MXC[:, 0:1], accum_out=MXC[:, 1:2]),
                                reads=[B("es_"), B("mxc")], writes=[B("es2"), B("mxc2")])
                            k.op("scalar", lambda e: e.activation(
                                out=MXC[:, 2:3], in_=MXC[:, 1:2], func=AF.Ln),
                                reads=[B("mxc2")], writes=[B("mxc3")])
                            k.op("vector", lambda e: e.tensor_tensor(
                                out=MXC[:, 2:3], in0=MXC[:, 0:1], in1=MXC[:, 2:3], op=ALU.subtract),
                                reads=[B("mxc"), B("mxc3")], writes=[B("mxc3")])
                            k.op("scalar", lambda e: e.activation(
                                out=ES_[:], in_=ES_[:], func=AF.Exp, bias=MXC[:, 2:3]),
                                reads=[B("es_"), B("mxc3")], writes=[B("es_")])

                            for st in range(32):
                                k.op("tensor", lambda e, st=st: e.transpose(
                                    PS[7][:, 0:128], ES_[:, st * 128:(st + 1) * 128], IDF[:]),
                                    reads=[B("es_"), B("idf")], writes=[PSB[7]])
                                k.op("vector", lambda e, st=st, tt=tt: e.tensor_copy(
                                    out=EST2[:, st * 256 + tt * 128: st * 256 + (tt + 1) * 128],
                                    in_=PS[7][:, 0:128]),
                                    reads=[PSB[7]], writes=[B("est2")])
                                free_psum(7, B("est2"))
                        for st in range(32):
                            wb = st % 2
                            k.dma("sync", DECNS[:, wb * D:(wb + 1) * D], decn_h[:, st * D:(st + 1) * D],
                                  writes=[B("decns%d" % wb)])
                            for do in range(KD):
                                lw = DECNS[:, wb * D + do * 128: wb * D + (do + 1) * 128]
                                rhs = EST2[:, st * 256:(st + 1) * 256]
                                k.op("tensor", lambda e, lw=lw, rhs=rhs, st=st, do=do: e.matmul(
                                    PS[do][:, 0:256], lw, rhs, start=(st == 0), stop=(st == 31)),
                                    reads=[B("decns%d" % wb), B("est2")], psum_rw=PSB[do])
                        for do in range(KD):
                            k.op("scalar", lambda e, do=do: e.copy(out=ES2[:, 0:256], in_=PS[do][:, 0:256]),
                                 reads=[PSB[do]], writes=[B("es2")])
                            free_psum(do, B("es2"))
                            xap = X[:, do * T + tg * 256: do * T + (tg + 1) * 256]
                            k.op("vector", lambda e, xap=xap, do=do: e.scalar_tensor_tensor(
                                out=xap, in0=ES2[:, 0:256], scalar=0.06, in1=xap, op0=ALU.mult, op1=ALU.add),
                                reads=[B("es2"), B("X%d" % do)], writes=[B("X%d" % do)])
                k.barrier()
                if KPH < 5:
                    return
                with ES3() as s_b3:
                    sb3 = lambda n2, sh, dt=F32: s_b3.enter_context(nc.sbuf_tensor("b3_" + n2, list(sh), dt))
                    AOT = sb3("AOT", (128, KD * T), BF16)
                    KT = sb3("KT", (128, 2 * T), BF16)
                    VN = sb3("VN", (128, 16 * 256), BF16)
                    QT = sb3("QT", (128, T), BF16)
                    with ES3() as s_kv:
                        WKV = s_kv.enter_context(nc.sbuf_tensor("b3_WKV", [128, KD * 512], BF16))
                        k.dma("gpsimd", WKV[:, 0:KD * 256].rearrange("p (a b) -> p a b", a=KD), h_in["sl3_wk"].rearrange("p (a b) -> p a b", a=KD), writes=[B("wkv")])
                        k.dma("gpsimd", WKV[:, KD * 256:].rearrange("p (a b) -> p a b", a=KD), h_in["sl3_wv"].rearrange("p (a b) -> p a b", a=KD), writes=[B("wkv")])
                        for ot in range(2):
                            for kd in range(KD):
                                for c in range(4):
                                    lw = WKV[:, kd * 256 + ot * 128: kd * 256 + (ot + 1) * 128]
                                    rhs = XOB[:, kd * T + c * 512: kd * T + (c + 1) * 512]
                                    k.op("tensor", lambda e, lw=lw, rhs=rhs, kd=kd, c=c: e.matmul(
                                        PS[c][:, 0:512], lw, rhs, start=(kd == 0), stop=(kd == KD - 1)),
                                        reads=[B("wkv"), B("XOB%d" % kd)], psum_rw=PSB[c])
                            for c in range(4):
                                k.op("scalar", lambda e, ot=ot, c=c: e.copy(
                                    out=KT[:, ot * T + c * 512: ot * T + (c + 1) * 512], in_=PS[c][:, 0:512]),
                                    reads=[PSB[c]], writes=[B("kt")])
                                free_psum(c, B("kt"))
                        for ti in range(16):
                            for kd in range(KD):
                                lw = XOB[:, kd * T + ti * 128: kd * T + (ti + 1) * 128]
                                rhs = WKV[:, KD * 256 + kd * 256: KD * 256 + (kd + 1) * 256]
                                k.op("tensor", lambda e, lw=lw, rhs=rhs, kd=kd, ti=ti: e.matmul(
                                    PS[4 + ti % 2][:, 0:256], lw, rhs, start=(kd == 0), stop=(kd == KD - 1)),
                                    reads=[B("wkv"), B("XOB%d" % kd)], psum_rw=PSB[4 + ti % 2])
                            k.op("scalar", lambda e, ti=ti: e.copy(
                                out=VN[:, ti * 256:(ti + 1) * 256], in_=PS[4 + ti % 2][:, 0:256]),
                                reads=[PSB[4 + ti % 2]], writes=[B("vn")])
                            free_psum(4 + ti % 2, B("vn"))
                    k.barrier()
                    WQS = sb3("WQS", (128, KD * 128), BF16)
                    BIAS = sb3("BIAS", (128, 768))
                    SC3 = sb3("SC3", (128, 768))
                    SC3B = sb3("SC3B", (128, 768))
                    ATT3 = sb3("ATT3", (128, 768), BF16)
                    NMX = sb3("NMX", (128, 4))
                    KWIN = sb3("KWIN", (128, 768), BF16)
                    AVS = sb3("AVS", (64, 128), BF16)
                    IDF3 = sb3("IDF3", (128, 128))
                    KB3S = int(os.environ.get("KB3S", "6"))
                    k.dma("sync", IDF3[:], h_in["eyerep"][:, 0:128], writes=[B("idf3")])
                    for j in range(int(os.environ.get("KB3J", str(KD)))):
                        if KB3S < 2:
                            break
                        for kd in range(KD):
                            k.dma("gpsimd", WQS[:, kd * 128:(kd + 1) * 128],
                                  h_in["sl3_wq"][:, kd * D + j * 128: kd * D + (j + 1) * 128],
                                  writes=[B("wqs")])
                        for c in range(4):
                            for kd in range(KD):
                                lw = WQS[:, kd * 128:(kd + 1) * 128]
                                rhs = XOB[:, kd * T + c * 512: kd * T + (c + 1) * 512]
                                k.op("tensor", lambda e, lw=lw, rhs=rhs, kd=kd, c=c: e.matmul(
                                    PS[c][:, 0:512], lw, rhs, start=(kd == 0), stop=(kd == KD - 1)),
                                    reads=[B("wqs"), B("XOB%d" % kd)], psum_rw=PSB[c])
                        for c in range(4):
                            k.op("scalar", lambda e, c=c: e.copy(
                                out=QT[:, c * 512:(c + 1) * 512], in_=PS[c][:, 0:512]),
                                reads=[PSB[c]], writes=[B("qt")])
                            free_psum(c, B("qt"))
                        kvh = (2 * j) // 4
                        for qt_ in range(16):
                            blk = qt_ // 2
                            par = qt_ % 2
                            kstart = max(0, (blk - 1) * 256)
                            kend = min(T, (blk + 2) * 256)
                            klen = kend - kstart
                            koff = 256 - (blk * 256 - kstart)
                            ksrc = KT[(kvh % 2) * 64:(kvh % 2) * 64 + 64,
                                      (kvh // 2) * T + kstart: (kvh // 2) * T + kstart + klen]
                            k.dma("scalar", KWIN[0:64, 0:klen], ksrc, reads=[B("kt")], writes=[B("kwin")])
                            k.dma("scalar", KWIN[64:128, 0:klen], ksrc, reads=[B("kt")], writes=[B("kwin")])
                            for h2 in range(2):
                                hh = 2 * j + h2
                                k.dma("sync", BIAS[:, 0:klen],
                                      h_in["sl3_bias"][hh][par][:, koff:koff + klen], writes=[B("bias0")])
                                qsl = QT[h2 * 64:(h2 + 1) * 64, qt_ * 128:(qt_ + 1) * 128]
                                nch = (klen + 511) // 512
                                for ch in range(nch):
                                    cl = min(512, klen - ch * 512)
                                    rhs = KWIN[h2 * 64:(h2 + 1) * 64, ch * 512: ch * 512 + cl]
                                    k.op("tensor", lambda e, qsl=qsl, rhs=rhs, h2=h2, ch=ch, cl=cl: e.matmul(
                                        PS[h2 * 2 + ch][:, 0:cl], qsl, rhs, start=True, stop=True),
                                        reads=[B("qt"), B("kwin")], writes=[PSB[h2 * 2 + ch]])
                                if KB3S < 3:
                                    for ch in range(nch):
                                        free_psum(h2 * 2 + ch, PSB[h2 * 2 + ch])
                                    continue
                                for ch in range(nch):
                                    cl = min(512, klen - ch * 512)
                                    k.op("vector", lambda e, h2=h2, ch=ch, cl=cl: e.scalar_tensor_tensor(
                                        out=SC3[:, ch * 512: ch * 512 + cl], in0=PS[h2 * 2 + ch][:, 0:cl],
                                        scalar=0.125, in1=BIAS[:, ch * 512: ch * 512 + cl],
                                        op0=ALU.mult, op1=ALU.add),
                                        reads=[PSB[h2 * 2 + ch], B("bias0")], writes=[B("sc3")])
                                    free_psum(h2 * 2 + ch, B("sc3"))
                                k.op("vector", lambda e, klen=klen: e.tensor_reduce(
                                    out=NMX[:, 0:1], in_=SC3[:, 0:klen], axis=mybir.AxisListType.X,
                                    op=ALU.max, negate=True),
                                    reads=[B("sc3")], writes=[B("nmx")])
                                k.op("scalar", lambda e, klen=klen: e.activation(
                                    out=SC3B[:, 0:klen], in_=SC3[:, 0:klen], func=AF.Exp,
                                    bias=NMX[:, 0:1], accum_out=NMX[:, 1:2]),
                                    reads=[B("sc3"), B("nmx")], writes=[B("sc3b"), B("nmx2")])
                                k.op("scalar", lambda e: e.activation(
                                    out=NMX[:, 2:3], in_=NMX[:, 1:2], func=AF.Ln),
                                    reads=[B("nmx2")], writes=[B("nmx3")])
                                k.op("vector", lambda e: e.tensor_tensor(
                                    out=NMX[:, 2:3], in0=NMX[:, 0:1], in1=NMX[:, 2:3], op=ALU.subtract),
                                    reads=[B("nmx"), B("nmx3")], writes=[B("nmx3")])
                                k.op("scalar", lambda e, klen=klen: e.activation(
                                    out=SC3[:, 0:klen], in_=SC3[:, 0:klen], func=AF.Exp, bias=NMX[:, 2:3]),
                                    reads=[B("sc3"), B("nmx3")], writes=[B("sc3")])
                                if KB3S < 4:
                                    continue
                                nkt = klen // 128
                                for kt_ in range(nkt):
                                    k.op("tensor", lambda e, kt_=kt_: e.transpose(
                                        PS[6][:, 0:128], SC3[:, kt_ * 128:(kt_ + 1) * 128], IDF3[:]),
                                        reads=[B("sc3"), B("idf3")], writes=[PSB[6]])
                                    k.op("vector", lambda e, kt_=kt_: e.tensor_copy(
                                        out=ATT3[:, kt_ * 128:(kt_ + 1) * 128], in_=PS[6][:, 0:128]),
                                        reads=[PSB[6]], writes=[B("att3")])
                                    free_psum(6, B("att3"))
                                if KB3S < 5:
                                    continue
                                for kt_ in range(nkt):
                                    tkt = kstart // 128 + kt_
                                    lw = VN[:, tkt * 256 + kvh * 64: tkt * 256 + kvh * 64 + 64]
                                    rhs = ATT3[:, kt_ * 128:(kt_ + 1) * 128]
                                    k.op("tensor", lambda e, lw=lw, rhs=rhs, kt_=kt_, nkt=nkt: e.matmul(
                                        PS[7][0:64, 0:128], lw, rhs,
                                        start=(kt_ == 0), stop=(kt_ == nkt - 1)),
                                        reads=[B("vn"), B("att3")], psum_rw=PSB[7])
                                k.op("scalar", lambda e, h2=h2: e.copy(out=AVS[0:64, 0:128], in_=PS[7][0:64, 0:128]),
                                     reads=[PSB[7]], writes=[B("avs")])
                                free_psum(7, B("avs"))
                                dsta = AOT[h2 * 64:(h2 + 1) * 64, j * T + qt_ * 128: j * T + (qt_ + 1) * 128]
                                k.dma("scalar", dsta, AVS[0:64, 0:128],
                                      reads=[B("avs")], writes=[B("aot%d" % j)])
                    WOS = WQS
                    for od in range(KD if KB3S >= 6 else 0):
                        for ko in range(KD):
                            k.dma("gpsimd", WOS[:, ko * 128:(ko + 1) * 128],
                                  h_in["sl3_wo"][:, ko * D + od * 128: ko * D + (od + 1) * 128],
                                  writes=[B("wqs")])
                        for ko in range(KD):
                            for c in range(4):
                                lw = WOS[:, ko * 128:(ko + 1) * 128]
                                if os.environ.get("KWOX"):
                                    rhs = XOB[:, ko * T + c * 512: ko * T + (c + 1) * 512]
                                    rb_ = B("XOB%d" % ko)
                                else:
                                    rhs = AOT[:, ko * T + c * 512: ko * T + (c + 1) * 512]
                                    rb_ = B("aot%d" % ko)
                                k.op("tensor", lambda e, lw=lw, rhs=rhs, ko=ko, c=c: e.matmul(
                                    PS[c][:, 0:512], lw, rhs, start=(ko == 0), stop=(ko == KD - 1)),
                                    reads=[B("wqs"), rb_], psum_rw=PSB[c])
                        for c in range(4):
                            k.op("scalar", lambda e, c=c: e.copy(out=SQ3[:, 0:512], in_=PS[c][:, 0:512]),
                                 reads=[PSB[c]], writes=[B("sq3c")])
                            free_psum(c, B("sq3c"))
                            xap = X[:, od * T + c * 512: od * T + (c + 1) * 512]
                            k.op("vector", lambda e, xap=xap: e.scalar_tensor_tensor(
                                out=xap, in0=SQ3[:, 0:512], scalar=0.2, in1=xap, op0=ALU.mult, op1=ALU.add),
                                reads=[B("sq3c"), B("X%d" % od)], writes=[B("X%d" % od)])
                k.barrier()
            k.barrier()
            if KPH < 6:
                return
            with ES3() as s_c:
                sbc = lambda n2, sh, dt=F32: s_c.enter_context(nc.sbuf_tensor("c_" + n2, list(sh), dt))
                YSB = sbc("YSB", (128, KD * T), BF16)
                s_cw = ES3()
                sbcw = lambda n2, sh, dt=F32: s_cw.enter_context(nc.sbuf_tensor("cw_" + n2, list(sh), dt))
                CARR = sbcw("CARR", (128, 512))
                DELQ = sbcw("DELQ", (128, 4 * 512))
                DTXQ = sbcw("DTXQ", (128, 4 * 512))
                BRB0 = sbcw("BRB0", (1, 2 * 512)); CRB0 = sbcw("CRB0", (1, 2 * 512))
                AH = sbcw("AH", (128, 512)); UH = sbcw("UH", (128, 512)); HH = sbcw("HH", (128, 512))
                YTB = sbcw("YTB", (128, 512), BF16)
                BBCS = sbcw("BBCS", (128, 2 * 512))
                k.op("vector", lambda e: e.memset(CARR[:], 0.0), writes=[B("carr")])
                for kh in range(2):
                    for tq in range(4):
                        tsl = slice(tq * 512, (tq + 1) * 512)
                        for ki in range(4):
                            kg = kh * 4 + ki
                            k.dma("sync", DELQ[:, ki * 512:(ki + 1) * 512],
                                  delta_h[:, kg * T:(kg + 1) * T][:, tsl], writes=[B("delq")])
                            k.dma("sync", DTXQ[:, ki * 512:(ki + 1) * 512],
                                  dtx_h[:, kg * T:(kg + 1) * T][:, tsl], writes=[B("dtxq")])
                        for n in range(64):
                            pp = n % 2
                            bkB = 4 + 2 * pp
                            bkC = 5 + 2 * pp
                            k.dma("scalar", BRB0[0:1, pp * 512:(pp + 1) * 512], BRB[n:n + 1, tsl],
                                  reads=[B("brb")], writes=[B("brb0%d" % pp)])
                            k.dma("scalar", CRB0[0:1, pp * 512:(pp + 1) * 512], CRB[n:n + 1, tsl],
                                  reads=[B("crb")], writes=[B("crb0%d" % pp)])
                            k.op("tensor", lambda e, pp=pp, bkB=bkB: e.matmul(
                                PS[bkB][:, 0:512], ONES1[0:1, :], BRB0[0:1, pp * 512:(pp + 1) * 512], start=True, stop=True),
                                reads=[B("ones1"), B("brb0%d" % pp)], writes=[PSB[bkB]])
                            k.op("scalar", lambda e, pp=pp, bkB=bkB: e.copy(
                                out=BBCS[:, pp * 512:(pp + 1) * 512], in_=PS[bkB][:, 0:512]),
                                reads=[PSB[bkB]], writes=[B("bbcs%d" % pp)])
                            free_psum(bkB, B("bbcs%d" % pp))
                            k.op("tensor", lambda e, pp=pp, bkC=bkC: e.matmul(
                                PS[bkC][:, 0:512], ONES1[0:1, :], CRB0[0:1, pp * 512:(pp + 1) * 512], start=True, stop=True),
                                reads=[B("ones1"), B("crb0%d" % pp)], writes=[PSB[bkC]])
                            for ki in range(4):
                                kg = kh * 4 + ki
                                ccol = kg * 64 + n
                                k.op("scalar", lambda e, ki=ki, kg=kg, n=n: e.activation(
                                    out=AH[:], in_=DELQ[:, ki * 512:(ki + 1) * 512], func=AF.Exp,
                                    scale=ACOL[:, kg * 64 + n: kg * 64 + n + 1]),
                                    reads=[B("delq"), B("acol")], writes=[B("ah")])
                                k.op("gpsimd", lambda e, ki=ki, pp=pp: e.tensor_tensor(
                                    out=UH[:], in0=DTXQ[:, ki * 512:(ki + 1) * 512],
                                    in1=BBCS[:, pp * 512:(pp + 1) * 512], op=ALU.mult),
                                    reads=[B("dtxq"), B("bbcs%d" % pp)], writes=[B("uh")])
                                k.op("vector", lambda e, ccol=ccol: e.tensor_tensor_scan(
                                    HH[:], AH[:], UH[:], CARR[:, ccol:ccol + 1], ALU.mult, ALU.add),
                                    reads=[B("ah"), B("uh"), B("carr")], writes=[B("hh")])
                                k.op("vector", lambda e, ccol=ccol: e.tensor_copy(
                                    out=CARR[:, ccol:ccol + 1], in_=HH[:, 511:512]),
                                    reads=[B("hh")], writes=[B("carr")])
                                k.op("vector", lambda e, bkC=bkC: e.tensor_tensor(
                                    out=YTB[:], in0=HH[:], in1=PS[bkC][:, 0:512], op=ALU.mult),
                                    reads=[B("hh"), PSB[bkC]], writes=[B("ytb")])
                                k.op("tensor", lambda e, ki=ki, n=n: e.matmul(
                                    PS[ki][:, 0:512], IDB[:], YTB[:], start=(n == 0), stop=(n == 63)),
                                    reads=[B("idb"), B("ytb")], psum_rw=PSB[ki])
                        for ki in range(4):
                            kg = kh * 4 + ki
                            k.op("scalar", lambda e, kg=kg, tsl=tsl, ki=ki: e.copy(
                                out=YSB[:, kg * T:(kg + 1) * T][:, tsl], in_=PS[ki][:, 0:512]),
                                reads=[PSB[ki]], writes=[B("ysb%d" % kg)])
                            free_psum(ki, B("ysb%d" % kg))
                s_cw.close()
                if os.environ.get("KDBG4"):
                    oys = nc.dram_tensor("dbg_ys", [128, KD * T], BF16, kind="ExternalOutput").ap()
                    k.dma("sync", oys, YSB[:], reads=[B("ysb%d" % g2) for g2 in range(KD)], writes=[B("dbg")])
                k.barrier()
                with ES3() as s_d:
                    sbd = lambda n2, sh, dt=F32: s_d.enter_context(nc.sbuf_tensor("d_" + n2, list(sh), dt))
                    SB_ = sbd("SB", (128, KD * T), BF16)
                    WSS = sbd("WSS", (128, KD * D), BF16)
                    DSQ = sbd("DSQ", (128, T))
                    XOFD = sbd("XOFD", (128, T))
                    k.dma("gpsimd", WSS[:], h_in["sl3_wssm"], writes=[B("wss")])
                    for kd in range(KD):
                        k.dma("sync", XOFD[:], xof_h[:, kd * T:(kd + 1) * T],
                              reads=[B("xofsp")], writes=[B("xofd")])
                        k.op("vector", lambda e, kd=kd: e.tensor_scalar(
                            out=SQ3[:], in0=XOFD[:], scalar1=dsc(kd), scalar2=None, op0=ALU.mult),
                            reads=[B("xofd"), B("cols3")], writes=[B("sq3c")])
                        k.op("vector", lambda e, kd=kd: e.tensor_copy(
                            out=DSQ[:], in_=YSB[:, kd * T:(kd + 1) * T]),
                            reads=[B("ysb%d" % kd)], writes=[B("dsq")])
                        k.op("vector", lambda e, kd=kd: e.tensor_tensor(
                            out=SB_[:, kd * T:(kd + 1) * T], in0=SQ3[:], in1=DSQ[:], op=ALU.add),
                            reads=[B("sq3c"), B("dsq")], writes=[B("sb%d" % kd)])
                        k.op("vector", lambda e, kd=kd: e.tensor_tensor(
                            out=X[:, kd * T:(kd + 1) * T], in0=X[:, kd * T:(kd + 1) * T], in1=XOFD[:], op=ALU.add),
                            reads=[B("xofd"), B("X%d" % kd)], writes=[B("X%d" % kd)])
                    if os.environ.get("KDBG4"):
                        osb = nc.dram_tensor("dbg_sb", [128, KD * T], BF16, kind="ExternalOutput").ap()
                        k.dma("sync", osb, SB_[:], reads=[B("sb%d" % g2) for g2 in range(KD)], writes=[B("dbg")])
                    for od in range(KD):
                        for kd in range(KD):
                            for c in range(4):
                                lw = WSS[:, kd * D + od * 128: kd * D + (od + 1) * 128]
                                rhs = SB_[:, kd * T + c * 512: kd * T + (c + 1) * 512]
                                k.op("tensor", lambda e, lw=lw, rhs=rhs, kd=kd, c=c: e.matmul(
                                    PS[c][:, 0:512], lw, rhs, start=(kd == 0), stop=(kd == KD - 1)),
                                    reads=[B("wss"), B("sb%d" % kd)], psum_rw=PSB[c])
                        for c in range(4):
                            k.op("scalar", lambda e, c=c: e.copy(out=SQ3[:, 0:512], in_=PS[c][:, 0:512]),
                                 reads=[PSB[c]], writes=[B("sq3c")])
                            free_psum(c, B("sq3c"))
                            xap = X[:, od * T + c * 512: od * T + (c + 1) * 512]
                            k.op("vector", lambda e, xap=xap: e.scalar_tensor_tensor(
                                out=xap, in0=SQ3[:, 0:512], scalar=0.45, in1=xap, op0=ALU.mult, op1=ALU.add),
                                reads=[B("sq3c"), B("X%d" % od)], writes=[B("X%d" % od)])


        qk_h = nc.dram_tensor("sl5_qk_h", [128, 4 * KD * T], BF16,
                              kind="ExternalOutput" if os.environ.get("KDBG5") else "Internal").ap()
        qkf_h = nc.dram_tensor("sl5_qkf_h", [128, 16 * T], F32, kind="Internal").ap()

        def sublayer5(sb, ps):
            from contextlib import ExitStack as ES5
            VN5 = sb("VN5", (128, 16 * D), BF16)
            with ES5() as s_bld:
                sbb = lambda n2, sh, dt=F32: s_bld.enter_context(nc.sbuf_tensor("s5b_" + n2, list(sh), dt))
                RB = sbb("RB", (128, T)); MB = sbb("MB", (128, T))
                SQ = sbb("SQ", (128, 1024)); STATR = sbb("STATR", (1, 1536))
                XNB5 = sbb("XNB5", (128, KD * T), BF16)
                QR = sbb("QR", (128, T), BF16)
                s_qk = ES5()
                sbq = lambda n2, sh, dt=F32: s_qk.enter_context(nc.sbuf_tensor("s5q_" + n2, list(sh), dt))
                C32 = sbq("C32", (128, T)); S32 = sbq("S32", (128, T))
                WQK = sbq("WQK", (128, KD * 128), BF16)
                T1 = sbq("T1", (128, T)); T2 = sbq("T2", (128, T))
                ln_stats(RB, MB, SQ, STATR, CONSTC[:, 0:1])
                k.barrier()
                for kd in range(KD):
                    for c in range(4):
                        make_xnb(RB, MB, SQ, XNB5, "X5%d" % kd, kd, slice(c * 512, (c + 1) * 512),
                                 slice(kd * T + c * 512, kd * T + (c + 1) * 512))
                TH = sbq("TH", (128, 2))
                k.dma("sync", TH[:, 0:1], h_in["sl5_thhi"], writes=[B("th")])
                k.dma("sync", TH[:, 1:2], h_in["sl5_thlo"], writes=[B("th")])
                for dst, dstb, shn, sln in ((S32, "c_s32", "sl5_shi", "sl5_slo"),
                                            (C32, "c_c32", "sl5_shic", "sl5_sloc")):
                    k.dma("sync", T1[:], h_in["sl5_m128"], writes=[B("t1")])
                    k.dma("sync", T2[:], h_in[shn], writes=[B("t2")])
                    k.op("vector", lambda e, dst=dst: e.tensor_scalar(
                        out=dst[:], in0=T1[:], scalar1=TH[:, 0:1], scalar2=None, op0=ALU.mult),
                        reads=[B("t1"), B("th")], writes=[B(dstb)])
                    k.op("vector", lambda e, dst=dst: e.tensor_tensor(out=dst[:], in0=dst[:], in1=T2[:], op=ALU.add),
                         reads=[B(dstb), B("t2")], writes=[B(dstb)])
                    k.op("vector", lambda e: e.tensor_scalar(
                        out=T2[:], in0=T1[:], scalar1=TH[:, 1:2], scalar2=None, op0=ALU.mult),
                        reads=[B("t1"), B("th")], writes=[B("t2")])
                    k.op("vector", lambda e, dst=dst: e.tensor_tensor(out=dst[:], in0=dst[:], in1=T2[:], op=ALU.add),
                         reads=[B(dstb), B("t2")], writes=[B(dstb)])
                    k.dma("sync", T2[:], h_in[sln], writes=[B("t2")])
                    k.op("vector", lambda e, dst=dst: e.tensor_tensor(out=dst[:], in0=dst[:], in1=T2[:], op=ALU.add),
                         reads=[B(dstb), B("t2")], writes=[B(dstb)])
                    k.op("vector", lambda e, dst=dst: e.tensor_scalar(
                        out=dst[:], in0=dst[:], scalar1=3.14159265, scalar2=-3.14159265,
                        op0=ALU.min, op1=ALU.max),
                        reads=[B(dstb)], writes=[B(dstb)])
                    k.op("scalar", lambda e, dst=dst: e.activation(out=dst[:], in_=dst[:], func=AF.Sin),
                         reads=[B(dstb)], writes=[B(dstb)])
                for ot in range(16):
                    for kd in range(KD):
                        wsl = slice(kd * 3072 + ot * 128, kd * 3072 + (ot + 1) * 128)
                        k.dma("gpsimd", WQK[:, kd * 128:(kd + 1) * 128], h_in["sl5_wqkvT"][:, wsl],
                              writes=[B("wqk")])
                    for c in range(4):
                        for kd in range(KD):
                            lw = WQK[:, kd * 128:(kd + 1) * 128]
                            rhs = XNB5[:, kd * T + c * 512: kd * T + (c + 1) * 512]
                            k.op("tensor", lambda e, lw=lw, rhs=rhs, kd=kd, c=c: e.matmul(
                                PS[c][:, 0:512], lw, rhs, start=(kd == 0), stop=(kd == KD - 1)),
                                reads=[B("wqk"), B("X5%d" % kd)], psum_rw=PSB[c])
                    for c in range(4):
                        k.op("scalar", lambda e, c=c: e.copy(out=T1[:, c * 512:(c + 1) * 512], in_=PS[c][:, 0:512]),
                             reads=[PSB[c]], writes=[B("t1")])
                        free_psum(c, B("t1"))
                    k.op("vector", lambda e: e.tensor_copy(out=QR[:], in_=T1[:]), reads=[B("t1")], writes=[B("qr")])
                    blk = 2 + (ot // 8)
                    qi = ot % 8
                    k.dma("sync", qk_h[:, (blk * KD + qi) * T:(blk * KD + qi + 1) * T], QR[:],
                          reads=[B("qr")], writes=[B("qkspill")])
                    k.dma("sync", qkf_h[:, ot * T:(ot + 1) * T], T1[:], reads=[B("t1")], writes=[B("qkfsp")])
                EVF = T1; ODF = T2
                RT = sbq("RT", (128, T))
                for isk in range(2):
                    for i in range(4):
                        k.dma("sync", EVF[:], qkf_h[:, (isk * 8 + i) * T:(isk * 8 + i + 1) * T], writes=[B("t1")])
                        k.dma("sync", ODF[:], qkf_h[:, (isk * 8 + i + 4) * T:(isk * 8 + i + 5) * T], writes=[B("t2")])
                        k.op("vector", lambda e: e.tensor_tensor(out=RT[:], in0=EVF[:], in1=C32[:], op=ALU.mult),
                             reads=[B("t1"), B("c_c32")], writes=[B("rt")])
                        for hf in range(2):
                            hsl = slice(hf * 1024, (hf + 1) * 1024)
                            k.op("vector", lambda e, hsl=hsl: e.tensor_tensor(
                                out=SQ[:, 0:1024], in0=ODF[:, hsl], in1=S32[:, hsl], op=ALU.mult),
                                reads=[B("t2"), B("c_s32")], writes=[B("sqx")])
                            k.op("vector", lambda e, hsl=hsl: e.tensor_tensor(
                                out=RT[:, hsl], in0=RT[:, hsl], in1=SQ[:, 0:1024], op=ALU.add),
                                reads=[B("rt"), B("sqx")], writes=[B("rt")])
                        k.op("vector", lambda e: e.tensor_copy(out=QR[:], in_=RT[:]), reads=[B("rt")], writes=[B("qr")])
                        k.dma("sync", qk_h[:, (isk * KD + i) * T:(isk * KD + i + 1) * T], QR[:],
                              reads=[B("qr")], writes=[B("qkspill")])
                        k.op("vector", lambda e: e.tensor_tensor(out=RT[:], in0=ODF[:], in1=C32[:], op=ALU.mult),
                             reads=[B("t2"), B("c_c32")], writes=[B("rt")])
                        for hf in range(2):
                            hsl = slice(hf * 1024, (hf + 1) * 1024)
                            k.op("vector", lambda e, hsl=hsl: e.tensor_tensor(
                                out=SQ[:, 0:1024], in0=EVF[:, hsl], in1=S32[:, hsl], op=ALU.mult),
                                reads=[B("t1"), B("c_s32")], writes=[B("sqx")])
                            k.op("vector", lambda e, hsl=hsl: e.tensor_tensor(
                                out=RT[:, hsl], in0=RT[:, hsl], in1=SQ[:, 0:1024], op=ALU.subtract),
                                reads=[B("rt"), B("sqx")], writes=[B("rt")])
                        k.op("vector", lambda e: e.tensor_copy(out=QR[:], in_=RT[:]), reads=[B("rt")], writes=[B("qr")])
                        k.dma("sync", qk_h[:, (isk * KD + i + 4) * T:(isk * KD + i + 5) * T], QR[:],
                              reads=[B("qr")], writes=[B("qkspill")])
                s_qk.close()
                k.barrier()
                WV58 = sbb("WV58", (128, KD * 1024), BF16)
                for kd in range(KD):
                    k.dma("gpsimd", WV58[:, kd * 1024:(kd + 1) * 1024],
                          h_in["sl5_wqkvT"][:, kd * 3072 + 2048: kd * 3072 + 3072], writes=[B("wv5")])
                for ti in range(16):
                    for kd in range(KD):
                        lw = XNB5[:, kd * T + ti * 128: kd * T + (ti + 1) * 128]
                        for dc in range(2):
                            k.op("tensor", lambda e, lw=lw, dc=dc, kd=kd, ti=ti: e.matmul(
                                PS[4 + (ti % 2) * 2 + dc][:, 0:512], lw,
                                WV58[:, kd * 1024 + dc * 512: kd * 1024 + (dc + 1) * 512],
                                start=(kd == 0), stop=(kd == KD - 1)),
                                reads=[B("X5%d" % kd), B("wv5")], psum_rw=PSB[4 + (ti % 2) * 2 + dc])
                    for dc in range(2):
                        k.op("scalar", lambda e, ti=ti, dc=dc: e.copy(
                            out=VN5[:, ti * D + dc * 512: ti * D + (dc + 1) * 512],
                            in_=PS[4 + (ti % 2) * 2 + dc][:, 0:512]),
                            reads=[PSB[4 + (ti % 2) * 2 + dc]], writes=[B("vn5")])
                        free_psum(4 + (ti % 2) * 2 + dc, B("vn5"))
            k.barrier()
            with ES5() as s_att:
                sba = lambda n2, sh, dt=F32: s_att.enter_context(nc.sbuf_tensor("s5a_" + n2, list(sh), dt))
                AOT5 = sba("AOT5", (128, KD * T), BF16)
                QH = sba("QH", (128, 2 * T), BF16)
                KH = sba("KH", (128, 2 * T), BF16)
                SC5 = sba("SC5", (128, T))
                EXD = sba("EXD", (128, 512))
                ATT5 = sba("ATT5", (128, T), BF16)
                TRI = sba("TRI", (128, 128))
                AVS5 = sba("AVS5", (64, 128), BF16)
                DC = sba("DC", (128, 8))
                SQ5 = sba("SQ5", (128, 512))
                IDF5 = sba("IDF5", (128, 128))
                k.dma("sync", TRI[:], h_in["sl5_tri"], writes=[B("tri")])
                k.dma("sync", IDF5[:], h_in["eyerep"][:, 0:128], writes=[B("idf5")])
                for h in range(16):
                    hb = h % 2
                    qsl = QH[:, hb * T:(hb + 1) * T]
                    ksl = KH[:, hb * T:(hb + 1) * T]
                    for dst, dstb, blk0 in ((qsl, "qh%d" % hb, 0), (ksl, "kh%d" % hb, 1)):
                        for part, src_tile in ((0, blk0 * KD + h // 4), (32, blk0 * KD + 4 + h // 4),
                                               (64, (2 + blk0) * KD + h // 4), (96, (2 + blk0) * KD + 4 + h // 4)):
                            k.dma("scalar", dst[part:part + 32, :],
                                  qk_h[(h % 4) * 32:(h % 4) * 32 + 32, src_tile * T:(src_tile + 1) * T],
                                  writes=[B(dstb)])
                    for qt in range(16):
                        nch = (qt * 128 + 128 + 511) // 512
                        for ch in range(nch):
                            cl = min(512, qt * 128 + 128 - ch * 512)
                            k.op("tensor", lambda e, qsl=qsl, ksl=ksl, qt=qt, ch=ch, cl=cl: e.matmul(
                                PS[ch][:, 0:cl], qsl[0:64, qt * 128:(qt + 1) * 128],
                                ksl[0:64, ch * 512: ch * 512 + cl], start=True, stop=True),
                                reads=[B("qh%d" % hb), B("kh%d" % hb)], writes=[PSB[ch]])
                        for c in range(4):
                            k.op("tensor", lambda e, qsl=qsl, ksl=ksl, qt=qt, c=c: e.matmul(
                                PS[4 + c][:, 0:512], qsl[64:128, qt * 128:(qt + 1) * 128],
                                ksl[64:128, c * 512:(c + 1) * 512], start=True, stop=True),
                                reads=[B("qh%d" % hb), B("kh%d" % hb)], writes=[PSB[4 + c]])
                        for c in range(4):
                            k.op("scalar", lambda e, c=c: e.activation(
                                out=EXD[:], in_=PS[4 + c][:, 0:512], func=AF.Exp, scale=0.125,
                                accum_out=DC[:, c:c + 1]),
                                reads=[PSB[4 + c]], writes=[B("exd"), B("dc")])
                            free_psum(4 + c, B("dc"))
                        k.op("vector", lambda e: e.tensor_tensor(out=DC[:, 4:5], in0=DC[:, 0:1], in1=DC[:, 1:2], op=ALU.add),
                             sreads=[B("dc")], writes=[B("dc")])
                        k.op("vector", lambda e: e.tensor_tensor(out=DC[:, 5:6], in0=DC[:, 2:3], in1=DC[:, 3:4], op=ALU.add),
                             sreads=[B("dc")], writes=[B("dc")])
                        k.op("vector", lambda e: e.tensor_tensor(out=DC[:, 4:5], in0=DC[:, 4:5], in1=DC[:, 5:6], op=ALU.add),
                             sreads=[B("dc")], writes=[B("dc")])
                        k.op("scalar", lambda e: e.activation(out=DC[:, 6:7], in_=DC[:, 4:5], func=AF.Ln),
                             sreads=[B("dc")], writes=[B("dc")])
                        k.op("vector", lambda e: e.tensor_scalar(out=DC[:, 6:7], in0=DC[:, 6:7], scalar1=-1.0,
                                                                 scalar2=None, op0=ALU.mult),
                             sreads=[B("dc")], writes=[B("dc")])
                        for ch in range(nch):
                            cl = min(512, qt * 128 + 128 - ch * 512)
                            k.op("scalar", lambda e, ch=ch, cl=cl: e.activation(
                                out=SC5[:, ch * 512: ch * 512 + cl], in_=PS[ch][:, 0:cl], func=AF.Exp,
                                scale=0.125, bias=DC[:, 6:7]),
                                reads=[PSB[ch], B("dc")], writes=[B("sc5")])
                            free_psum(ch, B("sc5"))
                        k.op("vector", lambda e, qt=qt: e.tensor_tensor(
                            out=SC5[:, qt * 128:(qt + 1) * 128], in0=SC5[:, qt * 128:(qt + 1) * 128],
                            in1=TRI[:], op=ALU.mult),
                            reads=[B("sc5"), B("tri")], writes=[B("sc5")])
                        for tk in range(qt + 1):
                            k.op("tensor", lambda e, tk=tk: e.transpose(
                                PS[4][:, 0:128], SC5[:, tk * 128:(tk + 1) * 128], IDF5[:]),
                                reads=[B("sc5"), B("idf5")], writes=[PSB[4]])
                            k.op("vector", lambda e, tk=tk: e.tensor_copy(
                                out=ATT5[:, tk * 128:(tk + 1) * 128], in_=PS[4][:, 0:128]),
                                reads=[PSB[4]], writes=[B("att5")])
                            free_psum(4, B("att5"))
                        for tk in range(qt + 1):
                            lw = VN5[:, tk * D + h * 64: tk * D + (h + 1) * 64]
                            k.op("tensor", lambda e, lw=lw, tk=tk, qt=qt: e.matmul(
                                PS[5][0:64, 0:128], lw, ATT5[:, tk * 128:(tk + 1) * 128],
                                start=(tk == 0), stop=(tk == qt)),
                                reads=[B("vn5"), B("att5")], psum_rw=PSB[5])
                        k.op("scalar", lambda e: e.copy(out=AVS5[0:64, 0:128], in_=PS[5][0:64, 0:128]),
                             reads=[PSB[5]], writes=[B("avs5")])
                        free_psum(5, B("avs5"))
                        k.dma("scalar", AOT5[(h % 2) * 64:(h % 2) * 64 + 64,
                                             (h // 2) * T + qt * 128:(h // 2) * T + (qt + 1) * 128],
                              AVS5[0:64, 0:128], reads=[B("avs5")], writes=[B("aot5_%d" % (h // 2))])
                if os.environ.get("KDBG5"):
                    oao = nc.dram_tensor("dbg_ao", [128, KD * T], BF16, kind="ExternalOutput").ap()
                    k.dma("sync", oao, AOT5[:], reads=[B("aot5_%d" % g2) for g2 in range(KD)], writes=[B("dbg")])
                WOS5 = sba("WOS5", (128, KD * 128), BF16)
                BO5 = sba("BO5", (128, KD))
                k.dma("sync", BO5[:], h_in["sl5_bout"], writes=[B("bo5")])
                for od in range(KD):
                    for ko in range(KD):
                        k.dma("gpsimd", WOS5[:, ko * 128:(ko + 1) * 128],
                              h_in["sl5_woutT"][:, ko * D + od * 128: ko * D + (od + 1) * 128],
                              writes=[B("wos5")])
                    for ko in range(KD):
                        for c in range(4):
                            lw = WOS5[:, ko * 128:(ko + 1) * 128]
                            rhs = AOT5[:, ko * T + c * 512: ko * T + (c + 1) * 512]
                            k.op("tensor", lambda e, lw=lw, rhs=rhs, ko=ko, c=c: e.matmul(
                                PS[c][:, 0:512], lw, rhs, start=(ko == 0), stop=(ko == KD - 1)),
                                reads=[B("wos5"), B("aot5_%d" % ko)], psum_rw=PSB[c])
                    for c in range(4):
                        k.op("scalar", lambda e, c=c, od=od: e.activation(
                            out=SQ5[:, 0:512], in_=PS[c][:, 0:512], func=AF.Identity, bias=BO5[:, od:od + 1]),
                            reads=[PSB[c], B("bo5")], writes=[B("sq5")])
                        free_psum(c, B("sq5"))
                        xap = X[:, od * T + c * 512: od * T + (c + 1) * 512]
                        k.op("vector", lambda e, xap=xap, od=od: e.tensor_tensor(
                            out=xap, in0=SQ5[:, 0:512], in1=xap, op=ALU.add),
                            reads=[B("sq5"), B("X%d" % od)], writes=[B("X%d" % od)])


        moe_call = [0]
        def sublayer_moe(sb, ps):
            from contextlib import ExitStack as ES6
            moe_call[0] += 1
            mpre = "m%d" % moe_call[0]
            TNB = sb("TNB", (128, KD * T), BF16)
            CT6 = sb("CT6", (8, T))
            SQ6 = sb("SQ6", (128, 512))
            NWC = sb("NWC", (128, 2 * KD))
            k.dma("sync", NWC[:, 0:KD], h_in["sl6_nw"], writes=[B("nwc")])
            with ES6() as s_n:
                sbn = lambda n2, sh, dt=F32: s_n.enter_context(nc.sbuf_tensor(mpre + "s6n_" + n2, list(sh), dt))
                RB = sbn("RB", (128, T)); MB = sbn("MB", (128, T))
                SQ = sbn("SQ", (128, 1024)); STATR = sbn("STATR", (1, 2048))
                RR = sbn("RR", (128, T))
                XOF6 = sbn("XOF6", (128, T))
                GWT = sbn("GWT", (128, KD * 8), BF16)
                CMB = sbn("CMB", (128, 16 * 8))
                MC6 = sbn("MC6", (128, 8))
                IDF6 = sbn("IDF6", (128, 128))
                k.dma("gpsimd", GWT[:], h_in["sl6_gwT"], writes=[B("gwt")])
                k.dma("sync", IDF6[:], h_in["eyerep"][:, 0:128], writes=[B("idf6")])
                ln_stats(RB, MB, SQ, STATR, CONSTC[:, 0:1])
                k.barrier()
                xof_chunk = xof_chunk_gen(RB, MB, "RB", "MB")
                mu2 = STATR[0:1, 1536:2048]
                for c in range(4):
                    csl = slice(c * 512, (c + 1) * 512)
                    for kd in range(KD):
                        xb = xof_chunk(XOF6[:, 0:512], kd, csl)
                        k.op("scalar", lambda e: e.activation(out=SQ[:, 0:512], in_=XOF6[:, 0:512], func=AF.Square),
                             reads=[xb], writes=[B("sq6s")])
                        k.op("tensor", lambda e, kd=kd: e.matmul(
                            PS[0][0:1, 0:512], ONESC[:, 0:1], SQ[:, 0:512], start=(kd == 0), stop=(kd == KD - 1)),
                            reads=[B("onesc"), B("sq6s")], psum_rw=PSB[0])
                    k.op("scalar", lambda e: e.mul(mu2, PS[0][0:1, 0:512], 1.0 / D),
                         reads=[PSB[0]], writes=[B("mu2")])
                    free_psum(0, B("mu2"))
                    k.op("scalar", lambda e: e.activation(out=mu2, in_=mu2, func=AF.Sqrt, bias=CONSTC[0:1, 2:3]),
                         reads=[B("mu2"), B("constc")], writes=[B("mu2")])
                    k.op("vector", lambda e: e.reciprocal(out=mu2, in_=mu2), reads=[B("mu2")], writes=[B("mu2")])
                    k.op("tensor", lambda e: e.matmul(PS[0][:, 0:512], ONES1[0:1, :], mu2, start=True, stop=True),
                         reads=[B("ones1"), B("mu2")], writes=[PSB[0]])
                    k.op("vector", lambda e, csl=csl: e.tensor_copy(out=RR[:, csl], in_=PS[0][:, 0:512]),
                         reads=[PSB[0]], writes=[B("rr")])
                    free_psum(0, B("rr"))
                if os.environ.get("KDBG6") and moe_call[0] == 1:
                    orr = nc.dram_tensor("dbg_rr", [128, T], F32, kind="ExternalOutput").ap()
                    k.dma("sync", orr, RR[:], reads=[B("rr")], writes=[B("dbg")])
                    orb = nc.dram_tensor("dbg_rb", [128, T], F32, kind="ExternalOutput").ap()
                    k.dma("sync", orb, RB[:], reads=[B("RB")], writes=[B("dbg")])
                    omb = nc.dram_tensor("dbg_mb", [128, T], F32, kind="ExternalOutput").ap()
                    k.dma("sync", omb, MB[:], reads=[B("MB")], writes=[B("dbg")])
                for kd in range(KD):
                    for c in range(4):
                        csl = slice(c * 512, (c + 1) * 512)
                        xb = xof_chunk(XOF6[:, 0:512], kd, csl)
                        k.op("vector", lambda e, csl=csl: e.tensor_tensor(
                            out=SQ[:, 0:512], in0=XOF6[:, 0:512], in1=RR[:, csl], op=ALU.mult),
                            reads=[xb, B("rr")], writes=[B("sq6s")])
                        k.op("scalar", lambda e, kd=kd, csl=csl: e.activation(
                            out=TNB[:, kd * T:(kd + 1) * T][:, csl], in_=SQ[:, 0:512], func=AF.Identity,
                            scale=NWC[:, kd:kd + 1]),
                            reads=[B("sq6s"), B("nwc")], writes=[B("tnb%d" % kd)])
                        xap = X[:, kd * T:(kd + 1) * T][:, csl]
                        k.op("vector", lambda e, xap=xap: e.tensor_tensor(
                            out=xap, in0=xap, in1=XOF6[:, 0:512], op=ALU.add),
                            reads=[xb, B("X%d" % kd)], writes=[B("X%d" % kd)])
                for ti in range(16):
                    for kd in range(KD):
                        lw = TNB[:, kd * T + ti * 128: kd * T + (ti + 1) * 128]
                        k.op("tensor", lambda e, lw=lw, kd=kd: e.matmul(
                            PS[1][:, 0:8], lw, GWT[:, kd * 8:(kd + 1) * 8], start=(kd == 0), stop=(kd == KD - 1)),
                            reads=[B("tnb%d" % kd), B("gwt")], psum_rw=PSB[1])
                    gsl = CMB[:, ti * 8:(ti + 1) * 8]
                    k.op("vector", lambda e, gsl=gsl: e.tensor_copy(out=gsl, in_=PS[1][:, 0:8]),
                         reads=[PSB[1]], writes=[B("cmb")])
                    free_psum(1, B("cmb"))
                    if os.environ.get("KDBG6") and ti == 0:
                        og = nc.dram_tensor("dbg_g0", [128, 8], F32, kind="ExternalOutput").ap()
                        k.dma("sync", og, gsl, reads=[B("cmb")], writes=[B("dbg")])
                    k.op("vector", lambda e, gsl=gsl: e.tensor_reduce(
                        out=MC6[:, 0:1], in_=gsl, axis=mybir.AxisListType.X, op=ALU.max),
                        sreads=[B("cmb")], writes=[B("mc6")])
                    k.op("vector", lambda e, gsl=gsl: e.tensor_scalar(
                        out=SQ[:, 0:8], in0=gsl, scalar1=MC6[:, 0:1], scalar2=None, op0=ALU.is_ge),
                        sreads=[B("cmb"), B("mc6")], writes=[B("sq6s")])
                    k.op("vector", lambda e, gsl=gsl: e.scalar_tensor_tensor(
                        out=SQ[:, 8:16], in0=SQ[:, 0:8], scalar=-1e30, in1=gsl, op0=ALU.mult, op1=ALU.add),
                        sreads=[B("sq6s"), B("cmb")], writes=[B("sq6s")])
                    k.op("vector", lambda e: e.tensor_reduce(
                        out=MC6[:, 1:2], in_=SQ[:, 8:16], axis=mybir.AxisListType.X, op=ALU.max),
                        sreads=[B("sq6s")], writes=[B("mc6")])
                    k.op("vector", lambda e: e.tensor_scalar(
                        out=SQ[:, 16:24], in0=SQ[:, 8:16], scalar1=MC6[:, 1:2], scalar2=None, op0=ALU.is_ge),
                        sreads=[B("sq6s"), B("mc6")], writes=[B("sq6s")])
                    if os.environ.get("KDBG6") and ti == 0:
                        omc = nc.dram_tensor("dbg_mc0", [128, 8], F32, kind="ExternalOutput").ap()
                        k.dma("sync", omc, MC6[:], reads=[B("mc6")], writes=[B("dbg")])
                        osq = nc.dram_tensor("dbg_sq0", [128, 24], F32, kind="ExternalOutput").ap()
                        k.dma("sync", osq, SQ[:, 0:24], reads=[B("sq6s")], writes=[B("dbg")])
                    k.op("vector", lambda e: e.tensor_tensor(out=MC6[:, 2:3], in0=MC6[:, 0:1], in1=MC6[:, 1:2],
                                                             op=ALU.subtract),
                         sreads=[B("mc6")], writes=[B("mc6")])
                    k.op("scalar", lambda e: e.activation(out=MC6[:, 3:4], in_=MC6[:, 2:3], func=AF.Sigmoid),
                         sreads=[B("mc6")], writes=[B("mc6")])
                    k.op("vector", lambda e: e.tensor_scalar(out=MC6[:, 4:5], in0=MC6[:, 3:4], scalar1=-1.0,
                                                             scalar2=1.0, op0=ALU.mult, op1=ALU.add),
                         sreads=[B("mc6")], writes=[B("mc6")])
                    k.op("vector", lambda e: e.tensor_scalar(
                        out=SQ[:, 0:8], in0=SQ[:, 0:8], scalar1=MC6[:, 3:4], scalar2=None, op0=ALU.mult),
                        sreads=[B("sq6s"), B("mc6")], writes=[B("sq6s")])
                    k.op("vector", lambda e: e.tensor_scalar(
                        out=SQ[:, 16:24], in0=SQ[:, 16:24], scalar1=MC6[:, 4:5], scalar2=None, op0=ALU.mult),
                        sreads=[B("sq6s"), B("mc6")], writes=[B("sq6s")])
                    k.op("vector", lambda e, gsl=gsl: e.tensor_tensor(
                        out=gsl, in0=SQ[:, 0:8], in1=SQ[:, 16:24], op=ALU.add),
                        sreads=[B("sq6s")], writes=[B("cmb")])
                    k.op("tensor", lambda e, ti=ti: e.transpose(PS[2][0:8, 0:128], CMB[:, ti * 8:(ti + 1) * 8], IDF6[:]),
                         reads=[B("cmb"), B("idf6")], writes=[PSB[2]])
                    k.op("vector", lambda e, ti=ti: e.tensor_copy(out=CT6[:, ti * 128:(ti + 1) * 128], in_=PS[2][0:8, 0:128]),
                         reads=[PSB[2]], writes=[B("ct6")])
                    free_psum(2, B("ct6"))
            if os.environ.get("KDBG6") and moe_call[0] == 1:
                oct6 = nc.dram_tensor("dbg_ct6", [8, T], F32, kind="ExternalOutput").ap()
                k.dma("sync", oct6, CT6[:], reads=[B("ct6")], writes=[B("dbg")])
                otnb = nc.dram_tensor("dbg_tnb", [128, KD * T], BF16, kind="ExternalOutput").ap()
                k.dma("sync", otnb, TNB[:], reads=[B("tnb%d" % g2) for g2 in range(KD)], writes=[B("dbg")])
            k.barrier()
            with ES6() as s_e:
                sbe = lambda n2, sh, dt=F32: s_e.enter_context(nc.sbuf_tensor(mpre + "s6e_" + n2, list(sh), dt))
                W1B = sbe("W1B", (128, KD * 2048), BF16)
                W2B = sbe("W2B", (128, 16 * D), BF16)
                HEB = sbe("HEB", (128, 16 * 512), BF16)
                CEB = sbe("CEB", (128, T))
                CR6 = sbe("CR6", (1, T))
                B1C = sbe("B1C", (128, 16)); B2C = sbe("B2C", (128, KD))
                TQ6 = sbe("TQ6", (128, 512))
                for e_ in range(8):
                    k.dma("gpsimd", W1B[:], h_in["sl6_w1T"][e_], writes=[B("w1b")])
                    k.dma("gpsimd", W2B[:], h_in["sl6_w2T"][e_], writes=[B("w2b")])
                    k.dma("sync", B1C[:], h_in["sl6_b1"][e_], writes=[B("b1c")])
                    k.dma("sync", B2C[:], h_in["sl6_b2"][e_], writes=[B("b2c")])
                    k.dma("scalar", CR6[0:1, :], CT6[e_:e_ + 1, :], reads=[B("ct6")], writes=[B("cr6")])
                    for c in range(4):
                        k.op("tensor", lambda e, c=c: e.matmul(
                            PS[5][:, 0:512], ONES1[0:1, :], CR6[0:1, c * 512:(c + 1) * 512], start=True, stop=True),
                            reads=[B("ones1"), B("cr6")], writes=[PSB[5]])
                        k.op("vector", lambda e, c=c: e.tensor_copy(out=CEB[:, c * 512:(c + 1) * 512], in_=PS[5][:, 0:512]),
                             reads=[PSB[5]], writes=[B("ceb")])
                        free_psum(5, B("ceb"))
                    for tc in range(4):
                        csl = slice(tc * 512, (tc + 1) * 512)
                        for ht in range(16):
                            for kd in range(KD):
                                lw = W1B[:, kd * 2048 + ht * 128: kd * 2048 + (ht + 1) * 128]
                                rhs = TNB[:, kd * T:(kd + 1) * T][:, csl]
                                k.op("tensor", lambda e, lw=lw, rhs=rhs, kd=kd: e.matmul(
                                    PS[4][:, 0:512], lw, rhs, start=(kd == 0), stop=(kd == KD - 1)),
                                    reads=[B("w1b"), B("tnb%d" % kd)], psum_rw=PSB[4])
                            k.op("scalar", lambda e, ht=ht: e.activation(
                                out=HEB[:, ht * 512:(ht + 1) * 512], in_=PS[4][:, 0:512], func=AF.Gelu,
                                bias=B1C[:, ht:ht + 1]),
                                reads=[PSB[4], B("b1c")], writes=[B("heb%d" % ht)])
                            free_psum(4, B("heb%d" % ht))
                        for od in range(KD):
                            for ht in range(16):
                                lw = W2B[:, ht * D + od * 128: ht * D + (od + 1) * 128]
                                k.op("tensor", lambda e, lw=lw, ht=ht, od=od: e.matmul(
                                    PS[od % 4][:, 0:512], lw, HEB[:, ht * 512:(ht + 1) * 512],
                                    start=(ht == 0), stop=(ht == 15)),
                                    reads=[B("w2b"), B("heb%d" % ht)], psum_rw=PSB[od % 4])
                            k.op("scalar", lambda e, od=od: e.activation(
                                out=SQ6[:, 0:512], in_=PS[od % 4][:, 0:512], func=AF.Identity,
                                bias=B2C[:, od:od + 1]),
                                reads=[PSB[od % 4], B("b2c")], writes=[B("sq6")])
                            free_psum(od % 4, B("sq6"))
                            k.op("vector", lambda e, csl=csl: e.tensor_tensor(
                                out=TQ6[:], in0=SQ6[:, 0:512], in1=CEB[:, csl], op=ALU.mult),
                                reads=[B("sq6"), B("ceb")], writes=[B("tq6")])
                            xap = X[:, od * T:(od + 1) * T][:, csl]
                            k.op("vector", lambda e, xap=xap, od=od: e.tensor_tensor(
                                out=xap, in0=xap, in1=TQ6[:], op=ALU.add),
                                reads=[B("tq6"), B("X%d" % od)], writes=[B("X%d" % od)])


        aot7_h = nc.dram_tensor("sl7_aot_h", [128, KD * T], BF16, kind="Internal").ap()
        qk7q_h = nc.dram_tensor("sl7_q_h", [128, KD * T], BF16, kind="Internal").ap()
        qk7k_h = nc.dram_tensor("sl7_k_h", [128, KD * T], BF16, kind="Internal").ap()

        def sublayer7(sb, ps):
            from contextlib import ExitStack as ES7
            VN7 = sb("VN7", (128, 16 * D), BF16)
            S1H = sb("S1H", (32, T), BF16)
            S1P = sb("S1P", (32, T), BF16)
            LBT = sb("LBT", (32, 3072), BF16)
            BA7 = sb("BA7", (128, 24))
            k.dma("gpsimd", LBT[:], h_in["sl7_lbT"], writes=[B("lbt")])
            k.dma("sync", BA7[:], h_in["sl7_ba"], writes=[B("ba7")])
            with ES7() as s_bld:
                sbb = lambda n2, sh, dt=F32: s_bld.enter_context(nc.sbuf_tensor("s7b_" + n2, list(sh), dt))
                RB = sbb("RB", (128, T)); MB = sbb("MB", (128, T))
                SQ = sbb("SQ", (128, 1024)); STATR = sbb("STATR", (1, 1536))
                XNB7 = sbb("XNB7", (128, KD * T), BF16)
                QR7 = sbb("QR7", (128, T), BF16)
                WA7 = sbb("WA7", (128, KD * 128), BF16)
                LAT = sbb("LAT", (128, KD * 32), BF16)
                BAV = sbb("BAV", (128, D))
                k.dma("gpsimd", LAT[:], h_in["sl7_laT"], writes=[B("lat")])
                k.dma("sync", BAV[:], h_in["sl7_bav"], writes=[B("bav")])
                ln_stats(RB, MB, SQ, STATR, CONSTC[:, 0:1])
                k.barrier()
                for kd in range(KD):
                    for c in range(4):
                        make_xnb(RB, MB, SQ, XNB7, "X7%d" % kd, kd, slice(c * 512, (c + 1) * 512),
                                 slice(kd * T + c * 512, kd * T + (c + 1) * 512))
                for c in range(4):
                    for kd in range(KD):
                        lw = LAT[:, kd * 32:(kd + 1) * 32]
                        rhs = XNB7[:, kd * T + c * 512: kd * T + (c + 1) * 512]
                        k.op("tensor", lambda e, lw=lw, rhs=rhs, kd=kd: e.matmul(
                            PS[0][0:32, 0:512], lw, rhs, start=(kd == 0), stop=(kd == KD - 1)),
                            reads=[B("lat"), B("X7%d" % kd)], psum_rw=PSB[0])
                    k.op("scalar", lambda e, c=c: e.activation(
                        out=S1H[:, c * 512:(c + 1) * 512], in_=PS[0][0:32, 0:512], func=AF.Identity, scale=0.5),
                        reads=[PSB[0]], writes=[B("s1h")])
                    free_psum(0, B("s1h"))
                for ot in range(16):
                    for kd in range(KD):
                        k.dma("gpsimd", WA7[:, kd * 128:(kd + 1) * 128],
                              h_in["sl7_waT"][:, kd * 3072 + ot * 128: kd * 3072 + (ot + 1) * 128],
                              writes=[B("wa7")])
                    for c in range(4):
                        for kd in range(KD):
                            lw = WA7[:, kd * 128:(kd + 1) * 128]
                            rhs = XNB7[:, kd * T + c * 512: kd * T + (c + 1) * 512]
                            k.op("tensor", lambda e, lw=lw, rhs=rhs, kd=kd: e.matmul(
                                PS[c][:, 0:512], lw, rhs, start=(kd == 0), stop=False),
                                reads=[B("wa7"), B("X7%d" % kd)], psum_rw=PSB[c])
                        lwl = LBT[:, ot * 128:(ot + 1) * 128]
                        rhsl = S1H[:, c * 512:(c + 1) * 512]
                        k.op("tensor", lambda e, lwl=lwl, rhsl=rhsl: e.matmul(
                            PS[c][:, 0:512], lwl, rhsl, start=False, stop=True),
                            reads=[B("lbt"), B("s1h")], psum_rw=PSB[c])
                    dsth = qk7q_h if ot < 8 else qk7k_h
                    oti = ot % 8
                    for c in range(4):
                        k.op("scalar", lambda e, c=c, ot=ot: e.activation(
                            out=QR7[:, c * 512:(c + 1) * 512], in_=PS[c][:, 0:512],
                            func=AF.Identity, bias=BA7[:, ot:ot + 1]),
                            reads=[PSB[c], B("ba7")], writes=[B("qr7")])
                        free_psum(c, B("qr7"))
                    k.dma("sync", dsth[:, oti * T:(oti + 1) * T], QR7[:], reads=[B("qr7")], writes=[B("qk7sp")])
                WAV = sbb("WAV", (128, KD * 1024), BF16)
                for kd in range(KD):
                    k.dma("gpsimd", WAV[:, kd * 1024:(kd + 1) * 1024],
                          h_in["sl7_waT"][:, kd * 3072 + 2048: kd * 3072 + 3072], writes=[B("wav")])
                for ti in range(16):
                    for dc in range(2):
                        bank = 4 + (ti % 2) * 2 + dc
                        for kd in range(KD):
                            lw = XNB7[:, kd * T + ti * 128: kd * T + (ti + 1) * 128]
                            k.op("tensor", lambda e, lw=lw, kd=kd, dc=dc, bank=bank: e.matmul(
                                PS[bank][:, 0:512], lw,
                                WAV[:, kd * 1024 + dc * 512: kd * 1024 + (dc + 1) * 512],
                                start=(kd == 0), stop=False),
                                reads=[B("X7%d" % kd), B("wav")], psum_rw=PSB[bank])
                        lwl = S1H[:, ti * 128:(ti + 1) * 128]
                        k.op("tensor", lambda e, lwl=lwl, dc=dc, bank=bank: e.matmul(
                            PS[bank][:, 0:512], lwl, LBT[:, 2048 + dc * 512: 2048 + (dc + 1) * 512],
                            start=False, stop=True),
                            reads=[B("s1h"), B("lbt")], psum_rw=PSB[bank])
                    for dc in range(2):
                        bank = 4 + (ti % 2) * 2 + dc
                        k.op("vector", lambda e, dc=dc, bank=bank: e.tensor_tensor(
                            out=SQ[:, dc * 512:(dc + 1) * 512], in0=PS[bank][:, 0:512],
                            in1=BAV[:, dc * 512:(dc + 1) * 512], op=ALU.add),
                            reads=[PSB[bank], B("bav")], writes=[B("sq7v")])
                        free_psum(bank, B("sq7v"))
                        k.op("vector", lambda e, ti=ti, dc=dc: e.tensor_copy(
                            out=VN7[:, ti * D + dc * 512: ti * D + (dc + 1) * 512], in_=SQ[:, dc * 512:(dc + 1) * 512]),
                            reads=[B("sq7v")], writes=[B("vn7")])
            k.barrier()
            with ES7() as s_att:
                sba = lambda n2, sh, dt=F32: s_att.enter_context(nc.sbuf_tensor("s7a_" + n2, list(sh), dt))
                SC7 = sba("SC7", (128, T))
                EXD7 = sba("EXD7", (128, 512))
                ATT7 = sba("ATT7", (128, T), BF16)
                TRIA = sba("TRIA", (128, 128))
                IDF7 = sba("IDF7", (128, 128))
                AVS7 = sba("AVS7", (64, 128), BF16)
                DC7 = sba("DC7", (128, 8))
                k.dma("sync", TRIA[:], h_in["sl7_tria"], writes=[B("tria")])
                k.dma("sync", IDF7[:], h_in["eyerep"][:, 0:128], writes=[B("idf7")])
                QH7 = sba("QH7", (64, 2 * T), BF16)
                KH7 = sba("KH7", (64, 2 * T), BF16)
                for h in range(16):
                    hb = h % 2
                    qsl = QH7[0:64, hb * T:(hb + 1) * T]
                    ksl = KH7[0:64, hb * T:(hb + 1) * T]
                    qb_ = B("qh7%d" % hb)
                    kb_ = B("kh7%d" % hb)
                    k.dma("scalar", qsl, qk7q_h[(h % 2) * 64:(h % 2) * 64 + 64, (h // 2) * T:(h // 2 + 1) * T],
                          writes=[qb_])
                    k.dma("scalar", ksl, qk7k_h[(h % 2) * 64:(h % 2) * 64 + 64, (h // 2) * T:(h // 2 + 1) * T],
                          writes=[kb_])
                    for qt in range(16):
                        nch = (qt * 128 + 128 + 511) // 512
                        for ch in range(nch):
                            cl = min(512, qt * 128 + 128 - ch * 512)
                            k.op("tensor", lambda e, qsl=qsl, ksl=ksl, qt=qt, ch=ch, cl=cl: e.matmul(
                                PS[ch][:, 0:cl], qsl[:, qt * 128:(qt + 1) * 128],
                                ksl[:, ch * 512: ch * 512 + cl], start=True, stop=True),
                                reads=[qb_, kb_], writes=[PSB[ch]])
                        for ch in range(nch):
                            cl = min(512, qt * 128 + 128 - ch * 512)
                            k.op("scalar", lambda e, ch=ch, cl=cl: e.activation(
                                out=SC7[:, ch * 512: ch * 512 + cl], in_=PS[ch][:, 0:cl],
                                func=AF.Identity, scale=0.125),
                                reads=[PSB[ch]], writes=[B("sc7")])
                            free_psum(ch, B("sc7"))
                        k.op("vector", lambda e, qt=qt: e.tensor_tensor(
                            out=SC7[:, qt * 128:(qt + 1) * 128], in0=SC7[:, qt * 128:(qt + 1) * 128],
                            in1=TRIA[:], op=ALU.add),
                            reads=[B("sc7"), B("tria")], writes=[B("sc7")])
                        klen = qt * 128 + 128
                        k.op("vector", lambda e, klen=klen: e.tensor_reduce(
                            out=DC7[:, 0:1], in_=SC7[:, 0:klen], axis=mybir.AxisListType.X,
                            op=ALU.max, negate=True),
                            sreads=[B("sc7")], writes=[B("dc7")])
                        for ch in range(nch):
                            cl = min(512, klen - ch * 512)
                            k.op("scalar", lambda e, ch=ch, cl=cl: e.activation(
                                out=EXD7[:, 0:cl], in_=SC7[:, ch * 512: ch * 512 + cl],
                                func=AF.Exp, bias=DC7[:, 0:1], accum_out=DC7[:, 2 + ch:3 + ch]),
                                sreads=[B("sc7"), B("dc7")], writes=[B("exd7"), B("dc7")])
                        for ch in range(1, nch):
                            k.op("vector", lambda e, ch=ch: e.tensor_tensor(
                                out=DC7[:, 2:3], in0=DC7[:, 2:3], in1=DC7[:, 2 + ch:3 + ch], op=ALU.add),
                                sreads=[B("dc7")], writes=[B("dc7")])
                        k.op("scalar", lambda e: e.activation(out=DC7[:, 6:7], in_=DC7[:, 2:3], func=AF.Ln),
                             sreads=[B("dc7")], writes=[B("dc7")])
                        k.op("vector", lambda e: e.tensor_tensor(
                            out=DC7[:, 6:7], in0=DC7[:, 0:1], in1=DC7[:, 6:7], op=ALU.subtract),
                            sreads=[B("dc7")], writes=[B("dc7")])
                        k.op("scalar", lambda e, klen=klen: e.activation(
                            out=SC7[:, 0:klen], in_=SC7[:, 0:klen], func=AF.Exp, bias=DC7[:, 6:7]),
                            sreads=[B("dc7")], reads=[B("sc7")], writes=[B("sc7")])
                        for tk in range(qt + 1):
                            k.op("tensor", lambda e, tk=tk: e.transpose(
                                PS[4][:, 0:128], SC7[:, tk * 128:(tk + 1) * 128], IDF7[:]),
                                reads=[B("sc7"), B("idf7")], writes=[PSB[4]])
                            k.op("vector", lambda e, tk=tk: e.tensor_copy(
                                out=ATT7[:, tk * 128:(tk + 1) * 128], in_=PS[4][:, 0:128]),
                                reads=[PSB[4]], writes=[B("att7")])
                            free_psum(4, B("att7"))
                        for tk in range(qt + 1):
                            lw = VN7[:, tk * D + h * 64: tk * D + (h + 1) * 64]
                            k.op("tensor", lambda e, lw=lw, tk=tk, qt=qt: e.matmul(
                                PS[5][0:64, 0:128], lw, ATT7[:, tk * 128:(tk + 1) * 128],
                                start=(tk == 0), stop=(tk == qt)),
                                reads=[B("vn7"), B("att7")], psum_rw=PSB[5])
                        k.op("scalar", lambda e: e.copy(out=AVS7[0:64, 0:128], in_=PS[5][0:64, 0:128]),
                             reads=[PSB[5]], writes=[B("avs7")])
                        free_psum(5, B("avs7"))
                        k.dma("scalar", aot7_h[(h % 2) * 64:(h % 2) * 64 + 64,
                                               (h // 2) * T + qt * 128:(h // 2) * T + (qt + 1) * 128],
                              AVS7[0:64, 0:128], reads=[B("avs7")], writes=[B("aotsp")])
            k.barrier()
            with ES7() as s_prj:
                sbp = lambda n2, sh, dt=F32: s_prj.enter_context(nc.sbuf_tensor("s7p_" + n2, list(sh), dt))
                WPS = sbp("WPS", (128, KD * D), BF16)
                LAPT = sbp("LAPT", (128, KD * 32), BF16)
                LBPT = sbp("LBPT", (32, D), BF16)
                BP7 = sbp("BP7", (128, KD))
                AOC = sbp("AOC", (128, 2 * T), BF16)
                SQ7 = sbp("SQ7", (128, 512))
                k.dma("gpsimd", WPS[:], h_in["sl7_wpT"], writes=[B("wps")])
                k.dma("gpsimd", LAPT[:], h_in["sl7_lapT"], writes=[B("lapt")])
                k.dma("gpsimd", LBPT[:], h_in["sl7_lbpT"], writes=[B("lbpt")])
                k.dma("sync", BP7[:], h_in["sl7_bp"], writes=[B("bp7")])
                for c in range(4):
                    for ko in range(KD):
                        wb = ko % 2
                        k.dma("sync", AOC[:, wb * T + c * 512: wb * T + (c + 1) * 512],
                              aot7_h[:, ko * T + c * 512: ko * T + (c + 1) * 512], writes=[B("aoc%d" % wb)])
                        lw = LAPT[:, ko * 32:(ko + 1) * 32]
                        k.op("tensor", lambda e, lw=lw, ko=ko, wb=wb, c=c: e.matmul(
                            PS[0][0:32, 0:512], lw, AOC[:, wb * T + c * 512: wb * T + (c + 1) * 512],
                            start=(ko == 0), stop=(ko == KD - 1)),
                            reads=[B("lapt"), B("aoc%d" % wb)], psum_rw=PSB[0])
                    k.op("scalar", lambda e, c=c: e.activation(
                        out=S1P[:, c * 512:(c + 1) * 512], in_=PS[0][0:32, 0:512], func=AF.Identity, scale=0.5),
                        reads=[PSB[0]], writes=[B("s1p")])
                    free_psum(0, B("s1p"))
                for c in range(4):
                    for ko in range(KD):
                        wb = ko % 2
                        k.dma("sync", AOC[:, wb * T + c * 512: wb * T + (c + 1) * 512],
                              aot7_h[:, ko * T + c * 512: ko * T + (c + 1) * 512], writes=[B("aoc%d" % wb)])
                        for od in range(KD):
                            lw = WPS[:, ko * D + od * 128: ko * D + (od + 1) * 128]
                            k.op("tensor", lambda e, lw=lw, ko=ko, od=od, wb=wb, c=c: e.matmul(
                                PS[od][:, 0:512], lw, AOC[:, wb * T + c * 512: wb * T + (c + 1) * 512],
                                start=(ko == 0), stop=False),
                                reads=[B("wps"), B("aoc%d" % wb)], psum_rw=PSB[od])
                    for od in range(KD):
                        lwl = LBPT[:, od * 128:(od + 1) * 128]
                        k.op("tensor", lambda e, lwl=lwl, od=od, c=c: e.matmul(
                            PS[od][:, 0:512], lwl, S1P[:, c * 512:(c + 1) * 512], start=False, stop=True),
                            reads=[B("lbpt"), B("s1p")], psum_rw=PSB[od])
                        k.op("scalar", lambda e, od=od: e.activation(
                            out=SQ7[:, 0:512], in_=PS[od][:, 0:512], func=AF.Identity, bias=BP7[:, od:od + 1]),
                            reads=[PSB[od], B("bp7")], writes=[B("sq7")])
                        free_psum(od, B("sq7"))
                        xap = X[:, od * T + c * 512: od * T + (c + 1) * 512]
                        k.op("vector", lambda e, xap=xap, od=od: e.tensor_tensor(
                            out=xap, in0=SQ7[:, 0:512], in1=xap, op=ALU.add),
                            reads=[B("sq7"), B("X%d" % od)], writes=[B("X%d" % od)])

        from contextlib import ExitStack as ES
        scope_id = [0]
        def run_scoped(fn, *a):
            scope_id[0] += 1
            sid = scope_id[0]
            with ES() as sub:
                sb2 = lambda name, shape, dt=F32: sub.enter_context(nc.sbuf_tensor("%s_s%d" % (name, sid), list(shape), dt))
                ps2 = lambda name, shape: sub.enter_context(nc.psum_tensor("%s_s%d" % (name, sid), list(shape), F32))
                fn(sb2, ps2, *a)
            k.barrier()

        if NSUB >= 1:
            run_scoped(sublayer1)
        if NSUB >= 2:
            run_scoped(compose_bfly)
            run_scoped(sublayer_ffn)
        if NSUB >= 3:
            run_scoped(sublayer3)
        if NSUB >= 4:
            run_scoped(sublayer_ffn)
        if NSUB >= 5:
            run_scoped(sublayer5)
        if NSUB >= 6:
            run_scoped(sublayer_moe)
        if NSUB >= 7:
            run_scoped(sublayer7)
        if NSUB >= 8:
            run_scoped(sublayer_moe)

        for kd in range(KD):
            k.dma("sync", y_h[kd * 128:(kd + 1) * 128, :], X[:, kd * T:(kd + 1) * T],
                  reads=[B("X%d" % kd)], writes=[B("ydone")])

        with nc.Block() as block:
            k.finalize(block)
    return nc


def kernel(**inputs):
    H = host_prep(inputs)
    x_full = np.ascontiguousarray(inputs["L0_attn_a_x"], dtype=np.float32)
    nc = build_program()
    in_maps = []
    for c in range(8):
        m = dict(H)
        m["x"] = np.ascontiguousarray(x_full[c].T)
        in_maps.append(m)
    res = run_bass_kernel_spmd(nc, in_maps, core_ids=list(range(8)))
    return np.ascontiguousarray(np.stack([np.asarray(r["y"], dtype=np.float32).T for r in res.results], axis=0))


# version 03
# speedup vs baseline: 1.0962x; 1.0962x over previous
import os
import numpy as np
import concourse.bass as bass
import concourse.mybir as mybir
from concourse.bass_utils import run_bass_kernel_spmd

F32 = mybir.dt.float32
BF16 = mybir.dt.bfloat16
AF = mybir.ActivationFunctionType
ALU = mybir.AluOpType

D = 1024
T = 2048
KD = D // 128
DH = 4096
NSUB = int(os.environ.get("KSUB", "8"))

ENGS = ("sync", "scalar", "vector", "tensor", "gpsimd")


class Buf:
    __slots__ = ("w", "r")
    def __init__(self):
        self.w = None
        self.r = []


class K:
    def __init__(self, nc, sems):
        self.nc = nc
        self.sems = sems
        self.cnt = {e: 0 for e in ENGS}
        self.prog = {e: [] for e in ENGS}
        self.hwm = {}
        self._barrier_snap = None
        self._barrier_done = {}

    def op(self, eng, fn, reads=(), writes=(), dma=False, psum_rw=None, sreads=()):
        deps = []
        force = []
        for b in sreads:
            if b.w is not None:
                force.append(b.w)
                deps.append(b.w)
        bs = self._barrier_snap
        if bs is not None and not self._barrier_done.get(eng, False):
            deps.extend((pe, v) for pe, v in bs.items() if pe != eng and v > 0)
            self._barrier_done[eng] = True
        for b in reads:
            if b.w is not None:
                deps.append(b.w)
        for b in writes:
            if b.w is not None:
                deps.append(b.w)
            deps.extend(b.r)
        if psum_rw is not None:
            if psum_rw.w is not None:
                deps.append(psum_rw.w)
            deps.extend(psum_rw.r)
        waits = []
        forceset = set((d[0], d[1]) for d in force)
        for dep in deps:
            pe, v = dep[0], dep[1]
            pdma = dep[2] if len(dep) > 2 else False
            forced = (pe, v) in forceset
            if pdma:
                v = self.cnt[pe]
            if pe == eng and not pdma and not forced:
                continue
            key = (eng, pe)
            if self.hwm.get(key, 0) >= v:
                continue
            self.hwm[key] = v
            waits.append((pe, v))
        inc = 16 if dma else 1
        self.cnt[eng] += inc
        tok = (eng, self.cnt[eng], dma)
        self.prog[eng].append((fn, waits, inc))
        for b in reads:
            b.r.append(tok)
        for b in writes:
            b.w = tok
            b.r = []
        if psum_rw is not None:
            psum_rw.w = tok
        return tok

    def dma(self, eng, out_ap, in_ap, reads=(), writes=()):
        return self.op(eng, lambda e, o=out_ap, i=in_ap: e.dma_start(out=o, in_=i),
                       reads=reads, writes=writes, dma=True)

    def barrier(self):
        self._barrier_snap = dict(self.cnt)
        self._barrier_done = {}

    def finalize(self, block, final_engine="sync"):
        snap = dict(self.cnt)
        progs, sems = self.prog, self.sems

        def make(engname):
            def emit(e):
                for fn, waits, inc in progs[engname]:
                    for (pe, v) in waits:
                        e.wait_ge(sems[pe], v)
                    fn(e).then_inc(sems[engname], inc)
                if engname == final_engine:
                    for pe in ENGS:
                        if snap[pe] > 0:
                            e.wait_ge(sems[pe], snap[pe])
            return emit

        block.sync(make("sync"))
        block.scalar(make("scalar"))
        block.vector(make("vector"))
        block.tensor(make("tensor"))
        block.gpsimd(make("gpsimd"))


def _pack_kd_cols(w):
    d_in, m = w.shape
    nk = d_in // 128
    return np.ascontiguousarray(w.reshape(nk, 128, m).transpose(1, 0, 2).reshape(128, nk * m))


def _cols(v):
    n = v.shape[0] // 128
    return np.ascontiguousarray(v.reshape(n, 128).T)


def _perm_order():
    g, wl, b = np.meshgrid(np.arange(32), np.arange(4), np.arange(32), indexing="ij")
    return (b * 128 + 4 * g + wl).reshape(-1)

_PERM = _perm_order()


def _bfly_coeffs(weights):
    out = np.empty((12, 4, 2048), np.float32)
    for j in range(12):
        s = 2 ** j
        if s <= 64:
            nb_l = 128 // (2 * s)
            b, nb, sl = np.meshgrid(np.arange(32), np.arange(nb_l), np.arange(s), indexing="ij")
            kmap = (b * (64 // s) + nb) * s + sl
        else:
            m = j - 7
            sm = 2 ** m
            nb_l = 32 // (2 * sm)
            g, wl, nb, sl = np.meshgrid(np.arange(32), np.arange(4), np.arange(nb_l), np.arange(sm), indexing="ij")
            kmap = nb * (128 * sm) + sl * 128 + (4 * g + wl)
        kf = kmap.reshape(-1)
        for ci, (r, c) in enumerate(((0, 0), (0, 1), (1, 0), (1, 1))):
            out[j, ci, :] = weights[j, kf, r, c]
    o = np.broadcast_to(out[:, :, None, :], (12, 4, 128, 2048))
    return np.ascontiguousarray(o.transpose(0, 2, 1, 3).reshape(12, 128, 4 * 2048))


def host_prep(I):
    H = {}
    H["sl1_wzT"] = _pack_kd_cols(np.ascontiguousarray(I["L0_attn_a_Wz"].T))
    H["sl1_whT"] = _pack_kd_cols(np.ascontiguousarray(I["L0_attn_a_Wh"].T))
    H["sl1_bz"] = _cols(I["L0_attn_a_bz"])
    H["sl1_bh"] = _cols(I["L0_attn_a_bh"])
    H["sl2_wexp"] = _pack_kd_cols(I["L0_ffn_W_exp"])
    wcp = I["L0_ffn_W_con"][_PERM, :]
    H["sl2_wcon"] = _pack_kd_cols(np.ascontiguousarray(wcp))
    H["sl2_bexp"] = _cols(I["L0_ffn_b_exp"])
    H["sl2_upbp"] = _cols(np.ascontiguousarray(I["L0_ffn_up_bias"][_PERM]))
    H["sl2_bcon"] = _cols(I["L0_ffn_b_con"])
    H["sl2_dwnb"] = _cols(I["L0_ffn_down_bias"])
    H["sl2_upco"] = _bfly_coeffs(I["L0_ffn_up_weights"])
    H["sl2_dnco"] = _bfly_coeffs(I["L0_ffn_down_weights"])
    H["sl3_wxp"] = _pack_kd_cols(I["L0_attn_b_w_xproj"])
    H["sl3_wdt"] = np.ascontiguousarray(I["L0_attn_b_w_dt"])
    H["sl3_bdt"] = _cols(I["L0_attn_b_b_dt"])
    H["sl3_alogc"] = _pack_kd_cols(I["L0_attn_b_A_log"])
    H["sl3_dssm"] = _cols(I["L0_attn_b_D_ssm"])
    H["sl3_lng"] = _cols(I["L0_attn_b_ln_g"])
    H["sl3_lnb"] = _cols(I["L0_attn_b_ln_b"])
    H["sl3_wconv"] = _pack_kd_cols(I["L0_attn_b_w_conv"])
    H["sl3_bconv"] = _cols(I["L0_attn_b_b_conv"])
    H["sl3_wssm"] = _pack_kd_cols(I["L0_attn_b_w_ssm_out"])
    H["sl3_wq"] = _pack_kd_cols(I["L0_attn_b_wq"])
    H["sl3_wk"] = _pack_kd_cols(I["L0_attn_b_wk"])
    H["sl3_wv"] = _pack_kd_cols(I["L0_attn_b_wv"])
    H["sl3_wo"] = _pack_kd_cols(I["L0_attn_b_wo"])
    H["sl3_wwm"] = _pack_kd_cols(I["L0_attn_b_w_wm"])
    H["sl3_bwm"] = _cols(I["L0_attn_b_b_wm"])
    H["sl3_episT"] = np.ascontiguousarray(I["L0_attn_b_episodic"][0].T)
    H["sl3_wdc1"] = np.ascontiguousarray(I["L0_attn_b_w_dc1"])
    H["sl3_bdc1"] = _cols(I["L0_attn_b_b_dc1"])
    H["sl3_wdc2"] = _pack_kd_cols(I["L0_attn_b_w_dc2"])
    H["sl3_bdc2"] = _cols(I["L0_attn_b_b_dc2"])
    H["sl3_bdc2b"] = np.ascontiguousarray(np.broadcast_to(I["L0_attn_b_b_dc2"][None, :], (128, 1024)))
    rb = I["L0_attn_b_rel_bias"]
    qi = np.arange(256)[:, None]
    kj = np.arange(768)[None, :]
    rel = kj - 256 - qi
    win = np.abs(rel) <= 256
    bias = rb[np.clip(rel + 256, 0, 512), :]
    bias = np.where(win[:, :, None], bias, np.float32(-1e9))
    H["sl3_bias"] = np.ascontiguousarray(bias.transpose(2, 0, 1).reshape(16, 2, 128, 768))
    wqkv = I["L1_attn_a_w_qkv"]
    hh_, kk_ = np.meshgrid(np.arange(16), np.arange(32), indexing="ij")
    ev = (hh_ * 64 + 2 * kk_).reshape(-1)
    od = (hh_ * 64 + 2 * kk_ + 1).reshape(-1)
    qperm = np.concatenate([ev, od])
    wq_p = wqkv[qperm, :]
    wk_p = wqkv[1024 + qperm, :]
    wv_n = wqkv[2048:, :]
    wqkv_cols = np.concatenate([wq_p, wk_p, wv_n], axis=0).T
    H["sl5_wqkvT"] = _pack_kd_cols(np.ascontiguousarray(wqkv_cols))
    H["sl5_woutT"] = _pack_kd_cols(np.ascontiguousarray(I["L1_attn_a_w_out"].T))
    H["sl5_bout"] = _cols(I["L1_attn_a_b_out"])
    H["sl5_tri"] = np.ascontiguousarray(np.tril(np.ones((128, 128), np.float32)))
    i32 = np.arange(32, dtype=np.float64)
    th64 = 10000.0 ** (-2.0 * (i32 - 1.0) / 64.0)
    th_p = np.tile(th64, 4)
    thhi = th_p.astype(np.float32)
    thlo = (th_p - thhi.astype(np.float64)).astype(np.float32)
    H["sl5_thhi"] = np.ascontiguousarray(thhi[:, None])
    H["sl5_thlo"] = np.ascontiguousarray(thlo[:, None])
    m64 = np.arange(2048, dtype=np.float64)
    H["sl5_m128"] = np.ascontiguousarray(np.broadcast_to(m64.astype(np.float32)[None, :], (128, 2048)))
    ang = th_p[:, None] * m64[None, :]
    tp = 2.0 * np.pi
    ks_ = np.round(ang / tp)
    sh_s = -tp * ks_
    H["sl5_shi"] = np.ascontiguousarray(sh_s.astype(np.float32))
    H["sl5_slo"] = np.ascontiguousarray((sh_s - H["sl5_shi"].astype(np.float64)).astype(np.float32))
    angc = ang + np.pi / 2
    kc_ = np.round(angc / tp)
    sh_c = np.pi / 2 - tp * kc_
    H["sl5_shic"] = np.ascontiguousarray(sh_c.astype(np.float32))
    H["sl5_sloc"] = np.ascontiguousarray((sh_c - H["sl5_shic"].astype(np.float64)).astype(np.float32))
    H["sl6_gwT"] = _pack_kd_cols(np.ascontiguousarray(I["L1_ffn_gate_w"].T))
    H["sl6_nw"] = _cols(I["L1_ffn_norm_weight"])
    H["sl6_w1T"] = np.ascontiguousarray(np.stack([_pack_kd_cols(np.ascontiguousarray(I["L1_ffn_w1"][e].T)) for e in range(8)]))
    H["sl6_w2T"] = np.ascontiguousarray(np.stack([_pack_kd_cols(np.ascontiguousarray(I["L1_ffn_w2"][e].T)) for e in range(8)]))
    H["sl6_b1"] = np.ascontiguousarray(np.stack([_cols(I["L1_ffn_b1"][e]) for e in range(8)]))
    H["sl6_b2"] = np.ascontiguousarray(np.stack([_cols(I["L1_ffn_b2"][e]) for e in range(8)]))
    H["sl7_waT"] = _pack_kd_cols(np.ascontiguousarray(I["L1_attn_b_w_attn"].T))
    H["sl7_ba"] = _cols(I["L1_attn_b_b_attn"])
    H["sl7_laT"] = _pack_kd_cols(np.ascontiguousarray(I["L1_attn_b_la_attn"].T))
    H["sl7_lbT"] = np.ascontiguousarray(I["L1_attn_b_lb_attn"].T)
    H["sl7_wpT"] = _pack_kd_cols(np.ascontiguousarray(I["L1_attn_b_w_proj"].T))
    H["sl7_bp"] = _cols(I["L1_attn_b_b_proj"])
    H["sl7_lapT"] = _pack_kd_cols(np.ascontiguousarray(I["L1_attn_b_la_proj"].T))
    H["sl7_lbpT"] = np.ascontiguousarray(I["L1_attn_b_lb_proj"].T)
    H["sl7_bav"] = np.ascontiguousarray(np.broadcast_to(I["L1_attn_b_b_attn"][2048:][None, :], (128, 1024)))
    H["sl7_tria"] = np.ascontiguousarray(np.triu(np.full((128, 128), -1e9, np.float32), 1))
    H["eyerep"] = np.ascontiguousarray(
        np.broadcast_to(np.eye(128, dtype=np.float32)[None], (32, 128, 128))
        .transpose(1, 0, 2).reshape(128, 32 * 128))
    return H


HOST_SPECS = {
    "sl1_wzT": (128, KD * D), "sl1_whT": (128, KD * D),
    "sl1_bz": (128, KD), "sl1_bh": (128, KD),
    "sl2_wexp": (128, KD * DH), "sl2_wcon": (128, 32 * D),
    "sl2_bexp": (128, 32), "sl2_upbp": (128, 32),
    "sl2_bcon": (128, KD), "sl2_dwnb": (128, KD),
    "sl2_upco": (12, 128, 4 * 2048), "sl2_dnco": (12, 128, 4 * 2048),
    "eyerep": (128, 32 * 128),
    "sl3_wxp": (128, KD * 192), "sl3_wdt": (64, D), "sl3_bdt": (128, KD),
    "sl3_alogc": (128, KD * 64), "sl3_dssm": (128, KD), "sl3_lng": (128, KD),
    "sl3_lnb": (128, KD), "sl3_wconv": (128, 32), "sl3_bconv": (128, KD),
    "sl3_wssm": (128, KD * D), "sl3_wq": (128, KD * D), "sl3_wk": (128, KD * 256),
    "sl3_wv": (128, KD * 256), "sl3_wo": (128, KD * D), "sl3_wwm": (128, KD * D),
    "sl3_bwm": (128, KD), "sl3_episT": (128, 4096), "sl3_wdc1": (128, 512),
    "sl3_bdc1": (128, 4), "sl3_wdc2": (128, 4 * D), "sl3_bdc2": (128, KD),
    "sl3_bdc2b": (128, D), "sl3_bias": (16, 2, 128, 768),
    "sl5_wqkvT": (128, KD * 3072), "sl5_woutT": (128, KD * D), "sl5_bout": (128, KD),
    "sl5_tri": (128, 128), "sl5_thhi": (128, 1), "sl5_thlo": (128, 1),
    "sl5_m128": (128, T), "sl5_shi": (128, T), "sl5_slo": (128, T),
    "sl5_shic": (128, T), "sl5_sloc": (128, T),
    "sl6_gwT": (128, KD * 8), "sl6_nw": (128, KD),
    "sl6_w1T": (8, 128, KD * 2048), "sl6_w2T": (8, 128, 16 * D),
    "sl6_b1": (8, 128, 16), "sl6_b2": (8, 128, KD),
    "sl7_waT": (128, KD * 3072), "sl7_ba": (128, 24), "sl7_laT": (128, KD * 32),
    "sl7_lbT": (32, 3072), "sl7_wpT": (128, KD * D), "sl7_bp": (128, KD),
    "sl7_lapT": (128, KD * 32), "sl7_lbpT": (32, D), "sl7_bav": (128, D),
    "sl7_tria": (128, 128),
}


def build_program():
    nc = bass.Bass("TRN2", target_bir_lowering=False)
    x_h = nc.dram_tensor("x", [D, T], F32, kind="ExternalInput").ap()
    y_h = nc.dram_tensor("y", [D, T], F32, kind="ExternalOutput").ap()
    h_in = {n: nc.dram_tensor(n, list(sh), F32, kind="ExternalInput").ap()
            for n, sh in HOST_SPECS.items()}

    from contextlib import ExitStack
    with ExitStack() as stack:
        sems = {e: stack.enter_context(nc.semaphore("s_" + e)) for e in ENGS}
        k = K(nc, sems)
        sb = lambda name, shape, dt=F32: stack.enter_context(nc.sbuf_tensor(name, list(shape), dt))
        ps = lambda name, shape: stack.enter_context(nc.psum_tensor(name, list(shape), F32))

        X = sb("X", (128, KD * T))
        ONES1 = sb("ONES1", (1, 128))
        ONESC = sb("ONESC", (128, 1))
        CONSTC = sb("CONSTC", (128, 8))
        bufs = {}
        def B(name):
            if name not in bufs:
                bufs[name] = Buf()
            return bufs[name]
        PS = [ps("ps%d" % i, (128, 512)) for i in range(8)]
        PSB = [B("ps%d" % i) for i in range(8)]

        def free_psum(i, tok_buf):
            PSB[i].r = []
            PSB[i].w = tok_buf.w

        k.op("vector", lambda e: e.memset(ONES1[:], 1.0), writes=[B("ones1")])
        k.op("vector", lambda e: e.memset(ONESC[:], 1.0), writes=[B("onesc")])
        k.op("vector", lambda e: e.memset(CONSTC[:, 0:1], 1e-5), writes=[B("constc")])
        k.op("vector", lambda e: e.memset(CONSTC[:, 1:2], 0.5), writes=[B("constc")])
        k.op("vector", lambda e: e.memset(CONSTC[:, 2:3], 1e-6), writes=[B("constc")])

        for kd in range(KD):
            k.dma("sync", X[:, kd * T:(kd + 1) * T], x_h[kd * 128:(kd + 1) * 128, :], writes=[B("X%d" % kd)])

        def ln_stats(RB, MB, SQ, STATR, eps_col):
            mu = STATR[0:1, 0:512]
            ex2 = STATR[0:1, 512:1024]
            rstd = STATR[0:1, 1024:1536]
            for c in range(4):
                for kd in range(KD):
                    xap = X[:, kd * T + c * 512: kd * T + (c + 1) * 512]
                    k.op("scalar", lambda e, xap=xap, kd=kd: e.activation(
                        out=SQ[:, (kd % 2) * 512:(kd % 2) * 512 + 512], in_=xap, func=AF.Square),
                        reads=[B("X%d" % kd)], writes=[B("sq%d" % (kd % 2))])
                    k.op("tensor", lambda e, xap=xap, kd=kd: e.matmul(
                        PS[0][0:1, 0:512], ONESC[:, 0:1], xap, start=(kd == 0), stop=(kd == KD - 1)),
                        reads=[B("onesc"), B("X%d" % kd)], psum_rw=PSB[0])
                    k.op("tensor", lambda e, kd=kd: e.matmul(
                        PS[1][0:1, 0:512], ONESC[:, 0:1], SQ[:, (kd % 2) * 512:(kd % 2) * 512 + 512],
                        start=(kd == 0), stop=(kd == KD - 1)),
                        reads=[B("onesc"), B("sq%d" % (kd % 2))], psum_rw=PSB[1])
                k.op("scalar", lambda e: e.mul(mu, PS[0][0:1, 0:512], 1.0 / D),
                     reads=[PSB[0]], writes=[B("mu")])
                free_psum(0, B("mu"))
                k.op("scalar", lambda e: e.mul(ex2, PS[1][0:1, 0:512], 1.0 / D),
                     reads=[PSB[1]], writes=[B("ex2")])
                free_psum(1, B("ex2"))
                k.op("vector", lambda e: e.tensor_tensor(out=rstd, in0=mu, in1=mu, op=ALU.mult),
                     reads=[B("mu")], writes=[B("rstd")])
                k.op("vector", lambda e: e.tensor_tensor(out=rstd, in0=ex2, in1=rstd, op=ALU.subtract),
                     reads=[B("ex2"), B("rstd")], writes=[B("rstd")])
                k.op("scalar", lambda e: e.activation(out=rstd, in_=rstd, func=AF.Sqrt, bias=eps_col[0:1, :]),
                     reads=[B("rstd"), B("constc")], writes=[B("rstd")])
                k.op("vector", lambda e: e.reciprocal(out=rstd, in_=rstd), reads=[B("rstd")], writes=[B("rstd")])
                k.op("vector", lambda e: e.tensor_tensor(out=ex2, in0=mu, in1=rstd, op=ALU.mult),
                     reads=[B("mu"), B("rstd")], writes=[B("ex2")])
                k.op("tensor", lambda e: e.matmul(PS[0][:, 0:512], ONES1[0:1, :], rstd, start=True, stop=True),
                     reads=[B("ones1"), B("rstd")], writes=[PSB[0]])
                k.op("vector", lambda e, c=c: e.tensor_copy(out=RB[:, c * 512:(c + 1) * 512], in_=PS[0][:, 0:512]),
                     reads=[PSB[0]], writes=[B("RB")])
                free_psum(0, B("RB"))
                k.op("tensor", lambda e: e.matmul(PS[1][:, 0:512], ONES1[0:1, :], ex2, start=True, stop=True),
                     reads=[B("ones1"), B("ex2")], writes=[PSB[1]])
                k.op("vector", lambda e, c=c: e.tensor_copy(out=MB[:, c * 512:(c + 1) * 512], in_=PS[1][:, 0:512]),
                     reads=[PSB[1]], writes=[B("MB")])
                free_psum(1, B("MB"))

        def make_xnb(RB, MB, SQ, dst, dst_bname, kd, csl, dsl):
            xap = X[:, kd * T:(kd + 1) * T][:, csl]
            n = csl.stop - csl.start
            sq = SQ[:, 0:n]
            k.op("vector", lambda e, xap=xap, sq=sq, csl=csl: e.tensor_tensor(
                out=sq, in0=xap, in1=RB[:, csl], op=ALU.mult),
                reads=[B("X%d" % kd), B("RB")], writes=[B("sqx")])
            k.op("vector", lambda e, sq=sq, dsl=dsl, csl=csl: e.tensor_tensor(
                out=dst[:, dsl], in0=sq, in1=MB[:, csl], op=ALU.subtract),
                reads=[B("sqx"), B("MB")], writes=[B(dst_bname)])

        def sublayer1(sb, ps):
            RB = sb("RB", (128, T)); MB = sb("MB", (128, T))
            SQ = sb("SQ", (128, T)); STATR = sb("STATR", (1, 1536))
            XNB = sb("XNB1", (128, KD * T), BF16)
            wz = sb("wz", (128, KD * D), BF16)
            wh = sb("wh", (128, KD * D), BF16)
            bzc = sb("bzc", (128, KD)); bhc = sb("bhc", (128, KD)); nbz = sb("nbz", (128, KD))
            Ct = sb("Ct", (128, T)); PREt = sb("PREt", (128, T)); Gt = sb("Gt", (128, T))
            for kd in range(KD):
                k.dma("gpsimd", wz[:, kd * D:(kd + 1) * D], h_in["sl1_wzT"][:, kd * D:(kd + 1) * D], writes=[B("wz")])
                k.dma("gpsimd", wh[:, kd * D:(kd + 1) * D], h_in["sl1_whT"][:, kd * D:(kd + 1) * D], writes=[B("wh")])
            k.dma("sync", bzc[:], h_in["sl1_bz"], writes=[B("bzc")])
            k.dma("sync", bhc[:], h_in["sl1_bh"], writes=[B("bhc")])
            k.op("vector", lambda e: e.tensor_scalar(out=nbz[:], in0=bzc[:], scalar1=-1.0, scalar2=None, op0=ALU.mult),
                 reads=[B("bzc")], writes=[B("nbz")])
            ln_stats(RB, MB, SQ, STATR, CONSTC[:, 0:1])
            k.barrier()
            for kd in range(KD):
                make_xnb(RB, MB, SQ, XNB, "XNB%d" % kd, kd, slice(0, T), slice(kd * T, (kd + 1) * T))
            for od in range(KD):
                for kd in range(KD):
                    for c in range(4):
                        rhs = XNB[:, kd * T + c * 512: kd * T + (c + 1) * 512]
                        lw = wz[:, kd * D + od * 128: kd * D + (od + 1) * 128]
                        k.op("tensor", lambda e, lw=lw, rhs=rhs, kd=kd, c=c: e.matmul(
                            PS[c][:, 0:512], lw, rhs, start=(kd == 0), stop=(kd == KD - 1)),
                            reads=[B("wz"), B("XNB%d" % kd)], psum_rw=PSB[c])
                    for c in range(4):
                        rhs = XNB[:, kd * T + c * 512: kd * T + (c + 1) * 512]
                        lw2 = wh[:, kd * D + od * 128: kd * D + (od + 1) * 128]
                        k.op("tensor", lambda e, lw2=lw2, rhs=rhs, kd=kd, c=c: e.matmul(
                            PS[4 + c][:, 0:512], lw2, rhs, start=(kd == 0), stop=(kd == KD - 1)),
                            reads=[B("wh"), B("XNB%d" % kd)], psum_rw=PSB[4 + c])
                for c in range(4):
                    k.op("scalar", lambda e, c=c, od=od: e.activation(
                        out=Ct[:, c * 512:(c + 1) * 512], in_=PS[c][:, 0:512], func=AF.Sigmoid,
                        bias=nbz[:, od:od + 1], scale=-1.0),
                        reads=[PSB[c], B("nbz")], writes=[B("Ct")])
                    free_psum(c, B("Ct"))
                    k.op("scalar", lambda e, c=c, od=od: e.activation(
                        out=PREt[:, c * 512:(c + 1) * 512], in_=PS[4 + c][:, 0:512], func=AF.Identity,
                        bias=bhc[:, od:od + 1]),
                        reads=[PSB[4 + c], B("bhc")], writes=[B("PREt")])
                    free_psum(4 + c, B("PREt"))
                k.op("vector", lambda e: e.tensor_scalar(out=SQ[:], in0=Ct[:], scalar1=-1.0, scalar2=1.0,
                                                         op0=ALU.mult, op1=ALU.add),
                     reads=[B("Ct")], writes=[B("sqx")])
                k.op("vector", lambda e: e.tensor_scalar(out=Gt[:], in0=PREt[:], scalar1=0.0, scalar2=None,
                                                         op0=ALU.min),
                     reads=[B("PREt")], writes=[B("Gt")])
                k.op("scalar", lambda e: e.activation(out=Gt[:], in_=Gt[:], func=AF.Sigmoid),
                     reads=[B("Gt")], writes=[B("Gt")])
                k.op("vector", lambda e: e.scalar_tensor_tensor(out=Gt[:], in0=PREt[:], scalar=0.0, in1=Gt[:],
                                                                op0=ALU.max, op1=ALU.add),
                     reads=[B("PREt"), B("Gt")], writes=[B("Gt")])
                k.op("vector", lambda e: e.tensor_tensor(out=Gt[:], in0=Gt[:], in1=SQ[:], op=ALU.mult),
                     reads=[B("Gt"), B("sqx")], writes=[B("Gt")])
                k.op("vector", lambda e: e.tensor_tensor_scan(PREt[:], Ct[:], Gt[:], CONSTC[:, 1:2],
                                                              ALU.mult, ALU.add),
                     reads=[B("Ct"), B("Gt"), B("constc")], writes=[B("PREt")])
                k.op("vector", lambda e, od=od: e.tensor_tensor(
                    out=X[:, od * T:(od + 1) * T], in0=X[:, od * T:(od + 1) * T], in1=PREt[:], op=ALU.add),
                    reads=[B("PREt"), B("X%d" % od)], writes=[B("X%d" % od)])

        bfst_h = nc.dram_tensor("bfst", [128, 4 * 4096], BF16, kind="Internal").ap()

        def compose_bfly(sb, ps):
            STLB_UP = sb("STLB_UP", (128, 32 * 128), BF16)
            STHB_UP = sb("STHB_UP", (128, 32 * 128), BF16)
            STLB_DN = sb("STLB_DN", (128, 32 * 128), BF16)
            STHB_DN = sb("STHB_DN", (128, 32 * 128), BF16)
            PING = sb("PING", (128, 32 * 128))
            PONG = sb("PONG", (128, 32 * 128))
            CO = sb("CO", (128, 4 * 2048))
            TA = sb("TA", (128, 2048)); TB = sb("TB", (128, 2048))

            def compose(co_name, stages, dstB, dname):
                k.dma("sync", PING[:], h_in["eyerep"], writes=[B("PING")])
                cur, nxt = PING, PONG
                curb, nxtb = "PING", "PONG"
                for j in stages:
                    s = 2 ** j
                    k.dma("sync", CO[:], h_in[co_name][j], writes=[B("CO")])
                    if s <= 64:
                        nb_l = 128 // (2 * s)
                        view = lambda t, nb_l=nb_l: t.rearrange("p (b nb two s) -> p b nb two s", b=32, nb=nb_l, two=2)
                        cview = lambda t, nb_l=nb_l: t.rearrange("p (b nb s) -> p b nb s", b=32, nb=nb_l)
                    else:
                        sm = s // 128
                        nb_l = 32 // (2 * sm)
                        view = lambda t, nb_l=nb_l: t.rearrange("p (g wl nb two s) -> p g wl nb two s", g=32, wl=4, nb=nb_l, two=2)
                        cview = lambda t, nb_l=nb_l: t.rearrange("p (g wl nb s) -> p g wl nb s", g=32, wl=4, nb=nb_l)
                    sel_a = (slice(None),) * 1
                    av = view(cur[:])
                    bv = view(cur[:])
                    nav = view(nxt[:])
                    nbv = view(nxt[:])
                    nd = len(av.shape)
                    idx_a = tuple([slice(None)] * (nd - 2) + [0, slice(None)])
                    idx_b = tuple([slice(None)] * (nd - 2) + [1, slice(None)])
                    av = av[idx_a]; bv = bv[idx_b]; nav = nav[idx_a]; nbv = nbv[idx_b]
                    w00 = cview(CO[:, 0:2048]); w01 = cview(CO[:, 2048:4096])
                    w10 = cview(CO[:, 4096:6144]); w11 = cview(CO[:, 6144:8192])
                    TAv = cview(TA[:]); TBv = cview(TB[:])
                    k.op("vector", lambda e, av=av, w00=w00, TAv=TAv: e.tensor_tensor(out=TAv, in0=av, in1=w00, op=ALU.mult),
                         reads=[B(curb), B("CO")], writes=[B("TA")])
                    k.op("vector", lambda e, bv=bv, w01=w01, TBv=TBv: e.tensor_tensor(out=TBv, in0=bv, in1=w01, op=ALU.mult),
                         reads=[B(curb), B("CO")], writes=[B("TB")])
                    k.op("vector", lambda e, nav=nav, TAv=TAv, TBv=TBv: e.tensor_tensor(out=nav, in0=TAv, in1=TBv, op=ALU.add),
                         reads=[B("TA"), B("TB")], writes=[B(nxtb)])
                    k.op("vector", lambda e, av=av, w10=w10, TAv=TAv: e.tensor_tensor(out=TAv, in0=av, in1=w10, op=ALU.mult),
                         reads=[B(curb), B("CO")], writes=[B("TA")])
                    k.op("vector", lambda e, bv=bv, w11=w11, TBv=TBv: e.tensor_tensor(out=TBv, in0=bv, in1=w11, op=ALU.mult),
                         reads=[B(curb), B("CO")], writes=[B("TB")])
                    k.op("vector", lambda e, nbv=nbv, TAv=TAv, TBv=TBv: e.tensor_tensor(out=nbv, in0=TAv, in1=TBv, op=ALU.add),
                         reads=[B("TA"), B("TB")], writes=[B(nxtb)])
                    cur, nxt = nxt, cur
                    curb, nxtb = nxtb, curb
                k.op("vector", lambda e, cur=cur, dstB=dstB: e.tensor_copy(out=dstB[:], in_=cur[:]),
                     reads=[B(curb)], writes=[B(dname)])

            compose("sl2_upco", range(0, 7), STLB_UP, "STLB_UP")
            compose("sl2_upco", range(7, 12), STHB_UP, "STHB_UP")
            compose("sl2_dnco", range(0, 7), STLB_DN, "STLB_DN")
            compose("sl2_dnco", range(7, 12), STHB_DN, "STHB_DN")
            for i, (t, nm) in enumerate(((STLB_UP, "STLB_UP"), (STHB_UP, "STHB_UP"),
                                         (STLB_DN, "STLB_DN"), (STHB_DN, "STHB_DN"))):
                k.dma("sync", bfst_h[:, i * 4096:(i + 1) * 4096], t[:], reads=[B(nm)], writes=[B("bfspill")])
            if os.environ.get("KDBG"):
                for nm, t in (("dbg_lu", STLB_UP), ("dbg_hu", STHB_UP), ("dbg_ld", STLB_DN), ("dbg_hd", STHB_DN)):
                    o = nc.dram_tensor(nm, [128, 32 * 128], BF16, kind="ExternalOutput").ap()
                    k.dma("sync", o, t[:], reads=[B(nm.replace("dbg_lu", "STLB_UP").replace("dbg_hu", "STHB_UP").replace("dbg_ld", "STLB_DN").replace("dbg_hd", "STHB_DN"))], writes=[B("dbgdone")])

        def sublayer_ffn(sb, ps):
            STLB_UP = sb("STLB_UP", (128, 32 * 128), BF16)
            STHB_UP = sb("STHB_UP", (128, 32 * 128), BF16)
            STLB_DN = sb("STLB_DN", (128, 32 * 128), BF16)
            STHB_DN = sb("STHB_DN", (128, 32 * 128), BF16)
            for i, (t, nm) in enumerate(((STLB_UP, "STLB_UP"), (STHB_UP, "STHB_UP"),
                                         (STLB_DN, "STLB_DN"), (STHB_DN, "STHB_DN"))):
                k.dma("sync", t[:], bfst_h[:, i * 4096:(i + 1) * 4096], reads=[B("bfspill")], writes=[B(nm)])
            RB = sb("RB", (128, T)); MB = sb("MB", (128, T))
            SQ = sb("SQ", (128, 1024)); STATR = sb("STATR", (1, 1536))
            XNC = sb("XNC", (128, KD * 512), BF16)
            WE = sb("WE", (128, 4 * 1024), BF16)
            WC = sb("WC", (128, 2 * 1024), BF16)
            HA = sb("HA", (128, 32 * 512), BF16)
            HB = sb("HB", (128, 32 * 512), BF16)
            BEX = sb("BEX", (128, 32)); UPB = sb("UPB", (128, 32))
            BCN = sb("BCN", (128, KD)); DWN = sb("DWN", (128, KD))
            k.dma("sync", BEX[:], h_in["sl2_bexp"], writes=[B("BEX")])
            k.dma("sync", UPB[:], h_in["sl2_upbp"], writes=[B("UPB")])
            k.dma("sync", BCN[:], h_in["sl2_bcon"], writes=[B("BCN")])
            k.dma("sync", DWN[:], h_in["sl2_dwnb"], writes=[B("DWN")])
            ln_stats(RB, MB, SQ, STATR, CONSTC[:, 0:1])
            k.barrier()
            for tc in range(4):
                csl = slice(tc * 512, (tc + 1) * 512)
                for kd in range(KD):
                    make_xnb(RB, MB, SQ, XNC, "XNC%d" % kd, kd, csl, slice(kd * 512, (kd + 1) * 512))
                for hog in range(8):
                    for kd in range(KD):
                        wsl = slice(kd * DH + hog * 512, kd * DH + (hog + 1) * 512)
                        k.dma("gpsimd", WE[:, kd * 512:(kd + 1) * 512], h_in["sl2_wexp"][:, wsl],
                              writes=[B("WE%d" % kd)])
                    for kd in range(KD):
                        for hh in range(4):
                            lw = WE[:, kd * 512 + hh * 128: kd * 512 + (hh + 1) * 128]
                            rhs = XNC[:, kd * 512:(kd + 1) * 512]
                            k.op("tensor", lambda e, lw=lw, rhs=rhs, kd=kd, hh=hh: e.matmul(
                                PS[hh][:, 0:512], lw, rhs, start=(kd == 0), stop=(kd == KD - 1)),
                                reads=[B("WE%d" % kd), B("XNC%d" % kd)], psum_rw=PSB[hh])
                    for hh in range(4):
                        ho = hog * 4 + hh
                        k.op("scalar", lambda e, hh=hh, ho=ho: e.activation(
                            out=HA[:, ho * 512:(ho + 1) * 512], in_=PS[hh][:, 0:512], func=AF.Identity,
                            bias=BEX[:, ho:ho + 1]),
                            reads=[PSB[hh], B("BEX")], writes=[B("HA%d" % ho)])
                        free_psum(hh, B("HA%d" % ho))
                if os.environ.get("KDBG2") and tc == 0:
                    o = nc.dram_tensor("dbgp1", [128, 32 * 512], BF16, kind="ExternalOutput").ap()
                    k.dma("sync", o, HA[:], reads=[B("HA%d" % bq) for bq in range(32)], writes=[B("dbg")])
                for bg in range(4):
                    for bb in range(8):
                        b_ = bg * 8 + bb
                        lw = STLB_UP[:, b_ * 128:(b_ + 1) * 128]
                        rhs = HA[:, b_ * 512:(b_ + 1) * 512]
                        k.op("tensor", lambda e, lw=lw, rhs=rhs, bb=bb: e.matmul(
                            PS[bb][:, 0:512], lw, rhs, start=True, stop=True),
                            reads=[B("STLB_UP"), B("HA%d" % b_)], writes=[PSB[bb]])
                    for bb in range(8):
                        b_ = bg * 8 + bb
                        k.op("vector", lambda e, bb=bb, b_=b_: e.tensor_copy(
                            out=HA[:, b_ * 512:(b_ + 1) * 512], in_=PS[bb][:, 0:512]),
                            reads=[PSB[bb]], writes=[B("HA%d" % b_)])
                        free_psum(bb, B("HA%d" % b_))
                if os.environ.get("KDBG2") and tc == 0:
                    o = nc.dram_tensor("dbgp2", [128, 32 * 512], BF16, kind="ExternalOutput").ap()
                    k.dma("sync", o, HA[:], reads=[B("HA%d" % bq) for bq in range(32)], writes=[B("dbg")])
                ha_all = [B("HA%d" % bq) for bq in range(32)]
                p3_last = None
                for g in range(32):
                    src = HA[4 * g:4 * g + 4, :].rearrange("p (b t) -> p b t", b=32)
                    p3_last = k.dma("scalar", HB[:, g * 512:(g + 1) * 512], src,
                                    reads=ha_all, writes=[B("HB%d" % g)])
                for bq in range(32):
                    B("HA%d" % bq).r = [p3_last]
                if os.environ.get("KDBG2") and tc == 0:
                    o = nc.dram_tensor("dbgp3", [128, 32 * 512], BF16, kind="ExternalOutput").ap()
                    k.dma("sync", o, HB[:], reads=[B("HB%d" % g) for g in range(32)], writes=[B("dbg")])
                for gg in range(4):
                    for gb in range(8):
                        g = gg * 8 + gb
                        lw = STHB_UP[:, g * 128:(g + 1) * 128]
                        rhs = HB[:, g * 512:(g + 1) * 512]
                        k.op("tensor", lambda e, lw=lw, rhs=rhs, gb=gb: e.matmul(
                            PS[gb][:, 0:512], lw, rhs, start=True, stop=True),
                            reads=[B("STHB_UP"), B("HB%d" % g)], writes=[PSB[gb]])
                    for gb in range(8):
                        g = gg * 8 + gb
                        k.op("scalar", lambda e, gb=gb, g=g: e.activation(
                            out=HB[:, g * 512:(g + 1) * 512], in_=PS[gb][:, 0:512], func=AF.Gelu,
                            bias=UPB[:, g:g + 1]),
                            reads=[PSB[gb], B("UPB")], writes=[B("HB%d" % g)])
                        free_psum(gb, B("HB%d" % g))
                if os.environ.get("KDBG2") and tc == 0:
                    o = nc.dram_tensor("dbgp4", [128, 32 * 512], BF16, kind="ExternalOutput").ap()
                    k.dma("sync", o, HB[:], reads=[B("HB%d" % g) for g in range(32)], writes=[B("dbg")])
                ha_all = [B("HA%d" % bq) for bq in range(32)]
                last = None
                for g in range(32):
                    dst = HA[4 * g:4 * g + 4, :].rearrange("p (b t) -> p b t", b=32)
                    last = k.dma("scalar", dst, HB[:, g * 512:(g + 1) * 512],
                                 reads=[B("HB%d" % g)] + (ha_all if g == 0 else []),
                                 writes=[])
                for bq in range(32):
                    B("HA%d" % bq).w = last
                    B("HA%d" % bq).r = []
                if os.environ.get("KDBG2") and tc == 0:
                    o = nc.dram_tensor("dbgp5", [128, 32 * 512], BF16, kind="ExternalOutput").ap()
                    k.dma("sync", o, HA[:], reads=[B("HA%d" % bq) for bq in range(32)], writes=[B("dbg")])
                for bg in range(4):
                    for bb in range(8):
                        b_ = bg * 8 + bb
                        lw = STLB_DN[:, b_ * 128:(b_ + 1) * 128]
                        rhs = HA[:, b_ * 512:(b_ + 1) * 512]
                        k.op("tensor", lambda e, lw=lw, rhs=rhs, bb=bb: e.matmul(
                            PS[bb][:, 0:512], lw, rhs, start=True, stop=True),
                            reads=[B("STLB_DN"), B("HA%d" % b_)], writes=[PSB[bb]])
                    for bb in range(8):
                        b_ = bg * 8 + bb
                        k.op("vector", lambda e, bb=bb, b_=b_: e.tensor_copy(
                            out=HA[:, b_ * 512:(b_ + 1) * 512], in_=PS[bb][:, 0:512]),
                            reads=[PSB[bb]], writes=[B("HA%d" % b_)])
                        free_psum(bb, B("HA%d" % b_))
                if os.environ.get("KDBG2") and tc == 0:
                    o = nc.dram_tensor("dbgp6", [128, 32 * 512], BF16, kind="ExternalOutput").ap()
                    k.dma("sync", o, HA[:], reads=[B("HA%d" % bq) for bq in range(32)], writes=[B("dbg")])
                ha_all = [B("HA%d" % bq) for bq in range(32)]
                for g in range(32):
                    src = HA[4 * g:4 * g + 4, :].rearrange("p (b t) -> p b t", b=32)
                    k.dma("scalar", HB[:, g * 512:(g + 1) * 512], src,
                          reads=ha_all, writes=[B("HB%d" % g)])
                for gg in range(4):
                    for gb in range(8):
                        g = gg * 8 + gb
                        lw = STHB_DN[:, g * 128:(g + 1) * 128]
                        rhs = HB[:, g * 512:(g + 1) * 512]
                        k.op("tensor", lambda e, lw=lw, rhs=rhs, gb=gb: e.matmul(
                            PS[gb][:, 0:512], lw, rhs, start=True, stop=True),
                            reads=[B("STHB_DN"), B("HB%d" % g)], writes=[PSB[gb]])
                    for gb in range(8):
                        g = gg * 8 + gb
                        k.op("vector", lambda e, gb=gb, g=g: e.tensor_copy(
                            out=HB[:, g * 512:(g + 1) * 512], in_=PS[gb][:, 0:512]),
                            reads=[PSB[gb]], writes=[B("HB%d" % g)])
                        free_psum(gb, B("HB%d" % g))
                if os.environ.get("KDBG2") and tc == 0:
                    o = nc.dram_tensor("dbgp8", [128, 32 * 512], BF16, kind="ExternalOutput").ap()
                    k.dma("sync", o, HB[:], reads=[B("HB%d" % g) for g in range(32)], writes=[B("dbg")])
                for kh in range(32):
                    wsl = slice(kh * 1024, (kh + 1) * 1024)
                    wbuf = kh % 2
                    k.dma("gpsimd", WC[:, wbuf * 1024:(wbuf + 1) * 1024], h_in["sl2_wcon"][:, wsl],
                          writes=[B("WC%d" % wbuf)])
                    for do in range(8):
                        lw = WC[:, wbuf * 1024 + do * 128: wbuf * 1024 + (do + 1) * 128]
                        rhs = HB[:, kh * 512:(kh + 1) * 512]
                        k.op("tensor", lambda e, lw=lw, rhs=rhs, kh=kh, do=do: e.matmul(
                            PS[do][:, 0:512], lw, rhs, start=(kh == 0), stop=(kh == 31)),
                            reads=[B("WC%d" % wbuf), B("HB%d" % kh)], psum_rw=PSB[do])
                for do in range(8):
                    k.op("scalar", lambda e, do=do: e.activation(
                        out=SQ[:, 0:512], in_=PS[do][:, 0:512], func=AF.Identity, bias=BCN[:, do:do + 1]),
                        reads=[PSB[do], B("BCN")], writes=[B("sqx")])
                    free_psum(do, B("sqx"))
                    xap = X[:, do * T:(do + 1) * T][:, csl]
                    k.op("vector", lambda e, xap=xap, do=do: e.scalar_tensor_tensor(
                        out=xap, in0=SQ[:, 0:512], scalar=DWN[:, do:do + 1], in1=xap,
                        op0=ALU.add, op1=ALU.add),
                        reads=[B("sqx"), B("DWN"), B("X%d" % do)], writes=[B("X%d" % do)])


        delta_h = nc.dram_tensor("delta_h", [128, KD * T], F32, kind="Internal").ap()
        xof_h = nc.dram_tensor("xof_h", [128, KD * T], F32, kind="Internal").ap()
        dtx_h = nc.dram_tensor("dtx_h", [128, KD * T], F32, kind="Internal").ap()
        _kd3 = "ExternalOutput" if os.environ.get("KDBG3") else "Internal"
        dect_h = nc.dram_tensor("dect_h", [128, KD * 4096], BF16, kind=_kd3).ap()
        decn_h = nc.dram_tensor("decn_h", [128, 32 * D], BF16, kind=_kd3).ap()

        def ln_stats_src(srcfn, RB, MB, STATR, eps_col, tagp):
            mu = STATR[0:1, 0:512]
            ex2 = STATR[0:1, 512:1024]
            rstd = STATR[0:1, 1024:1536]
            for c in range(4):
                for kd in range(KD):
                    xap, rdb, sqslot, sqb = srcfn(kd, c)
                    k.op("scalar", lambda e, xap=xap, sqslot=sqslot: e.activation(
                        out=sqslot, in_=xap, func=AF.Square),
                        reads=rdb, writes=[sqb])
                    k.op("tensor", lambda e, xap=xap, kd=kd: e.matmul(
                        PS[0][0:1, 0:512], ONESC[:, 0:1], xap, start=(kd == 0), stop=(kd == KD - 1)),
                        reads=[B("onesc")] + rdb, psum_rw=PSB[0])
                    k.op("tensor", lambda e, kd=kd, sqslot=sqslot: e.matmul(
                        PS[1][0:1, 0:512], ONESC[:, 0:1], sqslot, start=(kd == 0), stop=(kd == KD - 1)),
                        reads=[B("onesc"), sqb], psum_rw=PSB[1])
                k.op("scalar", lambda e: e.mul(mu, PS[0][0:1, 0:512], 1.0 / D), reads=[PSB[0]], writes=[B(tagp + "mu")])
                free_psum(0, B(tagp + "mu"))
                k.op("scalar", lambda e: e.mul(ex2, PS[1][0:1, 0:512], 1.0 / D), reads=[PSB[1]], writes=[B(tagp + "ex2")])
                free_psum(1, B(tagp + "ex2"))
                k.op("vector", lambda e: e.tensor_tensor(out=rstd, in0=mu, in1=mu, op=ALU.mult),
                     reads=[B(tagp + "mu")], writes=[B(tagp + "rstd")])
                k.op("vector", lambda e: e.tensor_tensor(out=rstd, in0=ex2, in1=rstd, op=ALU.subtract),
                     reads=[B(tagp + "ex2"), B(tagp + "rstd")], writes=[B(tagp + "rstd")])
                k.op("scalar", lambda e: e.activation(out=rstd, in_=rstd, func=AF.Sqrt, bias=eps_col[0:1, :]),
                     reads=[B(tagp + "rstd"), B("constc")], writes=[B(tagp + "rstd")])
                k.op("vector", lambda e: e.reciprocal(out=rstd, in_=rstd), reads=[B(tagp + "rstd")], writes=[B(tagp + "rstd")])
                k.op("vector", lambda e: e.tensor_tensor(out=ex2, in0=mu, in1=rstd, op=ALU.mult),
                     reads=[B(tagp + "mu"), B(tagp + "rstd")], writes=[B(tagp + "ex2")])
                k.op("tensor", lambda e: e.matmul(PS[0][:, 0:512], ONES1[0:1, :], rstd, start=True, stop=True),
                     reads=[B("ones1"), B(tagp + "rstd")], writes=[PSB[0]])
                k.op("vector", lambda e, c=c: e.tensor_copy(out=RB[:, c * 512:(c + 1) * 512], in_=PS[0][:, 0:512]),
                     reads=[PSB[0]], writes=[B(tagp + "RB")])
                free_psum(0, B(tagp + "RB"))
                k.op("tensor", lambda e: e.matmul(PS[1][:, 0:512], ONES1[0:1, :], ex2, start=True, stop=True),
                     reads=[B("ones1"), B(tagp + "ex2")], writes=[PSB[1]])
                k.op("vector", lambda e, c=c: e.tensor_copy(out=MB[:, c * 512:(c + 1) * 512], in_=PS[1][:, 0:512]),
                     reads=[PSB[1]], writes=[B(tagp + "MB")])
                free_psum(1, B(tagp + "MB"))

        def xof_chunk_gen(RBx, MBx, rbn, mbn):
            def f(dst, kd, csl, extra_reads=()):
                xap = X[:, kd * T:(kd + 1) * T][:, csl]
                k.op("vector", lambda e, xap=xap, dst=dst, csl=csl: e.tensor_tensor(
                    out=dst, in0=xap, in1=RBx[:, csl], op=ALU.mult),
                    reads=[B("X%d" % kd), B(rbn)] + list(extra_reads), writes=[B("xoft")])
                k.op("vector", lambda e, dst=dst, csl=csl: e.tensor_tensor(
                    out=dst, in0=dst, in1=MBx[:, csl], op=ALU.subtract),
                    reads=[B("xoft"), B(mbn)], writes=[B("xoft")])
                return B("xoft")
            return f

        def sublayer3(sb, ps):
            KPH = int(os.environ.get("KPH", "9"))
            from contextlib import ExitStack as ES3
            RBo = sb("RBo", (128, T)); MBo = sb("MBo", (128, T))
            BRB = sb("BRB", (64, T)); CRB = sb("CRB", (64, T))
            ACOL = sb("ACOL", (128, KD * 64))
            SQ3 = sb("SQ3", (128, T))
            COLS = sb("COLS3", (128, 8 * KD))
            WCV = sb("WCV", (128, 32))
            IDB = sb("IDB", (128, 128), BF16)
            dsc = lambda i: COLS[:, 0 * KD + i:0 * KD + i + 1]
            lng = lambda i: COLS[:, 1 * KD + i:1 * KD + i + 1]
            lnb = lambda i: COLS[:, 2 * KD + i:2 * KD + i + 1]
            bdt = lambda i: COLS[:, 3 * KD + i:3 * KD + i + 1]
            bcv = lambda i: COLS[:, 4 * KD + i:4 * KD + i + 1]
            bwm = lambda i: COLS[:, 5 * KD + i:5 * KD + i + 1]
            onec = COLS[:, 6 * KD:6 * KD + 1]
            for nm, col in (("sl3_dssm", 0), ("sl3_lng", 1), ("sl3_lnb", 2), ("sl3_bdt", 3),
                            ("sl3_bconv", 4), ("sl3_bwm", 5)):
                k.dma("sync", COLS[:, col * KD:(col + 1) * KD], h_in[nm], writes=[B("cols3")])
            k.op("vector", lambda e: e.memset(onec, 1.0), writes=[B("cols3")])
            k.dma("sync", WCV[:], h_in["sl3_wconv"], writes=[B("wcv")])
            k.dma("gpsimd", IDB[:], h_in["eyerep"][:, 0:128], writes=[B("idb")])
            k.dma("sync", ACOL[:], h_in["sl3_alogc"], writes=[B("acol")])
            k.op("scalar", lambda e: e.activation(out=ACOL[:], in_=ACOL[:], func=AF.Exp),
                 reads=[B("acol")], writes=[B("acol")])
            k.op("vector", lambda e: e.tensor_scalar(out=ACOL[:], in0=ACOL[:], scalar1=-1.0, scalar2=None, op0=ALU.mult),
                 reads=[B("acol")], writes=[B("acol")])

            xof_chunk = xof_chunk_gen(RBo, MBo, "oRB", "oMB")

            with ES3() as s_ab:
                sbb = lambda n2, sh, dt=F32: s_ab.enter_context(nc.sbuf_tensor("ab_" + n2, list(sh), dt))
                XOB = sbb("XOB", (128, KD * T), BF16)
                with ES3() as s_a:
                    sba = lambda n2, sh, dt=F32: s_a.enter_context(nc.sbuf_tensor("a_" + n2, list(sh), dt))
                    STATR = sba("STATR", (1, 1536))
                    RB2 = sba("RB2", (128, T)); MB2 = sba("MB2", (128, T))
                    XIBT = sba("XIBT", (128, T), BF16)
                    DTIB = sba("DTIB", (64, T), BF16)
                    WXP = sba("WXP", (128, KD * 192), BF16)
                    WDT = sba("WDT", (64, D), BF16)
                    XOFT = sba("XOFT", (128, T))
                    CVP = sba("CVP", (128, T + 3))
                    DELT = sba("DELT", (128, T))
                    DTXT = sba("DTXT", (128, T))
                    k.dma("gpsimd", WXP[:], h_in["sl3_wxp"], writes=[B("wxp")])
                    k.dma("gpsimd", WDT[:], h_in["sl3_wdt"], writes=[B("wdt")])
                    def src_x(kd, c):
                        xap = X[:, kd * T + c * 512: kd * T + (c + 1) * 512]
                        slot = SQ3[:, (kd % 2) * 512:(kd % 2) * 512 + 512]
                        return xap, [B("X%d" % kd)], slot, B("sq3%d" % (kd % 2))
                    ln_stats_src(src_x, RBo, MBo, STATR, CONSTC[:, 0:1], "o")
                    k.barrier()
                    for kd in range(KD):
                        xb = xof_chunk(XOFT[:, 0:T], kd, slice(0, T))
                        k.op("vector", lambda e, kd=kd: e.tensor_copy(
                            out=XOB[:, kd * T:(kd + 1) * T], in_=XOFT[:, 0:T]),
                            reads=[xb], writes=[B("XOB%d" % kd)])
                        k.dma("sync", xof_h[:, kd * T:(kd + 1) * T], XOFT[:, 0:T],
                              reads=[xb], writes=[B("xofsp")])
                    k.barrier()
                    def src_xof(kd, c):
                        csl = slice(c * 512, (c + 1) * 512)
                        slot2 = XOFT[:, c * 512:(c + 1) * 512]
                        xb = xof_chunk(slot2, kd, csl)
                        sq = SQ3[:, (kd % 2) * 512:(kd % 2) * 512 + 512]
                        return slot2, [xb], sq, B("sq3%d" % (kd % 2))
                    ln_stats_src(src_xof, RB2, MB2, STATR, CONSTC[:, 0:1], "i")
                    k.barrier()
                    for c in range(4):
                        csl = slice(c * 512, (c + 1) * 512)
                        for kd in range(KD):
                            xb = xof_chunk(XOFT[:, 0:512], kd, csl)
                            k.op("vector", lambda e, csl=csl: e.tensor_tensor(
                                out=XOFT[:, 0:512], in0=XOFT[:, 0:512], in1=RB2[:, csl], op=ALU.mult),
                                reads=[xb, B("iRB")], writes=[B("xoft")])
                            k.op("vector", lambda e, csl=csl: e.tensor_tensor(
                                out=XOFT[:, 0:512], in0=XOFT[:, 0:512], in1=MB2[:, csl], op=ALU.subtract),
                                reads=[B("xoft"), B("iMB")], writes=[B("xoft")])
                            k.op("scalar", lambda e, kd=kd: e.activation(
                                out=XIBT[:, 0:512], in_=XOFT[:, 0:512], func=AF.Identity,
                                bias=lnb(kd), scale=lng(kd)),
                                reads=[B("xoft"), B("cols3")], writes=[B("xibt")])
                            for mi, (m0, m1, bank) in enumerate(((0, 64, 0), (64, 128, 1), (128, 192, 2))):
                                lw = WXP[:, kd * 192 + m0: kd * 192 + m1]
                                k.op("tensor", lambda e, lw=lw, kd=kd, bank=bank, m1=m1, m0=m0: e.matmul(
                                    PS[bank][0:m1 - m0, 0:512], lw, XIBT[:, 0:512],
                                    start=(kd == 0), stop=(kd == KD - 1)),
                                    reads=[B("wxp"), B("xibt")], psum_rw=PSB[bank])
                        k.op("scalar", lambda e, csl=csl: e.copy(out=DTIB[:, csl], in_=PS[0][0:64, 0:512]),
                             reads=[PSB[0]], writes=[B("dtib")])
                        free_psum(0, B("dtib"))
                        k.op("scalar", lambda e, csl=csl: e.copy(out=BRB[:, csl], in_=PS[1][0:64, 0:512]),
                             reads=[PSB[1]], writes=[B("brb")])
                        free_psum(1, B("brb"))
                        k.op("scalar", lambda e, csl=csl: e.copy(out=CRB[:, csl], in_=PS[2][0:64, 0:512]),
                             reads=[PSB[2]], writes=[B("crb")])
                        free_psum(2, B("crb"))
                    for od in range(KD):
                        for c in range(4):
                            csl = slice(c * 512, (c + 1) * 512)
                            lw = WDT[:, od * 128:(od + 1) * 128]
                            k.op("tensor", lambda e, lw=lw, csl=csl: e.matmul(
                                PS[3][:, 0:512], lw, DTIB[:, csl], start=True, stop=True),
                                reads=[B("wdt"), B("dtib")], writes=[PSB[3]])
                            k.op("scalar", lambda e, od=od, csl=csl: e.activation(
                                out=SQ3[:, 0:512], in_=PS[3][:, 0:512], func=AF.Relu, bias=bdt(od)),
                                reads=[PSB[3], B("cols3")], writes=[B("sq3c")])
                            k.op("scalar", lambda e, od=od, csl=csl: e.activation(
                                out=DELT[:, csl], in_=PS[3][:, 0:512], func=AF.Abs, bias=bdt(od)),
                                reads=[PSB[3], B("cols3")], writes=[B("delt")])
                            free_psum(3, B("delt"))
                            k.op("scalar", lambda e, csl=csl: e.activation(
                                out=DELT[:, csl], in_=DELT[:, csl], func=AF.Exp, scale=-1.0),
                                reads=[B("delt")], writes=[B("delt")])
                            k.op("scalar", lambda e, csl=csl: e.activation(
                                out=DELT[:, csl], in_=DELT[:, csl], func=AF.Ln, bias=onec),
                                reads=[B("delt"), B("cols3")], writes=[B("delt")])
                            k.op("vector", lambda e, csl=csl: e.tensor_tensor(
                                out=DELT[:, csl], in0=DELT[:, csl], in1=SQ3[:, 0:512], op=ALU.add),
                                reads=[B("delt"), B("sq3c")], writes=[B("delt")])
                        k.dma("sync", delta_h[:, od * T:(od + 1) * T], DELT[:], reads=[B("delt")], writes=[B("dspill")])
                        for c in range(4):
                            csl = slice(c * 512, (c + 1) * 512)
                            xb = xof_chunk(XOFT[:, 0:512], od, csl)
                            k.op("vector", lambda e, csl=csl: e.tensor_tensor(
                                out=XOFT[:, 0:512], in0=XOFT[:, 0:512], in1=RB2[:, csl], op=ALU.mult),
                                reads=[xb, B("iRB")], writes=[B("xoft")])
                            k.op("vector", lambda e, csl=csl: e.tensor_tensor(
                                out=XOFT[:, 0:512], in0=XOFT[:, 0:512], in1=MB2[:, csl], op=ALU.subtract),
                                reads=[B("xoft"), B("iMB")], writes=[B("xoft")])
                            k.op("scalar", lambda e, od=od: e.activation(
                                out=XOFT[:, 0:512], in_=XOFT[:, 0:512], func=AF.Identity,
                                bias=lnb(od), scale=lng(od)),
                                reads=[B("xoft"), B("cols3")], writes=[B("xoft")])
                            k.op("vector", lambda e, csl=csl: e.tensor_tensor(
                                out=DTXT[:, csl], in0=DELT[:, csl], in1=XOFT[:, 0:512], op=ALU.mult),
                                reads=[B("delt"), B("xoft")], writes=[B("dtxt")])
                        k.dma("sync", dtx_h[:, od * T:(od + 1) * T], DTXT[:], reads=[B("dtxt")], writes=[B("dxspill")])
                    k.barrier()
                    for kd in range(KD):
                        xb = xof_chunk(XOFT[:, 0:T], kd, slice(0, T))
                        k.op("vector", lambda e: e.memset(CVP[:, 0:3], 0.0), writes=[B("cvp")])
                        k.op("vector", lambda e: e.tensor_copy(out=CVP[:, 3:T + 3], in_=XOFT[:, 0:T]),
                             reads=[xb], writes=[B("cvp")])
                        k.op("vector", lambda e, kd=kd: e.tensor_scalar(
                            out=SQ3[:], in0=CVP[:, 0:T], scalar1=WCV[:, kd * 4:kd * 4 + 1], scalar2=None, op0=ALU.mult),
                            reads=[B("cvp"), B("wcv")], writes=[B("sq3c")])
                        for j in (1, 2, 3):
                            k.op("vector", lambda e, kd=kd, j=j: e.scalar_tensor_tensor(
                                out=SQ3[:], in0=CVP[:, j:T + j], scalar=WCV[:, kd * 4 + j:kd * 4 + j + 1],
                                in1=SQ3[:], op0=ALU.mult, op1=ALU.add),
                                reads=[B("cvp"), B("wcv"), B("sq3c")], writes=[B("sq3c")])
                        k.op("scalar", lambda e, kd=kd: e.activation(out=SQ3[:], in_=SQ3[:], func=AF.Silu,
                                                                     bias=bcv(kd)),
                             reads=[B("sq3c"), B("cols3")], writes=[B("sq3c")])
                        k.op("vector", lambda e, kd=kd: e.scalar_tensor_tensor(
                            out=X[:, kd * T:(kd + 1) * T], in0=SQ3[:], scalar=0.15,
                            in1=X[:, kd * T:(kd + 1) * T], op0=ALU.mult, op1=ALU.add),
                            reads=[B("sq3c"), B("X%d" % kd)], writes=[B("X%d" % kd)])
                k.barrier()
                if KPH < 2:
                    return
                with ES3() as s_b1:
                    sb1 = lambda n2, sh, dt=F32: s_b1.enter_context(nc.sbuf_tensor("b1_" + n2, list(sh), dt))
                    WWM = sb1("WWM", (128, KD * D), BF16)
                    k.dma("gpsimd", WWM[:], h_in["sl3_wwm"], writes=[B("wwm")])
                    for od in range(KD):
                        for kd in range(KD):
                            for c in range(4):
                                lw = WWM[:, kd * D + od * 128: kd * D + (od + 1) * 128]
                                rhs = XOB[:, kd * T + c * 512: kd * T + (c + 1) * 512]
                                k.op("tensor", lambda e, lw=lw, rhs=rhs, kd=kd, c=c: e.matmul(
                                    PS[c][:, 0:512], lw, rhs, start=(kd == 0), stop=(kd == KD - 1)),
                                    reads=[B("wwm"), B("XOB%d" % kd)], psum_rw=PSB[c])
                        for c in range(4):
                            k.op("scalar", lambda e, c=c, od=od: e.activation(
                                out=SQ3[:, 0:512], in_=PS[c][:, 0:512], func=AF.Identity, bias=bwm(od)),
                                reads=[PSB[c], B("cols3")], writes=[B("sq3c")])
                            free_psum(c, B("sq3c"))
                            xap = X[:, od * T + c * 512: od * T + (c + 1) * 512]
                            k.op("vector", lambda e, xap=xap: e.scalar_tensor_tensor(
                                out=xap, in0=SQ3[:, 0:512], scalar=0.2, in1=xap, op0=ALU.mult, op1=ALU.add),
                                reads=[B("sq3c"), B("X%d" % od)], writes=[B("X%d" % od)])
                k.barrier()
                if KPH < 3:
                    return
                with ES3() as s_b2:
                    sb2a = lambda n2, sh, dt=F32: s_b2.enter_context(nc.sbuf_tensor("b2_" + n2, list(sh), dt))
                    GT = sb2a("GT", (128, 4 * 4096), BF16)
                    EPIT = sb2a("EPIT", (128, 4096), BF16)
                    WDC1 = sb2a("WDC1", (128, 512), BF16)
                    WDC2 = sb2a("WDC2", (128, 4 * D), BF16)
                    CB2 = sb2a("CB2", (128, 16))
                    B2B = sb2a("B2B", (128, D))
                    DECTT = sb2a("DECTT", (128, 4096), BF16)
                    k.dma("gpsimd", EPIT[:], h_in["sl3_episT"], writes=[B("epit")])
                    k.dma("gpsimd", WDC1[:], h_in["sl3_wdc1"], writes=[B("wdc1")])
                    k.dma("gpsimd", WDC2[:], h_in["sl3_wdc2"], writes=[B("wdc2")])
                    k.dma("sync", CB2[:, 0:4], h_in["sl3_bdc1"], writes=[B("cb2")])
                    k.dma("sync", CB2[:, 4:12], h_in["sl3_bdc2"], writes=[B("cb2")])
                    k.dma("sync", B2B[:], h_in["sl3_bdc2b"], writes=[B("b2b")])
                    for m in range(4):
                        for c in range(8):
                            lw = WDC1[:, m * 128:(m + 1) * 128]
                            k.op("tensor", lambda e, lw=lw, m=m, c=c: e.matmul(
                                PS[c][:, 0:512], lw, EPIT[:, c * 512:(c + 1) * 512], start=True, stop=True),
                                reads=[B("wdc1"), B("epit")], writes=[PSB[c]])
                        for c in range(8):
                            k.op("scalar", lambda e, m=m, c=c: e.activation(
                                out=GT[:, m * 4096 + c * 512: m * 4096 + (c + 1) * 512],
                                in_=PS[c][:, 0:512], func=AF.Gelu, bias=CB2[:, m:m + 1]),
                                reads=[PSB[c], B("cb2")], writes=[B("gt")])
                            free_psum(c, B("gt"))
                    for od in range(KD):
                        for kh in range(4):
                            for c in range(8):
                                lw = WDC2[:, kh * D + od * 128: kh * D + (od + 1) * 128]
                                rhs = GT[:, kh * 4096 + c * 512: kh * 4096 + (c + 1) * 512]
                                k.op("tensor", lambda e, lw=lw, rhs=rhs, kh=kh, c=c: e.matmul(
                                    PS[c][:, 0:512], lw, rhs, start=(kh == 0), stop=(kh == 3)),
                                    reads=[B("wdc2"), B("gt")], psum_rw=PSB[c])
                        for c in range(8):
                            k.op("scalar", lambda e, od=od, c=c: e.activation(
                                out=DECTT[:, c * 512:(c + 1) * 512], in_=PS[c][:, 0:512],
                                func=AF.Identity, bias=CB2[:, 4 + od:5 + od]),
                                reads=[PSB[c], B("cb2")], writes=[B("dectt")])
                            free_psum(c, B("dectt"))
                        k.dma("sync", dect_h[:, od * 4096:(od + 1) * 4096], DECTT[:], reads=[B("dectt")], writes=[B("dectsp")])
                    for st in range(32):
                        for kh in range(4):
                            for dc in range(2):
                                lw = GT[:, kh * 4096 + st * 128:(kh * 4096 + (st + 1) * 128)]
                                rhs = WDC2[:, kh * D + dc * 512: kh * D + (dc + 1) * 512]
                                k.op("tensor", lambda e, lw=lw, rhs=rhs, kh=kh, dc=dc: e.matmul(
                                    PS[4 + dc][:, 0:512], lw, rhs, start=(kh == 0), stop=(kh == 3)),
                                    reads=[B("gt"), B("wdc2")], psum_rw=PSB[4 + dc])
                        for dc in range(2):
                            k.op("vector", lambda e, dc=dc: e.tensor_tensor(
                                out=DECTT.bitcast(F32)[:, 0:512] if False else SQ3[:, 0:512],
                                in0=PS[4 + dc][:, 0:512], in1=B2B[:, dc * 512:(dc + 1) * 512], op=ALU.add),
                                reads=[PSB[4 + dc], B("b2b")], writes=[B("sq3c")])
                            free_psum(4 + dc, B("sq3c"))
                            k.op("vector", lambda e, st=st, dc=dc: e.tensor_copy(
                                out=DECTT[:, dc * 512:(dc + 1) * 512], in_=SQ3[:, 0:512]),
                                reads=[B("sq3c")], writes=[B("dectt")])
                        k.dma("sync", decn_h[:, st * D: st * D + D], DECTT[:, 0:1024], reads=[B("dectt")], writes=[B("decnsp")])
                k.barrier()
                if KPH < 4:
                    return
                with ES3() as s_b2b:
                    sbeb = lambda n2, sh, dt=F32: s_b2b.enter_context(nc.sbuf_tensor("b2b_" + n2, list(sh), dt))
                    ES_ = sbeb("ES", (128, 4096))
                    ES2 = sbeb("ES2", (128, 4096))
                    EST2 = sbeb("EST2", (128, 32 * 256), BF16)
                    IDF = sbeb("IDF", (128, 128))
                    DECTS = sbeb("DECTS", (128, 2 * 4096), BF16)
                    DECNS = sbeb("DECNS", (128, 2 * D), BF16)
                    MXC = sbeb("MXC", (128, 4))
                    k.dma("sync", IDF[:], h_in["eyerep"][:, 0:128], writes=[B("idf")])
                    for tg in range(8):
                        for tt in range(2):
                            ti = tg * 2 + tt
                            for kd in range(KD):
                                wb = kd % 2
                                k.dma("sync", DECTS[:, wb * 4096:(wb + 1) * 4096],
                                      dect_h[:, kd * 4096:(kd + 1) * 4096], writes=[B("dects%d" % wb)])
                                lw = XOB[:, kd * T + ti * 128: kd * T + (ti + 1) * 128]
                                for sc in range(8):
                                    rhs = DECTS[:, wb * 4096 + sc * 512: wb * 4096 + (sc + 1) * 512]
                                    k.op("tensor", lambda e, lw=lw, rhs=rhs, kd=kd, sc=sc: e.matmul(
                                        PS[sc][:, 0:512], lw, rhs, start=(kd == 0), stop=(kd == KD - 1)),
                                        reads=[B("XOB%d" % kd), B("dects%d" % wb)], psum_rw=PSB[sc])
                            for sc in range(8):
                                k.op("scalar", lambda e, sc=sc: e.mul(
                                    ES_[:, sc * 512:(sc + 1) * 512], PS[sc][:, 0:512], 0.03125),
                                    reads=[PSB[sc]], writes=[B("es_")])
                                free_psum(sc, B("es_"))
                            k.op("vector", lambda e: e.tensor_reduce(
                                out=MXC[:, 0:1], in_=ES_[:], axis=mybir.AxisListType.X, op=ALU.max, negate=True),
                                reads=[B("es_")], writes=[B("mxc")])
                            k.op("scalar", lambda e: e.activation(
                                out=ES2[:], in_=ES_[:], func=AF.Exp, bias=MXC[:, 0:1], accum_out=MXC[:, 1:2]),
                                reads=[B("es_"), B("mxc")], writes=[B("es2"), B("mxc2")])
                            k.op("scalar", lambda e: e.activation(
                                out=MXC[:, 2:3], in_=MXC[:, 1:2], func=AF.Ln),
                                reads=[B("mxc2")], writes=[B("mxc3")])
                            k.op("vector", lambda e: e.tensor_tensor(
                                out=MXC[:, 2:3], in0=MXC[:, 0:1], in1=MXC[:, 2:3], op=ALU.subtract),
                                reads=[B("mxc"), B("mxc3")], writes=[B("mxc3")])
                            k.op("scalar", lambda e: e.activation(
                                out=ES_[:], in_=ES_[:], func=AF.Exp, bias=MXC[:, 2:3]),
                                reads=[B("es_"), B("mxc3")], writes=[B("es_")])

                            for st in range(32):
                                bk = 6 + (st % 2)
                                k.op("tensor", lambda e, st=st, bk=bk: e.transpose(
                                    PS[bk][:, 0:128], ES_[:, st * 128:(st + 1) * 128], IDF[:]),
                                    reads=[B("es_"), B("idf")], writes=[PSB[bk]])
                                k.op("vector", lambda e, st=st, tt=tt, bk=bk: e.tensor_copy(
                                    out=EST2[:, st * 256 + tt * 128: st * 256 + (tt + 1) * 128],
                                    in_=PS[bk][:, 0:128]),
                                    reads=[PSB[bk]], writes=[B("est2")])
                                free_psum(bk, B("est2"))
                        for st in range(32):
                            wb = st % 2
                            k.dma("sync", DECNS[:, wb * D:(wb + 1) * D], decn_h[:, st * D:(st + 1) * D],
                                  writes=[B("decns%d" % wb)])
                            for do in range(KD):
                                lw = DECNS[:, wb * D + do * 128: wb * D + (do + 1) * 128]
                                rhs = EST2[:, st * 256:(st + 1) * 256]
                                k.op("tensor", lambda e, lw=lw, rhs=rhs, st=st, do=do: e.matmul(
                                    PS[do][:, 0:256], lw, rhs, start=(st == 0), stop=(st == 31)),
                                    reads=[B("decns%d" % wb), B("est2")], psum_rw=PSB[do])
                        for do in range(KD):
                            k.op("scalar", lambda e, do=do: e.copy(out=ES2[:, 0:256], in_=PS[do][:, 0:256]),
                                 reads=[PSB[do]], writes=[B("es2")])
                            free_psum(do, B("es2"))
                            xap = X[:, do * T + tg * 256: do * T + (tg + 1) * 256]
                            k.op("vector", lambda e, xap=xap, do=do: e.scalar_tensor_tensor(
                                out=xap, in0=ES2[:, 0:256], scalar=0.06, in1=xap, op0=ALU.mult, op1=ALU.add),
                                reads=[B("es2"), B("X%d" % do)], writes=[B("X%d" % do)])
                k.barrier()
                if KPH < 5:
                    return
                with ES3() as s_b3:
                    sb3 = lambda n2, sh, dt=F32: s_b3.enter_context(nc.sbuf_tensor("b3_" + n2, list(sh), dt))
                    AOT = sb3("AOT", (128, KD * T), BF16)
                    KT = sb3("KT", (128, 2 * T), BF16)
                    VN = sb3("VN", (128, 16 * 256), BF16)
                    QT = sb3("QT", (128, T), BF16)
                    with ES3() as s_kv:
                        WKV = s_kv.enter_context(nc.sbuf_tensor("b3_WKV", [128, KD * 512], BF16))
                        k.dma("gpsimd", WKV[:, 0:KD * 256].rearrange("p (a b) -> p a b", a=KD), h_in["sl3_wk"].rearrange("p (a b) -> p a b", a=KD), writes=[B("wkv")])
                        k.dma("gpsimd", WKV[:, KD * 256:].rearrange("p (a b) -> p a b", a=KD), h_in["sl3_wv"].rearrange("p (a b) -> p a b", a=KD), writes=[B("wkv")])
                        for ot in range(2):
                            for kd in range(KD):
                                for c in range(4):
                                    lw = WKV[:, kd * 256 + ot * 128: kd * 256 + (ot + 1) * 128]
                                    rhs = XOB[:, kd * T + c * 512: kd * T + (c + 1) * 512]
                                    k.op("tensor", lambda e, lw=lw, rhs=rhs, kd=kd, c=c: e.matmul(
                                        PS[c][:, 0:512], lw, rhs, start=(kd == 0), stop=(kd == KD - 1)),
                                        reads=[B("wkv"), B("XOB%d" % kd)], psum_rw=PSB[c])
                            for c in range(4):
                                k.op("scalar", lambda e, ot=ot, c=c: e.copy(
                                    out=KT[:, ot * T + c * 512: ot * T + (c + 1) * 512], in_=PS[c][:, 0:512]),
                                    reads=[PSB[c]], writes=[B("kt")])
                                free_psum(c, B("kt"))
                        for ti in range(16):
                            for kd in range(KD):
                                lw = XOB[:, kd * T + ti * 128: kd * T + (ti + 1) * 128]
                                rhs = WKV[:, KD * 256 + kd * 256: KD * 256 + (kd + 1) * 256]
                                k.op("tensor", lambda e, lw=lw, rhs=rhs, kd=kd, ti=ti: e.matmul(
                                    PS[4 + ti % 2][:, 0:256], lw, rhs, start=(kd == 0), stop=(kd == KD - 1)),
                                    reads=[B("wkv"), B("XOB%d" % kd)], psum_rw=PSB[4 + ti % 2])
                            k.op("scalar", lambda e, ti=ti: e.copy(
                                out=VN[:, ti * 256:(ti + 1) * 256], in_=PS[4 + ti % 2][:, 0:256]),
                                reads=[PSB[4 + ti % 2]], writes=[B("vn")])
                            free_psum(4 + ti % 2, B("vn"))
                    k.barrier()
                    WQS = sb3("WQS", (128, KD * 128), BF16)
                    BIAS = sb3("BIAS", (128, 768))
                    SC3 = sb3("SC3", (128, 768))
                    SC3B = sb3("SC3B", (128, 768))
                    ATT3 = sb3("ATT3", (128, 768), BF16)
                    NMX = sb3("NMX", (128, 4))
                    KWIN = sb3("KWIN", (128, 768), BF16)
                    AVS = sb3("AVS", (64, 128), BF16)
                    IDF3 = sb3("IDF3", (128, 128))
                    KB3S = int(os.environ.get("KB3S", "6"))
                    k.dma("sync", IDF3[:], h_in["eyerep"][:, 0:128], writes=[B("idf3")])
                    for j in range(int(os.environ.get("KB3J", str(KD)))):
                        if KB3S < 2:
                            break
                        for kd in range(KD):
                            k.dma("gpsimd", WQS[:, kd * 128:(kd + 1) * 128],
                                  h_in["sl3_wq"][:, kd * D + j * 128: kd * D + (j + 1) * 128],
                                  writes=[B("wqs")])
                        for c in range(4):
                            for kd in range(KD):
                                lw = WQS[:, kd * 128:(kd + 1) * 128]
                                rhs = XOB[:, kd * T + c * 512: kd * T + (c + 1) * 512]
                                k.op("tensor", lambda e, lw=lw, rhs=rhs, kd=kd, c=c: e.matmul(
                                    PS[c][:, 0:512], lw, rhs, start=(kd == 0), stop=(kd == KD - 1)),
                                    reads=[B("wqs"), B("XOB%d" % kd)], psum_rw=PSB[c])
                        for c in range(4):
                            k.op("scalar", lambda e, c=c: e.copy(
                                out=QT[:, c * 512:(c + 1) * 512], in_=PS[c][:, 0:512]),
                                reads=[PSB[c]], writes=[B("qt")])
                            free_psum(c, B("qt"))
                        kvh = (2 * j) // 4
                        for qt_ in range(16):
                            blk = qt_ // 2
                            par = qt_ % 2
                            kstart = max(0, (blk - 1) * 256)
                            kend = min(T, (blk + 2) * 256)
                            klen = kend - kstart
                            koff = 256 - (blk * 256 - kstart)
                            ksrc = KT[(kvh % 2) * 64:(kvh % 2) * 64 + 64,
                                      (kvh // 2) * T + kstart: (kvh // 2) * T + kstart + klen]
                            k.dma("scalar", KWIN[0:64, 0:klen], ksrc, reads=[B("kt")], writes=[B("kwin")])
                            k.dma("scalar", KWIN[64:128, 0:klen], ksrc, reads=[B("kt")], writes=[B("kwin")])
                            for h2 in range(2):
                                hh = 2 * j + h2
                                k.dma("sync", BIAS[:, 0:klen],
                                      h_in["sl3_bias"][hh][par][:, koff:koff + klen], writes=[B("bias0")])
                                qsl = QT[h2 * 64:(h2 + 1) * 64, qt_ * 128:(qt_ + 1) * 128]
                                nch = (klen + 511) // 512
                                for ch in range(nch):
                                    cl = min(512, klen - ch * 512)
                                    rhs = KWIN[h2 * 64:(h2 + 1) * 64, ch * 512: ch * 512 + cl]
                                    k.op("tensor", lambda e, qsl=qsl, rhs=rhs, h2=h2, ch=ch, cl=cl: e.matmul(
                                        PS[h2 * 2 + ch][:, 0:cl], qsl, rhs, start=True, stop=True),
                                        reads=[B("qt"), B("kwin")], writes=[PSB[h2 * 2 + ch]])
                                if KB3S < 3:
                                    for ch in range(nch):
                                        free_psum(h2 * 2 + ch, PSB[h2 * 2 + ch])
                                    continue
                                for ch in range(nch):
                                    cl = min(512, klen - ch * 512)
                                    k.op("vector", lambda e, h2=h2, ch=ch, cl=cl: e.scalar_tensor_tensor(
                                        out=SC3[:, ch * 512: ch * 512 + cl], in0=PS[h2 * 2 + ch][:, 0:cl],
                                        scalar=0.125, in1=BIAS[:, ch * 512: ch * 512 + cl],
                                        op0=ALU.mult, op1=ALU.add),
                                        reads=[PSB[h2 * 2 + ch], B("bias0")], writes=[B("sc3")])
                                    free_psum(h2 * 2 + ch, B("sc3"))
                                k.op("vector", lambda e, klen=klen: e.tensor_reduce(
                                    out=NMX[:, 0:1], in_=SC3[:, 0:klen], axis=mybir.AxisListType.X,
                                    op=ALU.max, negate=True),
                                    reads=[B("sc3")], writes=[B("nmx")])
                                k.op("scalar", lambda e, klen=klen: e.activation(
                                    out=SC3B[:, 0:klen], in_=SC3[:, 0:klen], func=AF.Exp,
                                    bias=NMX[:, 0:1], accum_out=NMX[:, 1:2]),
                                    reads=[B("sc3"), B("nmx")], writes=[B("sc3b"), B("nmx2")])
                                k.op("scalar", lambda e: e.activation(
                                    out=NMX[:, 2:3], in_=NMX[:, 1:2], func=AF.Ln),
                                    reads=[B("nmx2")], writes=[B("nmx3")])
                                k.op("vector", lambda e: e.tensor_tensor(
                                    out=NMX[:, 2:3], in0=NMX[:, 0:1], in1=NMX[:, 2:3], op=ALU.subtract),
                                    reads=[B("nmx"), B("nmx3")], writes=[B("nmx3")])
                                k.op("scalar", lambda e, klen=klen: e.activation(
                                    out=SC3[:, 0:klen], in_=SC3[:, 0:klen], func=AF.Exp, bias=NMX[:, 2:3]),
                                    reads=[B("sc3"), B("nmx3")], writes=[B("sc3")])
                                if KB3S < 4:
                                    continue
                                nkt = klen // 128
                                for kt_ in range(nkt):
                                    bk = 5 + (kt_ % 2)
                                    k.op("tensor", lambda e, kt_=kt_, bk=bk: e.transpose(
                                        PS[bk][:, 0:128], SC3[:, kt_ * 128:(kt_ + 1) * 128], IDF3[:]),
                                        reads=[B("sc3"), B("idf3")], writes=[PSB[bk]])
                                    k.op("vector", lambda e, kt_=kt_, bk=bk: e.tensor_copy(
                                        out=ATT3[:, kt_ * 128:(kt_ + 1) * 128], in_=PS[bk][:, 0:128]),
                                        reads=[PSB[bk]], writes=[B("att3")])
                                    free_psum(bk, B("att3"))
                                if KB3S < 5:
                                    continue
                                for kt_ in range(nkt):
                                    tkt = kstart // 128 + kt_
                                    lw = VN[:, tkt * 256 + kvh * 64: tkt * 256 + kvh * 64 + 64]
                                    rhs = ATT3[:, kt_ * 128:(kt_ + 1) * 128]
                                    k.op("tensor", lambda e, lw=lw, rhs=rhs, kt_=kt_, nkt=nkt: e.matmul(
                                        PS[7][0:64, 0:128], lw, rhs,
                                        start=(kt_ == 0), stop=(kt_ == nkt - 1)),
                                        reads=[B("vn"), B("att3")], psum_rw=PSB[7])
                                k.op("scalar", lambda e, h2=h2: e.copy(out=AVS[0:64, 0:128], in_=PS[7][0:64, 0:128]),
                                     reads=[PSB[7]], writes=[B("avs")])
                                free_psum(7, B("avs"))
                                dsta = AOT[h2 * 64:(h2 + 1) * 64, j * T + qt_ * 128: j * T + (qt_ + 1) * 128]
                                k.dma("scalar", dsta, AVS[0:64, 0:128],
                                      reads=[B("avs")], writes=[B("aot%d" % j)])
                    WOS = WQS
                    for od in range(KD if KB3S >= 6 else 0):
                        for ko in range(KD):
                            k.dma("gpsimd", WOS[:, ko * 128:(ko + 1) * 128],
                                  h_in["sl3_wo"][:, ko * D + od * 128: ko * D + (od + 1) * 128],
                                  writes=[B("wqs")])
                        for ko in range(KD):
                            for c in range(4):
                                lw = WOS[:, ko * 128:(ko + 1) * 128]
                                if os.environ.get("KWOX"):
                                    rhs = XOB[:, ko * T + c * 512: ko * T + (c + 1) * 512]
                                    rb_ = B("XOB%d" % ko)
                                else:
                                    rhs = AOT[:, ko * T + c * 512: ko * T + (c + 1) * 512]
                                    rb_ = B("aot%d" % ko)
                                k.op("tensor", lambda e, lw=lw, rhs=rhs, ko=ko, c=c: e.matmul(
                                    PS[c][:, 0:512], lw, rhs, start=(ko == 0), stop=(ko == KD - 1)),
                                    reads=[B("wqs"), rb_], psum_rw=PSB[c])
                        for c in range(4):
                            k.op("scalar", lambda e, c=c: e.copy(out=SQ3[:, 0:512], in_=PS[c][:, 0:512]),
                                 reads=[PSB[c]], writes=[B("sq3c")])
                            free_psum(c, B("sq3c"))
                            xap = X[:, od * T + c * 512: od * T + (c + 1) * 512]
                            k.op("vector", lambda e, xap=xap: e.scalar_tensor_tensor(
                                out=xap, in0=SQ3[:, 0:512], scalar=0.2, in1=xap, op0=ALU.mult, op1=ALU.add),
                                reads=[B("sq3c"), B("X%d" % od)], writes=[B("X%d" % od)])
                k.barrier()
            k.barrier()
            if KPH < 6:
                return
            with ES3() as s_c:
                sbc = lambda n2, sh, dt=F32: s_c.enter_context(nc.sbuf_tensor("c_" + n2, list(sh), dt))
                YSB = sbc("YSB", (128, KD * T), BF16)
                s_cw = ES3()
                sbcw = lambda n2, sh, dt=F32: s_cw.enter_context(nc.sbuf_tensor("cw_" + n2, list(sh), dt))
                CARR = sbcw("CARR", (128, 512))
                DELQ = sbcw("DELQ", (128, 4 * 512))
                DTXQ = sbcw("DTXQ", (128, 4 * 512))
                BRB0 = sbcw("BRB0", (1, 2 * 512)); CRB0 = sbcw("CRB0", (1, 2 * 512))
                AH = sbcw("AH", (128, 512)); UH = sbcw("UH", (128, 512)); HH = sbcw("HH", (128, 512))
                YTB = sbcw("YTB", (128, 512), BF16)
                BBCS = sbcw("BBCS", (128, 2 * 512))
                k.op("vector", lambda e: e.memset(CARR[:], 0.0), writes=[B("carr")])
                for kh in range(2):
                    for tq in range(4):
                        tsl = slice(tq * 512, (tq + 1) * 512)
                        for ki in range(4):
                            kg = kh * 4 + ki
                            k.dma("sync", DELQ[:, ki * 512:(ki + 1) * 512],
                                  delta_h[:, kg * T:(kg + 1) * T][:, tsl], writes=[B("delq")])
                            k.dma("sync", DTXQ[:, ki * 512:(ki + 1) * 512],
                                  dtx_h[:, kg * T:(kg + 1) * T][:, tsl], writes=[B("dtxq")])
                        for n in range(64):
                            pp = n % 2
                            bkB = 4 + 2 * pp
                            bkC = 5 + 2 * pp
                            k.dma("scalar", BRB0[0:1, pp * 512:(pp + 1) * 512], BRB[n:n + 1, tsl],
                                  reads=[B("brb")], writes=[B("brb0%d" % pp)])
                            k.dma("scalar", CRB0[0:1, pp * 512:(pp + 1) * 512], CRB[n:n + 1, tsl],
                                  reads=[B("crb")], writes=[B("crb0%d" % pp)])
                            k.op("tensor", lambda e, pp=pp, bkB=bkB: e.matmul(
                                PS[bkB][:, 0:512], ONES1[0:1, :], BRB0[0:1, pp * 512:(pp + 1) * 512], start=True, stop=True),
                                reads=[B("ones1"), B("brb0%d" % pp)], writes=[PSB[bkB]])
                            k.op("scalar", lambda e, pp=pp, bkB=bkB: e.copy(
                                out=BBCS[:, pp * 512:(pp + 1) * 512], in_=PS[bkB][:, 0:512]),
                                reads=[PSB[bkB]], writes=[B("bbcs%d" % pp)])
                            free_psum(bkB, B("bbcs%d" % pp))
                            k.op("tensor", lambda e, pp=pp, bkC=bkC: e.matmul(
                                PS[bkC][:, 0:512], ONES1[0:1, :], CRB0[0:1, pp * 512:(pp + 1) * 512], start=True, stop=True),
                                reads=[B("ones1"), B("crb0%d" % pp)], writes=[PSB[bkC]])
                            for ki in range(4):
                                kg = kh * 4 + ki
                                ccol = kg * 64 + n
                                k.op("scalar", lambda e, ki=ki, kg=kg, n=n: e.activation(
                                    out=AH[:], in_=DELQ[:, ki * 512:(ki + 1) * 512], func=AF.Exp,
                                    scale=ACOL[:, kg * 64 + n: kg * 64 + n + 1]),
                                    reads=[B("delq"), B("acol")], writes=[B("ah")])
                                k.op("gpsimd", lambda e, ki=ki, pp=pp: e.tensor_tensor(
                                    out=UH[:], in0=DTXQ[:, ki * 512:(ki + 1) * 512],
                                    in1=BBCS[:, pp * 512:(pp + 1) * 512], op=ALU.mult),
                                    reads=[B("dtxq"), B("bbcs%d" % pp)], writes=[B("uh")])
                                k.op("vector", lambda e, ccol=ccol: e.tensor_tensor_scan(
                                    HH[:], AH[:], UH[:], CARR[:, ccol:ccol + 1], ALU.mult, ALU.add),
                                    reads=[B("ah"), B("uh"), B("carr")], writes=[B("hh")])
                                k.op("vector", lambda e, ccol=ccol: e.tensor_copy(
                                    out=CARR[:, ccol:ccol + 1], in_=HH[:, 511:512]),
                                    reads=[B("hh")], writes=[B("carr")])
                                k.op("vector", lambda e, bkC=bkC: e.tensor_tensor(
                                    out=YTB[:], in0=HH[:], in1=PS[bkC][:, 0:512], op=ALU.mult),
                                    reads=[B("hh"), PSB[bkC]], writes=[B("ytb")])
                                k.op("tensor", lambda e, ki=ki, n=n: e.matmul(
                                    PS[ki][:, 0:512], IDB[:], YTB[:], start=(n == 0), stop=(n == 63)),
                                    reads=[B("idb"), B("ytb")], psum_rw=PSB[ki])
                        for ki in range(4):
                            kg = kh * 4 + ki
                            k.op("scalar", lambda e, kg=kg, tsl=tsl, ki=ki: e.copy(
                                out=YSB[:, kg * T:(kg + 1) * T][:, tsl], in_=PS[ki][:, 0:512]),
                                reads=[PSB[ki]], writes=[B("ysb%d" % kg)])
                            free_psum(ki, B("ysb%d" % kg))
                s_cw.close()
                if os.environ.get("KDBG4"):
                    oys = nc.dram_tensor("dbg_ys", [128, KD * T], BF16, kind="ExternalOutput").ap()
                    k.dma("sync", oys, YSB[:], reads=[B("ysb%d" % g2) for g2 in range(KD)], writes=[B("dbg")])
                k.barrier()
                with ES3() as s_d:
                    sbd = lambda n2, sh, dt=F32: s_d.enter_context(nc.sbuf_tensor("d_" + n2, list(sh), dt))
                    SB_ = sbd("SB", (128, KD * T), BF16)
                    WSS = sbd("WSS", (128, KD * D), BF16)
                    DSQ = sbd("DSQ", (128, T))
                    XOFD = sbd("XOFD", (128, T))
                    k.dma("gpsimd", WSS[:], h_in["sl3_wssm"], writes=[B("wss")])
                    for kd in range(KD):
                        k.dma("sync", XOFD[:], xof_h[:, kd * T:(kd + 1) * T],
                              reads=[B("xofsp")], writes=[B("xofd")])
                        k.op("vector", lambda e, kd=kd: e.tensor_scalar(
                            out=SQ3[:], in0=XOFD[:], scalar1=dsc(kd), scalar2=None, op0=ALU.mult),
                            reads=[B("xofd"), B("cols3")], writes=[B("sq3c")])
                        k.op("vector", lambda e, kd=kd: e.tensor_copy(
                            out=DSQ[:], in_=YSB[:, kd * T:(kd + 1) * T]),
                            reads=[B("ysb%d" % kd)], writes=[B("dsq")])
                        k.op("vector", lambda e, kd=kd: e.tensor_tensor(
                            out=SB_[:, kd * T:(kd + 1) * T], in0=SQ3[:], in1=DSQ[:], op=ALU.add),
                            reads=[B("sq3c"), B("dsq")], writes=[B("sb%d" % kd)])
                        k.op("vector", lambda e, kd=kd: e.tensor_tensor(
                            out=X[:, kd * T:(kd + 1) * T], in0=X[:, kd * T:(kd + 1) * T], in1=XOFD[:], op=ALU.add),
                            reads=[B("xofd"), B("X%d" % kd)], writes=[B("X%d" % kd)])
                    if os.environ.get("KDBG4"):
                        osb = nc.dram_tensor("dbg_sb", [128, KD * T], BF16, kind="ExternalOutput").ap()
                        k.dma("sync", osb, SB_[:], reads=[B("sb%d" % g2) for g2 in range(KD)], writes=[B("dbg")])
                    for od in range(KD):
                        for kd in range(KD):
                            for c in range(4):
                                lw = WSS[:, kd * D + od * 128: kd * D + (od + 1) * 128]
                                rhs = SB_[:, kd * T + c * 512: kd * T + (c + 1) * 512]
                                k.op("tensor", lambda e, lw=lw, rhs=rhs, kd=kd, c=c: e.matmul(
                                    PS[c][:, 0:512], lw, rhs, start=(kd == 0), stop=(kd == KD - 1)),
                                    reads=[B("wss"), B("sb%d" % kd)], psum_rw=PSB[c])
                        for c in range(4):
                            k.op("scalar", lambda e, c=c: e.copy(out=SQ3[:, 0:512], in_=PS[c][:, 0:512]),
                                 reads=[PSB[c]], writes=[B("sq3c")])
                            free_psum(c, B("sq3c"))
                            xap = X[:, od * T + c * 512: od * T + (c + 1) * 512]
                            k.op("vector", lambda e, xap=xap: e.scalar_tensor_tensor(
                                out=xap, in0=SQ3[:, 0:512], scalar=0.45, in1=xap, op0=ALU.mult, op1=ALU.add),
                                reads=[B("sq3c"), B("X%d" % od)], writes=[B("X%d" % od)])


        qk_h = nc.dram_tensor("sl5_qk_h", [128, 4 * KD * T], BF16,
                              kind="ExternalOutput" if os.environ.get("KDBG5") else "Internal").ap()
        qkf_h = nc.dram_tensor("sl5_qkf_h", [128, 16 * T], F32, kind="Internal").ap()

        def sublayer5(sb, ps):
            from contextlib import ExitStack as ES5
            VN5 = sb("VN5", (128, 16 * D), BF16)
            with ES5() as s_bld:
                sbb = lambda n2, sh, dt=F32: s_bld.enter_context(nc.sbuf_tensor("s5b_" + n2, list(sh), dt))
                RB = sbb("RB", (128, T)); MB = sbb("MB", (128, T))
                SQ = sbb("SQ", (128, 1024)); STATR = sbb("STATR", (1, 1536))
                XNB5 = sbb("XNB5", (128, KD * T), BF16)
                QR = sbb("QR", (128, T), BF16)
                s_qk = ES5()
                sbq = lambda n2, sh, dt=F32: s_qk.enter_context(nc.sbuf_tensor("s5q_" + n2, list(sh), dt))
                C32 = sbq("C32", (128, T)); S32 = sbq("S32", (128, T))
                WQK = sbq("WQK", (128, KD * 128), BF16)
                T1 = sbq("T1", (128, T)); T2 = sbq("T2", (128, T))
                ln_stats(RB, MB, SQ, STATR, CONSTC[:, 0:1])
                k.barrier()
                for kd in range(KD):
                    for c in range(4):
                        make_xnb(RB, MB, SQ, XNB5, "X5%d" % kd, kd, slice(c * 512, (c + 1) * 512),
                                 slice(kd * T + c * 512, kd * T + (c + 1) * 512))
                TH = sbq("TH", (128, 2))
                k.dma("sync", TH[:, 0:1], h_in["sl5_thhi"], writes=[B("th")])
                k.dma("sync", TH[:, 1:2], h_in["sl5_thlo"], writes=[B("th")])
                for dst, dstb, shn, sln in ((S32, "c_s32", "sl5_shi", "sl5_slo"),
                                            (C32, "c_c32", "sl5_shic", "sl5_sloc")):
                    k.dma("sync", T1[:], h_in["sl5_m128"], writes=[B("t1")])
                    k.dma("sync", T2[:], h_in[shn], writes=[B("t2")])
                    k.op("vector", lambda e, dst=dst: e.tensor_scalar(
                        out=dst[:], in0=T1[:], scalar1=TH[:, 0:1], scalar2=None, op0=ALU.mult),
                        reads=[B("t1"), B("th")], writes=[B(dstb)])
                    k.op("vector", lambda e, dst=dst: e.tensor_tensor(out=dst[:], in0=dst[:], in1=T2[:], op=ALU.add),
                         reads=[B(dstb), B("t2")], writes=[B(dstb)])
                    k.op("vector", lambda e: e.tensor_scalar(
                        out=T2[:], in0=T1[:], scalar1=TH[:, 1:2], scalar2=None, op0=ALU.mult),
                        reads=[B("t1"), B("th")], writes=[B("t2")])
                    k.op("vector", lambda e, dst=dst: e.tensor_tensor(out=dst[:], in0=dst[:], in1=T2[:], op=ALU.add),
                         reads=[B(dstb), B("t2")], writes=[B(dstb)])
                    k.dma("sync", T2[:], h_in[sln], writes=[B("t2")])
                    k.op("vector", lambda e, dst=dst: e.tensor_tensor(out=dst[:], in0=dst[:], in1=T2[:], op=ALU.add),
                         reads=[B(dstb), B("t2")], writes=[B(dstb)])
                    k.op("vector", lambda e, dst=dst: e.tensor_scalar(
                        out=dst[:], in0=dst[:], scalar1=3.14159265, scalar2=-3.14159265,
                        op0=ALU.min, op1=ALU.max),
                        reads=[B(dstb)], writes=[B(dstb)])
                    k.op("scalar", lambda e, dst=dst: e.activation(out=dst[:], in_=dst[:], func=AF.Sin),
                         reads=[B(dstb)], writes=[B(dstb)])
                for ot in range(16):
                    for kd in range(KD):
                        wsl = slice(kd * 3072 + ot * 128, kd * 3072 + (ot + 1) * 128)
                        k.dma("gpsimd", WQK[:, kd * 128:(kd + 1) * 128], h_in["sl5_wqkvT"][:, wsl],
                              writes=[B("wqk")])
                    for c in range(4):
                        for kd in range(KD):
                            lw = WQK[:, kd * 128:(kd + 1) * 128]
                            rhs = XNB5[:, kd * T + c * 512: kd * T + (c + 1) * 512]
                            k.op("tensor", lambda e, lw=lw, rhs=rhs, kd=kd, c=c: e.matmul(
                                PS[c][:, 0:512], lw, rhs, start=(kd == 0), stop=(kd == KD - 1)),
                                reads=[B("wqk"), B("X5%d" % kd)], psum_rw=PSB[c])
                    for c in range(4):
                        k.op("scalar", lambda e, c=c: e.copy(out=T1[:, c * 512:(c + 1) * 512], in_=PS[c][:, 0:512]),
                             reads=[PSB[c]], writes=[B("t1")])
                        free_psum(c, B("t1"))
                    k.op("vector", lambda e: e.tensor_copy(out=QR[:], in_=T1[:]), reads=[B("t1")], writes=[B("qr")])
                    blk = 2 + (ot // 8)
                    qi = ot % 8
                    k.dma("sync", qk_h[:, (blk * KD + qi) * T:(blk * KD + qi + 1) * T], QR[:],
                          reads=[B("qr")], writes=[B("qkspill")])
                    k.dma("sync", qkf_h[:, ot * T:(ot + 1) * T], T1[:], reads=[B("t1")], writes=[B("qkfsp")])
                EVF = T1; ODF = T2
                RT = sbq("RT", (128, T))
                for isk in range(2):
                    for i in range(4):
                        k.dma("sync", EVF[:], qkf_h[:, (isk * 8 + i) * T:(isk * 8 + i + 1) * T], writes=[B("t1")])
                        k.dma("sync", ODF[:], qkf_h[:, (isk * 8 + i + 4) * T:(isk * 8 + i + 5) * T], writes=[B("t2")])
                        k.op("vector", lambda e: e.tensor_tensor(out=RT[:], in0=EVF[:], in1=C32[:], op=ALU.mult),
                             reads=[B("t1"), B("c_c32")], writes=[B("rt")])
                        for hf in range(2):
                            hsl = slice(hf * 1024, (hf + 1) * 1024)
                            k.op("vector", lambda e, hsl=hsl: e.tensor_tensor(
                                out=SQ[:, 0:1024], in0=ODF[:, hsl], in1=S32[:, hsl], op=ALU.mult),
                                reads=[B("t2"), B("c_s32")], writes=[B("sqx")])
                            k.op("vector", lambda e, hsl=hsl: e.tensor_tensor(
                                out=RT[:, hsl], in0=RT[:, hsl], in1=SQ[:, 0:1024], op=ALU.add),
                                reads=[B("rt"), B("sqx")], writes=[B("rt")])
                        k.op("vector", lambda e: e.tensor_copy(out=QR[:], in_=RT[:]), reads=[B("rt")], writes=[B("qr")])
                        k.dma("sync", qk_h[:, (isk * KD + i) * T:(isk * KD + i + 1) * T], QR[:],
                              reads=[B("qr")], writes=[B("qkspill")])
                        k.op("vector", lambda e: e.tensor_tensor(out=RT[:], in0=ODF[:], in1=C32[:], op=ALU.mult),
                             reads=[B("t2"), B("c_c32")], writes=[B("rt")])
                        for hf in range(2):
                            hsl = slice(hf * 1024, (hf + 1) * 1024)
                            k.op("vector", lambda e, hsl=hsl: e.tensor_tensor(
                                out=SQ[:, 0:1024], in0=EVF[:, hsl], in1=S32[:, hsl], op=ALU.mult),
                                reads=[B("t1"), B("c_s32")], writes=[B("sqx")])
                            k.op("vector", lambda e, hsl=hsl: e.tensor_tensor(
                                out=RT[:, hsl], in0=RT[:, hsl], in1=SQ[:, 0:1024], op=ALU.subtract),
                                reads=[B("rt"), B("sqx")], writes=[B("rt")])
                        k.op("vector", lambda e: e.tensor_copy(out=QR[:], in_=RT[:]), reads=[B("rt")], writes=[B("qr")])
                        k.dma("sync", qk_h[:, (isk * KD + i + 4) * T:(isk * KD + i + 5) * T], QR[:],
                              reads=[B("qr")], writes=[B("qkspill")])
                s_qk.close()
                k.barrier()
                WV58 = sbb("WV58", (128, KD * 1024), BF16)
                for kd in range(KD):
                    k.dma("gpsimd", WV58[:, kd * 1024:(kd + 1) * 1024],
                          h_in["sl5_wqkvT"][:, kd * 3072 + 2048: kd * 3072 + 3072], writes=[B("wv5")])
                for ti in range(16):
                    for kd in range(KD):
                        lw = XNB5[:, kd * T + ti * 128: kd * T + (ti + 1) * 128]
                        for dc in range(2):
                            k.op("tensor", lambda e, lw=lw, dc=dc, kd=kd, ti=ti: e.matmul(
                                PS[4 + (ti % 2) * 2 + dc][:, 0:512], lw,
                                WV58[:, kd * 1024 + dc * 512: kd * 1024 + (dc + 1) * 512],
                                start=(kd == 0), stop=(kd == KD - 1)),
                                reads=[B("X5%d" % kd), B("wv5")], psum_rw=PSB[4 + (ti % 2) * 2 + dc])
                    for dc in range(2):
                        k.op("scalar", lambda e, ti=ti, dc=dc: e.copy(
                            out=VN5[:, ti * D + dc * 512: ti * D + (dc + 1) * 512],
                            in_=PS[4 + (ti % 2) * 2 + dc][:, 0:512]),
                            reads=[PSB[4 + (ti % 2) * 2 + dc]], writes=[B("vn5")])
                        free_psum(4 + (ti % 2) * 2 + dc, B("vn5"))
            k.barrier()
            with ES5() as s_att:
                sba = lambda n2, sh, dt=F32: s_att.enter_context(nc.sbuf_tensor("s5a_" + n2, list(sh), dt))
                AOT5 = sba("AOT5", (128, KD * T), BF16)
                QH = sba("QH", (128, 2 * T), BF16)
                KH = sba("KH", (128, 2 * T), BF16)
                SC5 = sba("SC5", (128, T))
                EXD = sba("EXD", (128, 512))
                ATT5 = sba("ATT5", (128, T), BF16)
                TRI = sba("TRI", (128, 128))
                AVS5 = sba("AVS5", (64, 128), BF16)
                DC = sba("DC", (128, 8))
                SQ5 = sba("SQ5", (128, 512))
                IDF5 = sba("IDF5", (128, 128))
                k.dma("sync", TRI[:], h_in["sl5_tri"], writes=[B("tri")])
                k.dma("sync", IDF5[:], h_in["eyerep"][:, 0:128], writes=[B("idf5")])
                for h in range(16):
                    hb = h % 2
                    qsl = QH[:, hb * T:(hb + 1) * T]
                    ksl = KH[:, hb * T:(hb + 1) * T]
                    for dst, dstb, blk0 in ((qsl, "qh%d" % hb, 0), (ksl, "kh%d" % hb, 1)):
                        for part, src_tile in ((0, blk0 * KD + h // 4), (32, blk0 * KD + 4 + h // 4),
                                               (64, (2 + blk0) * KD + h // 4), (96, (2 + blk0) * KD + 4 + h // 4)):
                            k.dma("scalar", dst[part:part + 32, :],
                                  qk_h[(h % 4) * 32:(h % 4) * 32 + 32, src_tile * T:(src_tile + 1) * T],
                                  writes=[B(dstb)])
                    for qt in range(16):
                        nch = (qt * 128 + 128 + 511) // 512
                        for ch in range(nch):
                            cl = min(512, qt * 128 + 128 - ch * 512)
                            k.op("tensor", lambda e, qsl=qsl, ksl=ksl, qt=qt, ch=ch, cl=cl: e.matmul(
                                PS[ch][:, 0:cl], qsl[0:64, qt * 128:(qt + 1) * 128],
                                ksl[0:64, ch * 512: ch * 512 + cl], start=True, stop=True),
                                reads=[B("qh%d" % hb), B("kh%d" % hb)], writes=[PSB[ch]])
                        for c in range(4):
                            k.op("tensor", lambda e, qsl=qsl, ksl=ksl, qt=qt, c=c: e.matmul(
                                PS[4 + c][:, 0:512], qsl[64:128, qt * 128:(qt + 1) * 128],
                                ksl[64:128, c * 512:(c + 1) * 512], start=True, stop=True),
                                reads=[B("qh%d" % hb), B("kh%d" % hb)], writes=[PSB[4 + c]])
                        for c in range(4):
                            k.op("scalar", lambda e, c=c: e.activation(
                                out=EXD[:], in_=PS[4 + c][:, 0:512], func=AF.Exp, scale=0.125,
                                accum_out=DC[:, c:c + 1]),
                                reads=[PSB[4 + c]], writes=[B("exd"), B("dc")])
                            free_psum(4 + c, B("dc"))
                        k.op("vector", lambda e: e.tensor_tensor(out=DC[:, 4:5], in0=DC[:, 0:1], in1=DC[:, 1:2], op=ALU.add),
                             sreads=[B("dc")], writes=[B("dc")])
                        k.op("vector", lambda e: e.tensor_tensor(out=DC[:, 5:6], in0=DC[:, 2:3], in1=DC[:, 3:4], op=ALU.add),
                             sreads=[B("dc")], writes=[B("dc")])
                        k.op("vector", lambda e: e.tensor_tensor(out=DC[:, 4:5], in0=DC[:, 4:5], in1=DC[:, 5:6], op=ALU.add),
                             sreads=[B("dc")], writes=[B("dc")])
                        k.op("scalar", lambda e: e.activation(out=DC[:, 6:7], in_=DC[:, 4:5], func=AF.Ln),
                             sreads=[B("dc")], writes=[B("dc")])
                        k.op("vector", lambda e: e.tensor_scalar(out=DC[:, 6:7], in0=DC[:, 6:7], scalar1=-1.0,
                                                                 scalar2=None, op0=ALU.mult),
                             sreads=[B("dc")], writes=[B("dc")])
                        for ch in range(nch):
                            cl = min(512, qt * 128 + 128 - ch * 512)
                            k.op("scalar", lambda e, ch=ch, cl=cl: e.activation(
                                out=SC5[:, ch * 512: ch * 512 + cl], in_=PS[ch][:, 0:cl], func=AF.Exp,
                                scale=0.125, bias=DC[:, 6:7]),
                                reads=[PSB[ch], B("dc")], writes=[B("sc5")])
                            free_psum(ch, B("sc5"))
                        k.op("vector", lambda e, qt=qt: e.tensor_tensor(
                            out=SC5[:, qt * 128:(qt + 1) * 128], in0=SC5[:, qt * 128:(qt + 1) * 128],
                            in1=TRI[:], op=ALU.mult),
                            reads=[B("sc5"), B("tri")], writes=[B("sc5")])
                        for tk in range(qt + 1):
                            bk = 4 + 2 * (tk % 2)
                            k.op("tensor", lambda e, tk=tk, bk=bk: e.transpose(
                                PS[bk][:, 0:128], SC5[:, tk * 128:(tk + 1) * 128], IDF5[:]),
                                reads=[B("sc5"), B("idf5")], writes=[PSB[bk]])
                            k.op("vector", lambda e, tk=tk, bk=bk: e.tensor_copy(
                                out=ATT5[:, tk * 128:(tk + 1) * 128], in_=PS[bk][:, 0:128]),
                                reads=[PSB[bk]], writes=[B("att5")])
                            free_psum(bk, B("att5"))
                        for tk in range(qt + 1):
                            lw = VN5[:, tk * D + h * 64: tk * D + (h + 1) * 64]
                            k.op("tensor", lambda e, lw=lw, tk=tk, qt=qt: e.matmul(
                                PS[5][0:64, 0:128], lw, ATT5[:, tk * 128:(tk + 1) * 128],
                                start=(tk == 0), stop=(tk == qt)),
                                reads=[B("vn5"), B("att5")], psum_rw=PSB[5])
                        k.op("scalar", lambda e: e.copy(out=AVS5[0:64, 0:128], in_=PS[5][0:64, 0:128]),
                             reads=[PSB[5]], writes=[B("avs5")])
                        free_psum(5, B("avs5"))
                        k.dma("scalar", AOT5[(h % 2) * 64:(h % 2) * 64 + 64,
                                             (h // 2) * T + qt * 128:(h // 2) * T + (qt + 1) * 128],
                              AVS5[0:64, 0:128], reads=[B("avs5")], writes=[B("aot5_%d" % (h // 2))])
                if os.environ.get("KDBG5"):
                    oao = nc.dram_tensor("dbg_ao", [128, KD * T], BF16, kind="ExternalOutput").ap()
                    k.dma("sync", oao, AOT5[:], reads=[B("aot5_%d" % g2) for g2 in range(KD)], writes=[B("dbg")])
                WOS5 = sba("WOS5", (128, KD * 128), BF16)
                BO5 = sba("BO5", (128, KD))
                k.dma("sync", BO5[:], h_in["sl5_bout"], writes=[B("bo5")])
                for od in range(KD):
                    for ko in range(KD):
                        k.dma("gpsimd", WOS5[:, ko * 128:(ko + 1) * 128],
                              h_in["sl5_woutT"][:, ko * D + od * 128: ko * D + (od + 1) * 128],
                              writes=[B("wos5")])
                    for ko in range(KD):
                        for c in range(4):
                            lw = WOS5[:, ko * 128:(ko + 1) * 128]
                            rhs = AOT5[:, ko * T + c * 512: ko * T + (c + 1) * 512]
                            k.op("tensor", lambda e, lw=lw, rhs=rhs, ko=ko, c=c: e.matmul(
                                PS[c][:, 0:512], lw, rhs, start=(ko == 0), stop=(ko == KD - 1)),
                                reads=[B("wos5"), B("aot5_%d" % ko)], psum_rw=PSB[c])
                    for c in range(4):
                        k.op("scalar", lambda e, c=c, od=od: e.activation(
                            out=SQ5[:, 0:512], in_=PS[c][:, 0:512], func=AF.Identity, bias=BO5[:, od:od + 1]),
                            reads=[PSB[c], B("bo5")], writes=[B("sq5")])
                        free_psum(c, B("sq5"))
                        xap = X[:, od * T + c * 512: od * T + (c + 1) * 512]
                        k.op("vector", lambda e, xap=xap, od=od: e.tensor_tensor(
                            out=xap, in0=SQ5[:, 0:512], in1=xap, op=ALU.add),
                            reads=[B("sq5"), B("X%d" % od)], writes=[B("X%d" % od)])


        moe_call = [0]
        def sublayer_moe(sb, ps):
            from contextlib import ExitStack as ES6
            moe_call[0] += 1
            mpre = "m%d" % moe_call[0]
            TNB = sb("TNB", (128, KD * T), BF16)
            CT6 = sb("CT6", (8, T))
            SQ6 = sb("SQ6", (128, 512))
            NWC = sb("NWC", (128, 2 * KD))
            k.dma("sync", NWC[:, 0:KD], h_in["sl6_nw"], writes=[B("nwc")])
            with ES6() as s_n:
                sbn = lambda n2, sh, dt=F32: s_n.enter_context(nc.sbuf_tensor(mpre + "s6n_" + n2, list(sh), dt))
                RB = sbn("RB", (128, T)); MB = sbn("MB", (128, T))
                SQ = sbn("SQ", (128, 1024)); STATR = sbn("STATR", (1, 2048))
                RR = sbn("RR", (128, T))
                XOF6 = sbn("XOF6", (128, T))
                GWT = sbn("GWT", (128, KD * 8), BF16)
                CMB = sbn("CMB", (128, 16 * 8))
                MC6 = sbn("MC6", (128, 8))
                IDF6 = sbn("IDF6", (128, 128))
                k.dma("gpsimd", GWT[:], h_in["sl6_gwT"], writes=[B("gwt")])
                k.dma("sync", IDF6[:], h_in["eyerep"][:, 0:128], writes=[B("idf6")])
                ln_stats(RB, MB, SQ, STATR, CONSTC[:, 0:1])
                k.barrier()
                xof_chunk = xof_chunk_gen(RB, MB, "RB", "MB")
                mu2 = STATR[0:1, 1536:2048]
                for c in range(4):
                    csl = slice(c * 512, (c + 1) * 512)
                    for kd in range(KD):
                        xb = xof_chunk(XOF6[:, 0:512], kd, csl)
                        k.op("scalar", lambda e: e.activation(out=SQ[:, 0:512], in_=XOF6[:, 0:512], func=AF.Square),
                             reads=[xb], writes=[B("sq6s")])
                        k.op("tensor", lambda e, kd=kd: e.matmul(
                            PS[0][0:1, 0:512], ONESC[:, 0:1], SQ[:, 0:512], start=(kd == 0), stop=(kd == KD - 1)),
                            reads=[B("onesc"), B("sq6s")], psum_rw=PSB[0])
                    k.op("scalar", lambda e: e.mul(mu2, PS[0][0:1, 0:512], 1.0 / D),
                         reads=[PSB[0]], writes=[B("mu2")])
                    free_psum(0, B("mu2"))
                    k.op("scalar", lambda e: e.activation(out=mu2, in_=mu2, func=AF.Sqrt, bias=CONSTC[0:1, 2:3]),
                         reads=[B("mu2"), B("constc")], writes=[B("mu2")])
                    k.op("vector", lambda e: e.reciprocal(out=mu2, in_=mu2), reads=[B("mu2")], writes=[B("mu2")])
                    k.op("tensor", lambda e: e.matmul(PS[0][:, 0:512], ONES1[0:1, :], mu2, start=True, stop=True),
                         reads=[B("ones1"), B("mu2")], writes=[PSB[0]])
                    k.op("vector", lambda e, csl=csl: e.tensor_copy(out=RR[:, csl], in_=PS[0][:, 0:512]),
                         reads=[PSB[0]], writes=[B("rr")])
                    free_psum(0, B("rr"))
                if os.environ.get("KDBG6") and moe_call[0] == 1:
                    orr = nc.dram_tensor("dbg_rr", [128, T], F32, kind="ExternalOutput").ap()
                    k.dma("sync", orr, RR[:], reads=[B("rr")], writes=[B("dbg")])
                    orb = nc.dram_tensor("dbg_rb", [128, T], F32, kind="ExternalOutput").ap()
                    k.dma("sync", orb, RB[:], reads=[B("RB")], writes=[B("dbg")])
                    omb = nc.dram_tensor("dbg_mb", [128, T], F32, kind="ExternalOutput").ap()
                    k.dma("sync", omb, MB[:], reads=[B("MB")], writes=[B("dbg")])
                for kd in range(KD):
                    for c in range(4):
                        csl = slice(c * 512, (c + 1) * 512)
                        xb = xof_chunk(XOF6[:, 0:512], kd, csl)
                        k.op("vector", lambda e, csl=csl: e.tensor_tensor(
                            out=SQ[:, 0:512], in0=XOF6[:, 0:512], in1=RR[:, csl], op=ALU.mult),
                            reads=[xb, B("rr")], writes=[B("sq6s")])
                        k.op("scalar", lambda e, kd=kd, csl=csl: e.activation(
                            out=TNB[:, kd * T:(kd + 1) * T][:, csl], in_=SQ[:, 0:512], func=AF.Identity,
                            scale=NWC[:, kd:kd + 1]),
                            reads=[B("sq6s"), B("nwc")], writes=[B("tnb%d" % kd)])
                        xap = X[:, kd * T:(kd + 1) * T][:, csl]
                        k.op("vector", lambda e, xap=xap: e.tensor_tensor(
                            out=xap, in0=xap, in1=XOF6[:, 0:512], op=ALU.add),
                            reads=[xb, B("X%d" % kd)], writes=[B("X%d" % kd)])
                for ti in range(16):
                    for kd in range(KD):
                        lw = TNB[:, kd * T + ti * 128: kd * T + (ti + 1) * 128]
                        k.op("tensor", lambda e, lw=lw, kd=kd: e.matmul(
                            PS[1][:, 0:8], lw, GWT[:, kd * 8:(kd + 1) * 8], start=(kd == 0), stop=(kd == KD - 1)),
                            reads=[B("tnb%d" % kd), B("gwt")], psum_rw=PSB[1])
                    gsl = CMB[:, ti * 8:(ti + 1) * 8]
                    k.op("vector", lambda e, gsl=gsl: e.tensor_copy(out=gsl, in_=PS[1][:, 0:8]),
                         reads=[PSB[1]], writes=[B("cmb")])
                    free_psum(1, B("cmb"))
                    if os.environ.get("KDBG6") and ti == 0:
                        og = nc.dram_tensor("dbg_g0", [128, 8], F32, kind="ExternalOutput").ap()
                        k.dma("sync", og, gsl, reads=[B("cmb")], writes=[B("dbg")])
                    k.op("vector", lambda e, gsl=gsl: e.tensor_reduce(
                        out=MC6[:, 0:1], in_=gsl, axis=mybir.AxisListType.X, op=ALU.max),
                        sreads=[B("cmb")], writes=[B("mc6")])
                    k.op("vector", lambda e, gsl=gsl: e.tensor_scalar(
                        out=SQ[:, 0:8], in0=gsl, scalar1=MC6[:, 0:1], scalar2=None, op0=ALU.is_ge),
                        sreads=[B("cmb"), B("mc6")], writes=[B("sq6s")])
                    k.op("vector", lambda e, gsl=gsl: e.scalar_tensor_tensor(
                        out=SQ[:, 8:16], in0=SQ[:, 0:8], scalar=-1e30, in1=gsl, op0=ALU.mult, op1=ALU.add),
                        sreads=[B("sq6s"), B("cmb")], writes=[B("sq6s")])
                    k.op("vector", lambda e: e.tensor_reduce(
                        out=MC6[:, 1:2], in_=SQ[:, 8:16], axis=mybir.AxisListType.X, op=ALU.max),
                        sreads=[B("sq6s")], writes=[B("mc6")])
                    k.op("vector", lambda e: e.tensor_scalar(
                        out=SQ[:, 16:24], in0=SQ[:, 8:16], scalar1=MC6[:, 1:2], scalar2=None, op0=ALU.is_ge),
                        sreads=[B("sq6s"), B("mc6")], writes=[B("sq6s")])
                    if os.environ.get("KDBG6") and ti == 0:
                        omc = nc.dram_tensor("dbg_mc0", [128, 8], F32, kind="ExternalOutput").ap()
                        k.dma("sync", omc, MC6[:], reads=[B("mc6")], writes=[B("dbg")])
                        osq = nc.dram_tensor("dbg_sq0", [128, 24], F32, kind="ExternalOutput").ap()
                        k.dma("sync", osq, SQ[:, 0:24], reads=[B("sq6s")], writes=[B("dbg")])
                    k.op("vector", lambda e: e.tensor_tensor(out=MC6[:, 2:3], in0=MC6[:, 0:1], in1=MC6[:, 1:2],
                                                             op=ALU.subtract),
                         sreads=[B("mc6")], writes=[B("mc6")])
                    k.op("scalar", lambda e: e.activation(out=MC6[:, 3:4], in_=MC6[:, 2:3], func=AF.Sigmoid),
                         sreads=[B("mc6")], writes=[B("mc6")])
                    k.op("vector", lambda e: e.tensor_scalar(out=MC6[:, 4:5], in0=MC6[:, 3:4], scalar1=-1.0,
                                                             scalar2=1.0, op0=ALU.mult, op1=ALU.add),
                         sreads=[B("mc6")], writes=[B("mc6")])
                    k.op("vector", lambda e: e.tensor_scalar(
                        out=SQ[:, 0:8], in0=SQ[:, 0:8], scalar1=MC6[:, 3:4], scalar2=None, op0=ALU.mult),
                        sreads=[B("sq6s"), B("mc6")], writes=[B("sq6s")])
                    k.op("vector", lambda e: e.tensor_scalar(
                        out=SQ[:, 16:24], in0=SQ[:, 16:24], scalar1=MC6[:, 4:5], scalar2=None, op0=ALU.mult),
                        sreads=[B("sq6s"), B("mc6")], writes=[B("sq6s")])
                    k.op("vector", lambda e, gsl=gsl: e.tensor_tensor(
                        out=gsl, in0=SQ[:, 0:8], in1=SQ[:, 16:24], op=ALU.add),
                        sreads=[B("sq6s")], writes=[B("cmb")])
                    k.op("tensor", lambda e, ti=ti: e.transpose(PS[2][0:8, 0:128], CMB[:, ti * 8:(ti + 1) * 8], IDF6[:]),
                         reads=[B("cmb"), B("idf6")], writes=[PSB[2]])
                    k.op("vector", lambda e, ti=ti: e.tensor_copy(out=CT6[:, ti * 128:(ti + 1) * 128], in_=PS[2][0:8, 0:128]),
                         reads=[PSB[2]], writes=[B("ct6")])
                    free_psum(2, B("ct6"))
            if os.environ.get("KDBG6") and moe_call[0] == 1:
                oct6 = nc.dram_tensor("dbg_ct6", [8, T], F32, kind="ExternalOutput").ap()
                k.dma("sync", oct6, CT6[:], reads=[B("ct6")], writes=[B("dbg")])
                otnb = nc.dram_tensor("dbg_tnb", [128, KD * T], BF16, kind="ExternalOutput").ap()
                k.dma("sync", otnb, TNB[:], reads=[B("tnb%d" % g2) for g2 in range(KD)], writes=[B("dbg")])
            k.barrier()
            with ES6() as s_e:
                sbe = lambda n2, sh, dt=F32: s_e.enter_context(nc.sbuf_tensor(mpre + "s6e_" + n2, list(sh), dt))
                W1B = sbe("W1B", (128, KD * 2048), BF16)
                W2B = sbe("W2B", (128, 16 * D), BF16)
                HEB = sbe("HEB", (128, 16 * 512), BF16)
                CEB = sbe("CEB", (128, T))
                CR6 = sbe("CR6", (1, T))
                B1C = sbe("B1C", (128, 16)); B2C = sbe("B2C", (128, KD))
                TQ6 = sbe("TQ6", (128, 512))
                for e_ in range(8):
                    k.dma("gpsimd", W1B[:], h_in["sl6_w1T"][e_], writes=[B("w1b")])
                    k.dma("gpsimd", W2B[:], h_in["sl6_w2T"][e_], writes=[B("w2b")])
                    k.dma("sync", B1C[:], h_in["sl6_b1"][e_], writes=[B("b1c")])
                    k.dma("sync", B2C[:], h_in["sl6_b2"][e_], writes=[B("b2c")])
                    k.dma("scalar", CR6[0:1, :], CT6[e_:e_ + 1, :], reads=[B("ct6")], writes=[B("cr6")])
                    for c in range(4):
                        k.op("tensor", lambda e, c=c: e.matmul(
                            PS[5][:, 0:512], ONES1[0:1, :], CR6[0:1, c * 512:(c + 1) * 512], start=True, stop=True),
                            reads=[B("ones1"), B("cr6")], writes=[PSB[5]])
                        k.op("vector", lambda e, c=c: e.tensor_copy(out=CEB[:, c * 512:(c + 1) * 512], in_=PS[5][:, 0:512]),
                             reads=[PSB[5]], writes=[B("ceb")])
                        free_psum(5, B("ceb"))
                    for tc in range(4):
                        csl = slice(tc * 512, (tc + 1) * 512)
                        for ht in range(16):
                            bk = 4 + (ht % 2)
                            for kd in range(KD):
                                lw = W1B[:, kd * 2048 + ht * 128: kd * 2048 + (ht + 1) * 128]
                                rhs = TNB[:, kd * T:(kd + 1) * T][:, csl]
                                k.op("tensor", lambda e, lw=lw, rhs=rhs, kd=kd, bk=bk: e.matmul(
                                    PS[bk][:, 0:512], lw, rhs, start=(kd == 0), stop=(kd == KD - 1)),
                                    reads=[B("w1b"), B("tnb%d" % kd)], psum_rw=PSB[bk])
                            k.op("scalar", lambda e, ht=ht, bk=bk: e.activation(
                                out=HEB[:, ht * 512:(ht + 1) * 512], in_=PS[bk][:, 0:512], func=AF.Gelu,
                                bias=B1C[:, ht:ht + 1]),
                                reads=[PSB[bk], B("b1c")], writes=[B("heb%d" % ht)])
                            free_psum(bk, B("heb%d" % ht))
                        for od in range(KD):
                            for ht in range(16):
                                lw = W2B[:, ht * D + od * 128: ht * D + (od + 1) * 128]
                                k.op("tensor", lambda e, lw=lw, ht=ht, od=od: e.matmul(
                                    PS[od % 4][:, 0:512], lw, HEB[:, ht * 512:(ht + 1) * 512],
                                    start=(ht == 0), stop=(ht == 15)),
                                    reads=[B("w2b"), B("heb%d" % ht)], psum_rw=PSB[od % 4])
                            k.op("scalar", lambda e, od=od: e.activation(
                                out=SQ6[:, 0:512], in_=PS[od % 4][:, 0:512], func=AF.Identity,
                                bias=B2C[:, od:od + 1]),
                                reads=[PSB[od % 4], B("b2c")], writes=[B("sq6")])
                            free_psum(od % 4, B("sq6"))
                            k.op("vector", lambda e, csl=csl: e.tensor_tensor(
                                out=TQ6[:], in0=SQ6[:, 0:512], in1=CEB[:, csl], op=ALU.mult),
                                reads=[B("sq6"), B("ceb")], writes=[B("tq6")])
                            xap = X[:, od * T:(od + 1) * T][:, csl]
                            k.op("vector", lambda e, xap=xap, od=od: e.tensor_tensor(
                                out=xap, in0=xap, in1=TQ6[:], op=ALU.add),
                                reads=[B("tq6"), B("X%d" % od)], writes=[B("X%d" % od)])


        aot7_h = nc.dram_tensor("sl7_aot_h", [128, KD * T], BF16, kind="Internal").ap()
        qk7q_h = nc.dram_tensor("sl7_q_h", [128, KD * T], BF16, kind="Internal").ap()
        qk7k_h = nc.dram_tensor("sl7_k_h", [128, KD * T], BF16, kind="Internal").ap()

        def sublayer7(sb, ps):
            from contextlib import ExitStack as ES7
            VN7 = sb("VN7", (128, 16 * D), BF16)
            S1H = sb("S1H", (32, T), BF16)
            S1P = sb("S1P", (32, T), BF16)
            LBT = sb("LBT", (32, 3072), BF16)
            BA7 = sb("BA7", (128, 24))
            k.dma("gpsimd", LBT[:], h_in["sl7_lbT"], writes=[B("lbt")])
            k.dma("sync", BA7[:], h_in["sl7_ba"], writes=[B("ba7")])
            with ES7() as s_bld:
                sbb = lambda n2, sh, dt=F32: s_bld.enter_context(nc.sbuf_tensor("s7b_" + n2, list(sh), dt))
                RB = sbb("RB", (128, T)); MB = sbb("MB", (128, T))
                SQ = sbb("SQ", (128, 1024)); STATR = sbb("STATR", (1, 1536))
                XNB7 = sbb("XNB7", (128, KD * T), BF16)
                QR7 = sbb("QR7", (128, T), BF16)
                WA7 = sbb("WA7", (128, KD * 128), BF16)
                LAT = sbb("LAT", (128, KD * 32), BF16)
                BAV = sbb("BAV", (128, D))
                k.dma("gpsimd", LAT[:], h_in["sl7_laT"], writes=[B("lat")])
                k.dma("sync", BAV[:], h_in["sl7_bav"], writes=[B("bav")])
                ln_stats(RB, MB, SQ, STATR, CONSTC[:, 0:1])
                k.barrier()
                for kd in range(KD):
                    for c in range(4):
                        make_xnb(RB, MB, SQ, XNB7, "X7%d" % kd, kd, slice(c * 512, (c + 1) * 512),
                                 slice(kd * T + c * 512, kd * T + (c + 1) * 512))
                for c in range(4):
                    for kd in range(KD):
                        lw = LAT[:, kd * 32:(kd + 1) * 32]
                        rhs = XNB7[:, kd * T + c * 512: kd * T + (c + 1) * 512]
                        k.op("tensor", lambda e, lw=lw, rhs=rhs, kd=kd: e.matmul(
                            PS[0][0:32, 0:512], lw, rhs, start=(kd == 0), stop=(kd == KD - 1)),
                            reads=[B("lat"), B("X7%d" % kd)], psum_rw=PSB[0])
                    k.op("scalar", lambda e, c=c: e.activation(
                        out=S1H[:, c * 512:(c + 1) * 512], in_=PS[0][0:32, 0:512], func=AF.Identity, scale=0.5),
                        reads=[PSB[0]], writes=[B("s1h")])
                    free_psum(0, B("s1h"))
                for ot in range(16):
                    for kd in range(KD):
                        k.dma("gpsimd", WA7[:, kd * 128:(kd + 1) * 128],
                              h_in["sl7_waT"][:, kd * 3072 + ot * 128: kd * 3072 + (ot + 1) * 128],
                              writes=[B("wa7")])
                    for c in range(4):
                        for kd in range(KD):
                            lw = WA7[:, kd * 128:(kd + 1) * 128]
                            rhs = XNB7[:, kd * T + c * 512: kd * T + (c + 1) * 512]
                            k.op("tensor", lambda e, lw=lw, rhs=rhs, kd=kd: e.matmul(
                                PS[c][:, 0:512], lw, rhs, start=(kd == 0), stop=False),
                                reads=[B("wa7"), B("X7%d" % kd)], psum_rw=PSB[c])
                        lwl = LBT[:, ot * 128:(ot + 1) * 128]
                        rhsl = S1H[:, c * 512:(c + 1) * 512]
                        k.op("tensor", lambda e, lwl=lwl, rhsl=rhsl: e.matmul(
                            PS[c][:, 0:512], lwl, rhsl, start=False, stop=True),
                            reads=[B("lbt"), B("s1h")], psum_rw=PSB[c])
                    dsth = qk7q_h if ot < 8 else qk7k_h
                    oti = ot % 8
                    for c in range(4):
                        k.op("scalar", lambda e, c=c, ot=ot: e.activation(
                            out=QR7[:, c * 512:(c + 1) * 512], in_=PS[c][:, 0:512],
                            func=AF.Identity, bias=BA7[:, ot:ot + 1]),
                            reads=[PSB[c], B("ba7")], writes=[B("qr7")])
                        free_psum(c, B("qr7"))
                    k.dma("sync", dsth[:, oti * T:(oti + 1) * T], QR7[:], reads=[B("qr7")], writes=[B("qk7sp")])
                WAV = sbb("WAV", (128, KD * 1024), BF16)
                for kd in range(KD):
                    k.dma("gpsimd", WAV[:, kd * 1024:(kd + 1) * 1024],
                          h_in["sl7_waT"][:, kd * 3072 + 2048: kd * 3072 + 3072], writes=[B("wav")])
                for ti in range(16):
                    for dc in range(2):
                        bank = 4 + (ti % 2) * 2 + dc
                        for kd in range(KD):
                            lw = XNB7[:, kd * T + ti * 128: kd * T + (ti + 1) * 128]
                            k.op("tensor", lambda e, lw=lw, kd=kd, dc=dc, bank=bank: e.matmul(
                                PS[bank][:, 0:512], lw,
                                WAV[:, kd * 1024 + dc * 512: kd * 1024 + (dc + 1) * 512],
                                start=(kd == 0), stop=False),
                                reads=[B("X7%d" % kd), B("wav")], psum_rw=PSB[bank])
                        lwl = S1H[:, ti * 128:(ti + 1) * 128]
                        k.op("tensor", lambda e, lwl=lwl, dc=dc, bank=bank: e.matmul(
                            PS[bank][:, 0:512], lwl, LBT[:, 2048 + dc * 512: 2048 + (dc + 1) * 512],
                            start=False, stop=True),
                            reads=[B("s1h"), B("lbt")], psum_rw=PSB[bank])
                    for dc in range(2):
                        bank = 4 + (ti % 2) * 2 + dc
                        k.op("vector", lambda e, dc=dc, bank=bank: e.tensor_tensor(
                            out=SQ[:, dc * 512:(dc + 1) * 512], in0=PS[bank][:, 0:512],
                            in1=BAV[:, dc * 512:(dc + 1) * 512], op=ALU.add),
                            reads=[PSB[bank], B("bav")], writes=[B("sq7v")])
                        free_psum(bank, B("sq7v"))
                        k.op("vector", lambda e, ti=ti, dc=dc: e.tensor_copy(
                            out=VN7[:, ti * D + dc * 512: ti * D + (dc + 1) * 512], in_=SQ[:, dc * 512:(dc + 1) * 512]),
                            reads=[B("sq7v")], writes=[B("vn7")])
            k.barrier()
            with ES7() as s_att:
                sba = lambda n2, sh, dt=F32: s_att.enter_context(nc.sbuf_tensor("s7a_" + n2, list(sh), dt))
                SC7 = sba("SC7", (128, T))
                EXD7 = sba("EXD7", (128, 512))
                ATT7 = sba("ATT7", (128, T), BF16)
                TRIA = sba("TRIA", (128, 128))
                IDF7 = sba("IDF7", (128, 128))
                AVS7 = sba("AVS7", (64, 128), BF16)
                DC7 = sba("DC7", (128, 8))
                k.dma("sync", TRIA[:], h_in["sl7_tria"], writes=[B("tria")])
                k.dma("sync", IDF7[:], h_in["eyerep"][:, 0:128], writes=[B("idf7")])
                QH7 = sba("QH7", (64, 2 * T), BF16)
                KH7 = sba("KH7", (64, 2 * T), BF16)
                for h in range(16):
                    hb = h % 2
                    qsl = QH7[0:64, hb * T:(hb + 1) * T]
                    ksl = KH7[0:64, hb * T:(hb + 1) * T]
                    qb_ = B("qh7%d" % hb)
                    kb_ = B("kh7%d" % hb)
                    k.dma("scalar", qsl, qk7q_h[(h % 2) * 64:(h % 2) * 64 + 64, (h // 2) * T:(h // 2 + 1) * T],
                          writes=[qb_])
                    k.dma("scalar", ksl, qk7k_h[(h % 2) * 64:(h % 2) * 64 + 64, (h // 2) * T:(h // 2 + 1) * T],
                          writes=[kb_])
                    for qt in range(16):
                        nch = (qt * 128 + 128 + 511) // 512
                        for ch in range(nch):
                            cl = min(512, qt * 128 + 128 - ch * 512)
                            k.op("tensor", lambda e, qsl=qsl, ksl=ksl, qt=qt, ch=ch, cl=cl: e.matmul(
                                PS[ch][:, 0:cl], qsl[:, qt * 128:(qt + 1) * 128],
                                ksl[:, ch * 512: ch * 512 + cl], start=True, stop=True),
                                reads=[qb_, kb_], writes=[PSB[ch]])
                        for ch in range(nch):
                            cl = min(512, qt * 128 + 128 - ch * 512)
                            k.op("scalar", lambda e, ch=ch, cl=cl: e.activation(
                                out=SC7[:, ch * 512: ch * 512 + cl], in_=PS[ch][:, 0:cl],
                                func=AF.Identity, scale=0.125),
                                reads=[PSB[ch]], writes=[B("sc7")])
                            free_psum(ch, B("sc7"))
                        k.op("vector", lambda e, qt=qt: e.tensor_tensor(
                            out=SC7[:, qt * 128:(qt + 1) * 128], in0=SC7[:, qt * 128:(qt + 1) * 128],
                            in1=TRIA[:], op=ALU.add),
                            reads=[B("sc7"), B("tria")], writes=[B("sc7")])
                        klen = qt * 128 + 128
                        k.op("vector", lambda e, klen=klen: e.tensor_reduce(
                            out=DC7[:, 0:1], in_=SC7[:, 0:klen], axis=mybir.AxisListType.X,
                            op=ALU.max, negate=True),
                            sreads=[B("sc7")], writes=[B("dc7")])
                        for ch in range(nch):
                            cl = min(512, klen - ch * 512)
                            k.op("scalar", lambda e, ch=ch, cl=cl: e.activation(
                                out=EXD7[:, 0:cl], in_=SC7[:, ch * 512: ch * 512 + cl],
                                func=AF.Exp, bias=DC7[:, 0:1], accum_out=DC7[:, 2 + ch:3 + ch]),
                                sreads=[B("sc7"), B("dc7")], writes=[B("exd7"), B("dc7")])
                        for ch in range(1, nch):
                            k.op("vector", lambda e, ch=ch: e.tensor_tensor(
                                out=DC7[:, 2:3], in0=DC7[:, 2:3], in1=DC7[:, 2 + ch:3 + ch], op=ALU.add),
                                sreads=[B("dc7")], writes=[B("dc7")])
                        k.op("scalar", lambda e: e.activation(out=DC7[:, 6:7], in_=DC7[:, 2:3], func=AF.Ln),
                             sreads=[B("dc7")], writes=[B("dc7")])
                        k.op("vector", lambda e: e.tensor_tensor(
                            out=DC7[:, 6:7], in0=DC7[:, 0:1], in1=DC7[:, 6:7], op=ALU.subtract),
                            sreads=[B("dc7")], writes=[B("dc7")])
                        k.op("scalar", lambda e, klen=klen: e.activation(
                            out=SC7[:, 0:klen], in_=SC7[:, 0:klen], func=AF.Exp, bias=DC7[:, 6:7]),
                            sreads=[B("dc7")], reads=[B("sc7")], writes=[B("sc7")])
                        for tk in range(qt + 1):
                            bk = 4 + 2 * (tk % 2)
                            k.op("tensor", lambda e, tk=tk, bk=bk: e.transpose(
                                PS[bk][:, 0:128], SC7[:, tk * 128:(tk + 1) * 128], IDF7[:]),
                                reads=[B("sc7"), B("idf7")], writes=[PSB[bk]])
                            k.op("vector", lambda e, tk=tk, bk=bk: e.tensor_copy(
                                out=ATT7[:, tk * 128:(tk + 1) * 128], in_=PS[bk][:, 0:128]),
                                reads=[PSB[bk]], writes=[B("att7")])
                            free_psum(bk, B("att7"))
                        for tk in range(qt + 1):
                            lw = VN7[:, tk * D + h * 64: tk * D + (h + 1) * 64]
                            k.op("tensor", lambda e, lw=lw, tk=tk, qt=qt: e.matmul(
                                PS[5][0:64, 0:128], lw, ATT7[:, tk * 128:(tk + 1) * 128],
                                start=(tk == 0), stop=(tk == qt)),
                                reads=[B("vn7"), B("att7")], psum_rw=PSB[5])
                        k.op("scalar", lambda e: e.copy(out=AVS7[0:64, 0:128], in_=PS[5][0:64, 0:128]),
                             reads=[PSB[5]], writes=[B("avs7")])
                        free_psum(5, B("avs7"))
                        k.dma("scalar", aot7_h[(h % 2) * 64:(h % 2) * 64 + 64,
                                               (h // 2) * T + qt * 128:(h // 2) * T + (qt + 1) * 128],
                              AVS7[0:64, 0:128], reads=[B("avs7")], writes=[B("aotsp")])
            k.barrier()
            with ES7() as s_prj:
                sbp = lambda n2, sh, dt=F32: s_prj.enter_context(nc.sbuf_tensor("s7p_" + n2, list(sh), dt))
                WPS = sbp("WPS", (128, KD * D), BF16)
                LAPT = sbp("LAPT", (128, KD * 32), BF16)
                LBPT = sbp("LBPT", (32, D), BF16)
                BP7 = sbp("BP7", (128, KD))
                AOC = sbp("AOC", (128, 2 * T), BF16)
                SQ7 = sbp("SQ7", (128, 512))
                k.dma("gpsimd", WPS[:], h_in["sl7_wpT"], writes=[B("wps")])
                k.dma("gpsimd", LAPT[:], h_in["sl7_lapT"], writes=[B("lapt")])
                k.dma("gpsimd", LBPT[:], h_in["sl7_lbpT"], writes=[B("lbpt")])
                k.dma("sync", BP7[:], h_in["sl7_bp"], writes=[B("bp7")])
                for c in range(4):
                    for ko in range(KD):
                        wb = ko % 2
                        k.dma("sync", AOC[:, wb * T + c * 512: wb * T + (c + 1) * 512],
                              aot7_h[:, ko * T + c * 512: ko * T + (c + 1) * 512], writes=[B("aoc%d" % wb)])
                        lw = LAPT[:, ko * 32:(ko + 1) * 32]
                        k.op("tensor", lambda e, lw=lw, ko=ko, wb=wb, c=c: e.matmul(
                            PS[0][0:32, 0:512], lw, AOC[:, wb * T + c * 512: wb * T + (c + 1) * 512],
                            start=(ko == 0), stop=(ko == KD - 1)),
                            reads=[B("lapt"), B("aoc%d" % wb)], psum_rw=PSB[0])
                    k.op("scalar", lambda e, c=c: e.activation(
                        out=S1P[:, c * 512:(c + 1) * 512], in_=PS[0][0:32, 0:512], func=AF.Identity, scale=0.5),
                        reads=[PSB[0]], writes=[B("s1p")])
                    free_psum(0, B("s1p"))
                for c in range(4):
                    for ko in range(KD):
                        wb = ko % 2
                        k.dma("sync", AOC[:, wb * T + c * 512: wb * T + (c + 1) * 512],
                              aot7_h[:, ko * T + c * 512: ko * T + (c + 1) * 512], writes=[B("aoc%d" % wb)])
                        for od in range(KD):
                            lw = WPS[:, ko * D + od * 128: ko * D + (od + 1) * 128]
                            k.op("tensor", lambda e, lw=lw, ko=ko, od=od, wb=wb, c=c: e.matmul(
                                PS[od][:, 0:512], lw, AOC[:, wb * T + c * 512: wb * T + (c + 1) * 512],
                                start=(ko == 0), stop=False),
                                reads=[B("wps"), B("aoc%d" % wb)], psum_rw=PSB[od])
                    for od in range(KD):
                        lwl = LBPT[:, od * 128:(od + 1) * 128]
                        k.op("tensor", lambda e, lwl=lwl, od=od, c=c: e.matmul(
                            PS[od][:, 0:512], lwl, S1P[:, c * 512:(c + 1) * 512], start=False, stop=True),
                            reads=[B("lbpt"), B("s1p")], psum_rw=PSB[od])
                        k.op("scalar", lambda e, od=od: e.activation(
                            out=SQ7[:, 0:512], in_=PS[od][:, 0:512], func=AF.Identity, bias=BP7[:, od:od + 1]),
                            reads=[PSB[od], B("bp7")], writes=[B("sq7")])
                        free_psum(od, B("sq7"))
                        xap = X[:, od * T + c * 512: od * T + (c + 1) * 512]
                        k.op("vector", lambda e, xap=xap, od=od: e.tensor_tensor(
                            out=xap, in0=SQ7[:, 0:512], in1=xap, op=ALU.add),
                            reads=[B("sq7"), B("X%d" % od)], writes=[B("X%d" % od)])

        from contextlib import ExitStack as ES
        scope_id = [0]
        def run_scoped(fn, *a):
            scope_id[0] += 1
            sid = scope_id[0]
            with ES() as sub:
                sb2 = lambda name, shape, dt=F32: sub.enter_context(nc.sbuf_tensor("%s_s%d" % (name, sid), list(shape), dt))
                ps2 = lambda name, shape: sub.enter_context(nc.psum_tensor("%s_s%d" % (name, sid), list(shape), F32))
                fn(sb2, ps2, *a)
            k.barrier()

        if NSUB >= 1:
            run_scoped(sublayer1)
        if NSUB >= 2:
            run_scoped(compose_bfly)
            run_scoped(sublayer_ffn)
        if NSUB >= 3:
            run_scoped(sublayer3)
        if NSUB >= 4:
            run_scoped(sublayer_ffn)
        if NSUB >= 5:
            run_scoped(sublayer5)
        if NSUB >= 6:
            run_scoped(sublayer_moe)
        if NSUB >= 7:
            run_scoped(sublayer7)
        if NSUB >= 8:
            run_scoped(sublayer_moe)

        for kd in range(KD):
            k.dma("sync", y_h[kd * 128:(kd + 1) * 128, :], X[:, kd * T:(kd + 1) * T],
                  reads=[B("X%d" % kd)], writes=[B("ydone")])

        with nc.Block() as block:
            k.finalize(block)
    return nc


def kernel(**inputs):
    H = host_prep(inputs)
    x_full = np.ascontiguousarray(inputs["L0_attn_a_x"], dtype=np.float32)
    nc = build_program()
    in_maps = []
    for c in range(8):
        m = dict(H)
        m["x"] = np.ascontiguousarray(x_full[c].T)
        in_maps.append(m)
    res = run_bass_kernel_spmd(nc, in_maps, core_ids=list(range(8)))
    return np.ascontiguousarray(np.stack([np.asarray(r["y"], dtype=np.float32).T for r in res.results], axis=0))
